# version 17
# baseline (speedup 1.0000x reference)
import math
import numpy as np

import concourse.bass as bass
import concourse.mybir as mybir
from concourse.bass_utils import run_bass_kernel_spmd

F32 = mybir.dt.float32
BF16 = mybir.dt.bfloat16
AF = mybir.ActivationFunctionType
ALU = mybir.AluOpType
AX = mybir.AxisListType

NCORES = 8
T = 4096
D = 1024
NSEG = 8
SEG = 512

PART_BYTES = 207 * 1024

PE, DVE, ACT, POOL, SP = "PE", "DVE", "ACT", "POOL", "SP"
ENGINES = [PE, DVE, ACT, POOL, SP]


_DT_SIZE = {F32: 4, BF16: 2, mybir.dt.float16: 2, mybir.dt.int32: 4,
            mybir.dt.uint32: 4, mybir.dt.uint8: 1, mybir.dt.int8: 1}

_SBUF_BASE = {}


def spans_of_ap(ap):
    esz = _DT_SIZE.get(ap.dtype, 4)
    sp = str(ap.space)
    dims = ap.ap
    onchip = ("SBUF" in sp) or ("PSUM" in sp)
    if onchip:
        pstride = dims[0][0] if dims and dims[0][0] > 0 else None
        off = ap.offset % pstride if pstride else ap.offset
        dims = dims[1:]
        base = _SBUF_BASE.get(ap.name, 0)
    else:
        off = ap.offset
        base = 0
    lo = 0
    hi = 0
    for step, count in dims:
        if count <= 1:
            continue
        d = step * (count - 1)
        if d >= 0:
            hi += d
        else:
            lo += d
    start = base + (off + lo) * esz
    end = base + (off + hi) * esz + esz
    return start, end


class Op:
    __slots__ = ("idx", "eng", "emit", "deps", "is_dma", "chan", "clock",
                 "inc_needed", "tag", "chan_seq")

    def __init__(self, idx, eng, emit, is_dma=False, chan=None, tag=""):
        self.idx = idx
        self.eng = eng
        self.emit = emit
        self.deps = []
        self.is_dma = is_dma
        self.chan = chan
        self.clock = None
        self.inc_needed = False
        self.tag = tag


CHAN_DEPTH = 4


class Chan:

    def __init__(self, name, depth=CHAN_DEPTH):
        self.name = name
        self.depth = depth
        self.count = 0
        self.last_ops = [None] * depth


class Seg:
    __slots__ = ("start", "end", "writer", "readers")

    def __init__(self, start, end, writer, readers):
        self.start = start
        self.end = end
        self.writer = writer
        self.readers = readers


class IntervalMap:

    def __init__(self):
        self.segs = []

    def read(self, start, end, op_idx):
        deps = []
        for s in self.segs:
            if s.end <= start or s.start >= end:
                continue
            if s.writer is not None:
                deps.append(s.writer)
            s.readers.append(op_idx)
        return deps

    def write(self, start, end, op_idx):
        deps = []
        new = []
        for s in self.segs:
            if s.end <= start or s.start >= end:
                new.append(s)
                continue
            if s.writer is not None:
                deps.append(s.writer)
            deps.extend(s.readers)
            if s.start < start:
                new.append(Seg(s.start, start, s.writer, list(s.readers)))
            if s.end > end:
                new.append(Seg(end, s.end, s.writer, list(s.readers)))
        new.append(Seg(start, end, op_idx, []))
        new.sort(key=lambda s: s.start)
        self.segs = new
        return deps


class Prog:
    def __init__(self, nc, same_engine_sync=True):
        self.nc = nc
        self.ops = []
        self.maps = {"sbuf": IntervalMap(), "psum": IntervalMap()}
        self.chans = {}
        self.same_engine_sync = same_engine_sync
        self.wait_edges = []
        self._final_waits = []

    def chan(self, name):
        if name not in self.chans:
            hot = ("x0", "x1", "y0", "y1", "acc0", "acc1")
            med = ("w0", "w0b", "w0c")
            d = 6 if name in hot else (4 if name in med else 2)
            self.chans[name] = Chan(name, depth=d)
        return self.chans[name]

    def _space_of(self, ap):
        t = ap.tensor if hasattr(ap, "tensor") else None
        sp = str(ap.space)
        if "SB" in sp or "sb" in sp:
            return "sbuf"
        if "PSUM" in sp or "Psum" in sp or "psum" in sp:
            return "psum"
        key = "dram:" + ap.name
        if key not in self.maps:
            self.maps[key] = IntervalMap()
        return key

    def _add(self, eng, emit, reads=(), writes=(), is_dma=False, chan=None,
             tag=""):
        idx = len(self.ops)
        op = Op(idx, eng, emit, is_dma, chan, tag)
        deps = set()
        for ap in reads:
            m = self.maps[self._space_of(ap)]
            s, e = spans_of_ap(ap)
            deps.update(m.read(s, e, idx))
        for ap in writes:
            m = self.maps[self._space_of(ap)]
            s, e = spans_of_ap(ap)
            deps.update(m.write(s, e, idx))
        if is_dma:
            c = self.chan(chan)
            slot = c.count % c.depth
            if c.last_ops[slot] is not None:
                deps.add(c.last_ops[slot])
            c.last_ops[slot] = idx
            c.count += 1
            op.chan_seq = c.count
        deps.discard(idx)
        op.deps = sorted(deps)
        self.ops.append(op)
        return idx

    def dma(self, qeng, out, in_, chan, tag="", **kw):
        if "accum_op" in kw and "max_dma_last_dim" not in kw:
            kw["max_dma_last_dim"] = 2048

        def emit(eng):
            return eng.dma_start(out, in_, **kw)
        return self._add(qeng, emit, reads=[in_], writes=[out], is_dma=True,
                         chan=chan, tag=tag)

    def dma_transpose(self, qeng, out, in_, chan, tag=""):
        def emit(eng):
            return eng.dma_start_transpose(out, in_)
        return self._add(qeng, emit, reads=[in_], writes=[out], is_dma=True,
                         chan=chan, tag=tag)

    def mm(self, out, lhsT, rhs, start=True, stop=True, tag="", **kw):
        def emit(eng):
            return eng.matmul(out, lhsT, rhs, start=start, stop=stop, **kw)
        return self._add(PE, emit, reads=[lhsT, rhs], writes=[out], tag=tag)

    def transpose(self, out, in_, identity, tag="", **kw):
        def emit(eng):
            return eng.transpose(out, in_, identity, **kw)
        return self._add(PE, emit, reads=[in_, identity], writes=[out], tag=tag)

    def act(self, out, in_, func, bias=0.0, scale=1.0, accum_out=None, tag=""):
        extra_r = [b for b in (bias, scale) if hasattr(b, "ap")]
        extra_w = [accum_out] if accum_out is not None else []

        def emit(eng):
            return eng.activation(out, in_, func, bias=bias, scale=scale,
                                  accum_out=accum_out)
        return self._add(ACT, emit, reads=[in_] + extra_r,
                         writes=[out] + extra_w, tag=tag)

    def act_copy(self, out, in_, bias=0.0, scale=1.0, eng=ACT, tag=""):
        return self.act(out, in_, AF.Copy, bias=bias, scale=scale, tag=tag)

    def tt(self, out, a, b, alu, eng=DVE, tag=""):
        def emit(e):
            return e.tensor_tensor(out, a, b, alu)
        return self._add(eng, emit, reads=[a, b], writes=[out], tag=tag)

    def ts(self, out, in_, scalar, alu, eng=DVE, tag=""):
        extra = [scalar] if hasattr(scalar, "ap") else []

        def emit(e):
            return e.tensor_single_scalar(out, in_, scalar, alu)
        return self._add(eng, emit, reads=[in_] + extra, writes=[out], tag=tag)

    def ts2(self, out, in_, s1, s2, op0, op1, eng=DVE, tag=""):
        extra = [s for s in (s1, s2) if hasattr(s, "ap")]

        def emit(e):
            return e.tensor_scalar(out, in_, s1, s2, op0, op1)
        return self._add(eng, emit, reads=[in_] + extra, writes=[out], tag=tag)

    def stt(self, out, in0, scalar, in1, op0, op1, eng=DVE, tag=""):
        extra = [scalar] if hasattr(scalar, "ap") else []

        def emit(e):
            return e.scalar_tensor_tensor(out, in0, scalar, in1, op0, op1)
        return self._add(eng, emit, reads=[in0, in1] + extra, writes=[out],
                         tag=tag)

    def copy(self, out, in_, eng=DVE, tag=""):
        if eng == ACT:
            def emit(e):
                return e.copy(out, in_)
        else:
            def emit(e):
                return e.tensor_copy(out, in_)
        return self._add(eng, emit, reads=[in_], writes=[out], tag=tag)

    def red(self, out, in_, alu, eng=DVE, negate=None, tag=""):
        def emit(e):
            return e.tensor_reduce(out, in_, AX.X, alu, negate=negate)
        return self._add(eng, emit, reads=[in_], writes=[out], tag=tag)

    def recip(self, out, in_, tag=""):
        def emit(e):
            return e.reciprocal(out, in_)
        return self._add(DVE, emit, reads=[in_], writes=[out], tag=tag)

    def memset(self, ap, v, eng=DVE, tag=""):
        def emit(e):
            return e.memset(ap, v)
        return self._add(eng, emit, reads=[], writes=[ap], tag=tag)

    def iota(self, ap, pattern, base=0, channel_multiplier=0, tag=""):
        def emit(e):
            return e.iota(ap, pattern, base=base,
                          channel_multiplier=channel_multiplier,
                          allow_small_or_imprecise_dtypes=True)
        return self._add(POOL, emit, reads=[], writes=[ap], tag=tag)

    def custom(self, eng, emit, reads=(), writes=(), tag=""):
        return self._add(eng, emit, reads=reads, writes=writes, tag=tag)

    def finalize(self):
        nc = self.nc
        eng_sem = {e: nc.alloc_semaphore("sem_" + e) for e in ENGINES}
        chan_sem = {c: [nc.alloc_semaphore(f"dma_{c}_{d}")
                        for d in range(ch.depth)]
                    for c, ch in self.chans.items()}

        clocks = {e: {} for e in ENGINES}
        pos_in_eng = {}
        eng_count = {e: 0 for e in ENGINES}
        for op in self.ops:
            pos_in_eng[op.idx] = eng_count[op.eng]
            eng_count[op.eng] += 1

        waits_per_op = {}
        for op in self.ops:
            clk = clocks[op.eng]
            waits = []
            for d in sorted(op.deps):
                dop = self.ops[d]
                if dop.eng == op.eng and not dop.is_dma:
                    if dop.eng == PE:
                        continue
                    if not self.same_engine_sync:
                        continue
                if dop.is_dma:
                    dd = self.chans[dop.chan].depth
                    key = ("c", dop.chan, (dop.chan_seq - 1) % dd)
                    val = (dop.chan_seq - 1) // dd + 1
                else:
                    key = ("e", dop.eng)
                    val = pos_in_eng[d] + 1
                if clk.get(key, 0) >= val:
                    continue
                waits.append(d)
                tclk = dop.clock
                if tclk:
                    for k, v in tclk.items():
                        if clk.get(k, 0) < v:
                            clk[k] = v
                clk[key] = max(clk.get(key, 0), val)
                dop.inc_needed = True
            waits_per_op[op.idx] = waits
            snap = dict(clk)
            if op.is_dma:
                dd = self.chans[op.chan].depth
                snap[("c", op.chan, (op.chan_seq - 1) % dd)] = \
                    (op.chan_seq - 1) // dd + 1
            else:
                snap[("e", op.eng)] = pos_in_eng[op.idx] + 1
            op.clock = snap

        inc_no = {}
        cnt = {e: 0 for e in ENGINES}
        for op in self.ops:
            if not op.is_dma and op.inc_needed:
                cnt[op.eng] += 1
                inc_no[op.idx] = cnt[op.eng]
            elif not op.is_dma:
                inc_no[op.idx] = None

        final_chan_waits = []
        for c in self.chans.values():
            for d in range(c.depth):
                ncomp = (c.count - d + c.depth - 1) // c.depth
                if ncomp > 0:
                    final_chan_waits.append((c.name, d, ncomp))

        streams = {e: [] for e in ENGINES}
        for op in self.ops:
            streams[op.eng].append(op)

        def emit_stream(eng_obj, eng_name):
            my_clkmax = {}
            for op in streams[eng_name]:
                for t in waits_per_op[op.idx]:
                    top = self.ops[t]
                    if top.is_dma:
                        dd = self.chans[top.chan].depth
                        sem = chan_sem[top.chan][(top.chan_seq - 1) % dd]
                        val = 16 * ((top.chan_seq - 1) // dd + 1)
                    else:
                        sem = eng_sem[top.eng]
                        val = inc_no[t]
                        assert val is not None
                    k = sem.num if hasattr(sem, "num") else id(sem)
                    if my_clkmax.get(k, -1) >= val:
                        continue
                    my_clkmax[k] = val
                    eng_obj.wait_ge(sem, val)
                inst = op.emit(eng_obj)
                if op.is_dma:
                    dd = self.chans[op.chan].depth
                    inst.then_inc(chan_sem[op.chan][(op.chan_seq - 1) % dd],
                                  16)
                elif op.inc_needed:
                    inst.then_inc(eng_sem[op.eng], 1)
            if eng_name == SP:
                for cname, d, ncomp in final_chan_waits:
                    eng_obj.wait_ge(chan_sem[cname][d], 16 * ncomp)

        with nc.Block() as block:
            block.tensor(lambda e: emit_stream(e, PE))
            block.vector(lambda e: emit_stream(e, DVE))
            block.scalar(lambda e: emit_stream(e, ACT))
            block.gpsimd(lambda e: emit_stream(e, POOL))
            block.sync(lambda e: emit_stream(e, SP))

        n_waits = sum(len(w) for w in waits_per_op.values())
        n_incs = sum(1 for o in self.ops if (o.is_dma or o.inc_needed))
        return dict(n_ops=len(self.ops), n_waits=n_waits, n_incs=n_incs)



class SbufAlloc:
    def __init__(self, nc, limit=None):
        self.nc = nc
        self.off = (nc.sbuf_base + 31) // 32 * 32
        self.limit = limit or nc.sbuf_top
        self.marks = []
        self.n = 0

    def alloc(self, cols, dtype, name=None, parts=128):
        esz = 4 if dtype == F32 else 2
        sz = cols * esz
        off = (self.off + 31) // 32 * 32
        assert off + sz <= self.limit, (
            f"sbuf overflow: {off}+{sz} > {self.limit} ({name})")
        self.off = off + sz
        self.n += 1
        nm = (name or "sb") + f"_{self.n}"
        h = self.nc.alloc_sbuf_tensor_at(nm, [parts, cols], dtype, offset=off)
        ap = h.ap()
        _SBUF_BASE[ap.name] = off
        return ap

    def mark(self):
        self.marks.append(self.off)

    def release(self):
        self.off = self.marks.pop()


class PsumAlloc:

    def __init__(self, nc):
        self.tiles = []
        for i in range(8):
            h = nc.alloc_psum_tensor(f"pb{i}", [128, 512], F32)
            ap = h.ap()
            _SBUF_BASE[ap.name] = i * 2048
            self.tiles.append(ap)
        self.free = list(range(8))

    def acquire(self):
        return self.free.pop(0)

    def release(self, i):
        self.free.append(i)

    def ap(self, i):
        return self.tiles[i]



def build_model_nc(dense_moe=True, layers=(0, 1, 2, 3), same_engine_sync=True):
    nc = bass.Bass("TRN2", target_bir_lowering=False)
    P = Prog(nc, same_engine_sync=same_engine_sync)
    sb = SbufAlloc(nc)
    ps = PsumAlloc(nc)

    dram = {}

    def din(name, shape, dtype=F32):
        dram[name] = nc.dram_tensor(name, list(shape), dtype,
                                    kind="ExternalInput").ap()
        return dram[name]

    def dscratch(name, shape, dtype=F32):
        dram[name] = nc.dram_tensor(name, list(shape), dtype,
                                    kind="Internal").ap()
        return dram[name]

    x_in = din("x", (T, D))
    out = nc.dram_tensor("out", [T, D], F32, kind="ExternalOutput").ap()

    import json
    spec_inputs = _INPUT_SPECS
    for nm, shp in spec_inputs:
        if nm != "L0_attn_x":
            din(nm, shp)

    X = dscratch("X_resid", (T, D))
    XN_HBM = dscratch("XN_hbm", (T, D), BF16)

    meta = dict(P=P, sb=sb, ps=ps, dram=dram, nc=nc, x_in=x_in, out=out,
                X=X, XN_HBM=XN_HBM)

    build_program(meta, dense_moe=dense_moe, layers=layers)
    stats = P.finalize()
    return nc, stats


_INPUT_SPECS = [
    ("L0_attn_x", (8, 4096, 1024)),
    ("L0_attn_wq", (1024, 1024)), ("L0_attn_wk", (1024, 1024)),
    ("L0_attn_wv", (1024, 1024)), ("L0_attn_wo", (1024, 1024)),
    ("L0_attn_w1", (1024, 2816)), ("L0_attn_w2", (2816, 1024)),
    ("L0_attn_w3", (1024, 2816)),
    ("L0_attn_ln1_g", (1024,)), ("L0_attn_ln1_b", (1024,)),
    ("L0_attn_ln2_g", (1024,)), ("L0_attn_ln2_b", (1024,)),
    ("L0_ffn_router_w", (7, 1024)), ("L0_ffn_routing_bias", (7,)),
    ("L0_ffn_sw1", (2816, 1024)), ("L0_ffn_sw2", (1024, 2816)),
    ("L0_ffn_sw3", (2816, 1024)),
    ("L0_ffn_ew1", (7, 2816, 1024)), ("L0_ffn_ew2", (7, 1024, 2816)),
    ("L0_ffn_ew3", (7, 2816, 1024)),
    ("L1_attn_w_in", (1024, 3072)), ("L1_attn_b_in", (3072,)),
    ("L1_attn_w_conv", (3072, 1, 3)), ("L1_attn_b_conv", (3072,)),
    ("L1_attn_w1", (3, 64)), ("L1_attn_b1", (64,)),
    ("L1_attn_w2", (64, 64)), ("L1_attn_b2", (64,)),
    ("L1_attn_w3", (64, 1024)), ("L1_attn_filt_bias", (1024,)),
    ("L1_attn_w_out", (1024, 1024)), ("L1_attn_b_out", (1024,)),
    ("L1_ffn_gate_w", (2, 1024)),
    ("L1_ffn_w1", (2, 1024, 4096)), ("L1_ffn_b1", (2, 4096)),
    ("L1_ffn_w2", (2, 4096, 1024)), ("L1_ffn_b2", (2, 1024)),
    ("L1_ffn_sw1", (1024, 4096)), ("L1_ffn_sb1", (4096,)),
    ("L1_ffn_sw2", (4096, 1024)), ("L1_ffn_sb2", (1024,)),
    ("L2_attn_Wq", (1024, 1024)), ("L2_attn_Wk", (1024, 1024)),
    ("L2_attn_Wv", (1024, 1024)), ("L2_attn_Wb", (1024, 16)),
    ("L2_attn_Wo", (1024, 1024)),
    ("L2_attn_ln1", (1024,)), ("L2_attn_ln2", (1024,)),
    ("L2_attn_Wg", (1024, 4096)), ("L2_attn_Wu", (1024, 4096)),
    ("L2_attn_Wd", (4096, 1024)),
    ("L2_ffn_W_r", (1024, 1024)), ("L2_ffn_W_k", (1024, 1024)),
    ("L2_ffn_W_v", (1024, 1024)),
    ("L2_ffn_mu_r", (1024,)), ("L2_ffn_mu_k", (1024,)),
    ("L3_attn_Wq", (1024, 1024)), ("L3_attn_bq", (1024,)),
    ("L3_attn_Wk", (1024, 1024)), ("L3_attn_bk", (1024,)),
    ("L3_attn_Wv", (1024, 1024)), ("L3_attn_bv", (1024,)),
    ("L3_attn_Wo", (1024, 1024)), ("L3_attn_bo", (1024,)),
    ("L3_ffn_weight", (1024, 1024)), ("L3_ffn_alpha", (1,)),
    ("L3_ffn_bias", (1024,)),
]



class Ctx:

    def __init__(self, nc, P, sb, ps, dram):
        self.nc = nc
        self.P = P
        self.sb = sb
        self.ps = ps
        self.dram = dram
        self.const = {}
        self.identity_bf16()
        self.identity_f32()
        self.ones_col_bf16()
        self.ones_col_f32()
        self.ones_row_f32()
        self.ones_row_bf16()

    def w(self, name):
        return self.dram[name]

    def identity_bf16(self):
        if "id_bf16" not in self.const:
            P, sb = self.P, self.sb
            io = sb.alloc(128, F32, "c_io")
            P.iota(io, [[1, 128]], base=0, channel_multiplier=-1)
            idn = sb.alloc(128, BF16, "c_idbf")
            P.ts(idn, io, 0.0, ALU.is_equal)
            self.const["id_bf16"] = idn
        return self.const["id_bf16"]

    def identity_f32(self):
        if "id_f32" not in self.const:
            P, sb = self.P, self.sb
            io = sb.alloc(128, F32, "c_iof")
            P.iota(io, [[1, 128]], base=0, channel_multiplier=-1)
            idn = sb.alloc(128, F32, "c_idf")
            P.ts(idn, io, 0.0, ALU.is_equal)
            self.const["id_f32"] = idn
        return self.const["id_f32"]

    def reversal_bf16(self):
        if "rev_bf16" not in self.const:
            P, sb = self.P, self.sb
            io = sb.alloc(128, F32, "c_ior")
            P.iota(io, [[1, 128]], base=-127, channel_multiplier=1)
            idn = sb.alloc(128, BF16, "c_rev")
            P.ts(idn, io, 0.0, ALU.is_equal)
            self.const["rev_bf16"] = idn
        return self.const["rev_bf16"]

    def ones_col_bf16(self):
        if "ones_col" not in self.const:
            t = self.sb.alloc(1, BF16, "c_ones")
            self.P.memset(t, 1.0)
            self.const["ones_col"] = t
        return self.const["ones_col"]

    def ones_col_f32(self):
        if "ones_colf" not in self.const:
            t = self.sb.alloc(1, F32, "c_onesf")
            self.P.memset(t, 1.0)
            self.const["ones_colf"] = t
        return self.const["ones_colf"]

    def ones_row_f32(self):
        if "ones_row" not in self.const:
            t = self.sb.alloc(128, F32, "c_onesr")
            self.P.memset(t[0:1, :], 1.0)
            self.const["ones_row"] = t[0:1, :]
        return self.const["ones_row"]

    def load_w(self, name, chan, rows=None, cols=None, col0=0, row0=0,
               dtype=BF16):
        w = self.dram[name]
        shp = _shape_of(name)
        R = rows or shp[-2]
        Cc = cols or shp[-1]
        tiles = []
        for r0 in range(0, R, 128):
            t = self.sb.alloc(Cc, dtype, f"w_{name}_{r0}")
            src = w[row0 + r0:row0 + r0 + 128, col0:col0 + Cc]
            self.P.dma(POOL, t, src, chan=chan, tag=f"ldw:{name}")
            tiles.append(t)
        return tiles

    def stage_wT(self, name, idx=None):
        key = f"wTstage:{name}:{idx}"
        if key not in self.const:
            w = self.dram[name]
            shp = list(_shape_of(name))
            if idx is not None:
                w = w[idx]
                shp = shp[1:]
            st = self.nc.dram_tensor(f"stg_{name}_{idx}", shp, BF16,
                                     kind="Internal").ap()
            self.P.dma(POOL, st, w, chan="wstage", tag=f"stage:{name}")
            self.const[key] = st
        return self.const[key]

    def load_wT(self, name, chan, idx=None, rows=None, cols=None, col0=0):
        st = self.stage_wT(name, idx)
        shp = _shape_of(name)
        if idx is not None:
            shp = shp[1:]
        R = rows or shp[0]
        Cc = cols or shp[1]
        tiles = []
        for c0 in range(0, Cc, 128):
            t = self.sb.alloc(R, BF16, f"wt_{name}_{c0}")
            src = st[0:R, col0 + c0:col0 + c0 + 128]
            self.P.dma_transpose(SP, t, src, chan=chan, tag=f"ldwT:{name}")
            tiles.append(t)
        return tiles

    def ones_row_bf16(self):
        if "ones_rowb" not in self.const:
            t = self.sb.alloc(128, BF16, "c_onesrb")
            self.P.memset(t[0:1, :], 1.0)
            self.const["ones_rowb"] = t[0:1, :]
        return self.const["ones_rowb"]

    def load_wT_small(self, name, chan, idx=None):
        P, sb, ps = self.P, self.sb, self.ps
        shp = _shape_of(name)
        if idx is not None:
            shp = shp[1:]
        R, Cc = shp
        src = self.dram[name] if idx is None else self.dram[name][idx]
        out = [sb.alloc(R, BF16, f"wsmT_{name}_{c0}")
               for c0 in range(0, Cc, 128)]
        idb = self.identity_bf16()[0:R, 0:R]
        sb.mark()
        rows = sb.alloc(Cc, BF16, f"wsm_{name}")
        P.dma(POOL, rows[0:R, :], src, chan=chan, tag=f"ldsm:{name}")
        for c0 in range(0, Cc, 128):
            pb = ps.acquire()
            pv = ps.ap(pb).bitcast(BF16)
            P.transpose(pv[:, 0:R], rows[0:R, c0:c0 + 128], idb, tag="wsmtr")
            P.copy(out[c0 // 128], pv[:, 0:R])
            ps.release(pb)
        sb.release()
        return out

    def shift64_sel(self, pbase):
        if pbase == 0:
            return self.identity_bf16()[0:64, 0:64]
        if "shift64" not in self.const:
            P, sb = self.P, self.sb
            t = sb.alloc(128, BF16, "c_shift64")
            sb.mark()
            io = sb.alloc(128, F32, "sh_io")
            P.iota(io, [[1, 128]], base=64, channel_multiplier=-1)
            m = sb.alloc(128, F32, "sh_m")
            P.ts(m, io, 0.0, ALU.is_equal)
            P.copy(t, m)
            sb.release()
            self.const["shift64"] = t
        return self.const["shift64"][64:128, 0:64]

    def bound_reg(self, val):
        key = f"breg:{val}"
        if key not in self.const:
            reg = self.nc.alloc_register(mybir.EngineType.Pool,
                                         f"breg{len(self.const)}")
            def em(e, r=reg, v=val):
                return e.reg_mov(r, v)
            self.P.custom(POOL, em, tag="bregmov")
            self.const[key] = reg
        return self.const[key]

    def bcast_row(self, row, psum_idx, n):
        ones = self.ones_row_bf16() if row.dtype == BF16 else self.ones_row_f32()
        pb = self.ps.ap(psum_idx)
        self.P.mm(pb[:, 0:n], ones, row, tag="bcast")
        return pb[:, 0:n]


def _shape_of(name):
    for nm, shp in _INPUT_SPECS:
        if nm == name:
            return shp
    raise KeyError(name)



def mm_fm(C, w_tiles, x_tiles, seg0, nseg, dout, epilogue, segw=SEG,
          extra_acc=None, tag=""):
    P, ps = C.P, C.ps
    ncc = len(x_tiles)
    for jc in range(0, dout, 128):
        for s in range(seg0, seg0 + nseg):
            pb = ps.acquire()
            pap = ps.ap(pb)[:, 0:segw]
            for ci, (wt, xt) in enumerate(zip(w_tiles, x_tiles)):
                P.mm(pap, wt[:, jc:jc + 128], xt[:, s * segw:(s + 1) * segw],
                     start=(ci == 0), stop=(ci == ncc - 1), tag=tag)
            epilogue(pap, jc, s, pb)
            ps.release(pb)


def load_xn_fm(C, src_hbm, chan, cols=None, dtype=BF16, name="xn"):
    cols = cols or T
    tiles = []
    for cc in range(0, D, 128):
        t = C.sb.alloc(cols, dtype, f"{name}_{cc}")
        C.P.dma(SP, t, src_hbm[cc:cc + 128, 0:cols], chan=chan, tag="ldxn")
        tiles.append(t)
    return tiles


def fm_bias_col(C, name, chan, idx=None):
    b = C.dram[name]
    if idx is not None:
        b = b[idx]
        n = _shape_of(name)[1]
    else:
        n = _shape_of(name)[0]
    nch = (n + 127) // 128
    P, sb, ps = C.P, C.sb, C.ps
    t = sb.alloc(nch, F32, f"bias_{name}")
    sb.mark()
    row = sb.alloc(n, F32, f"brow_{name}")
    P.dma(SP, row[0:1, :], b[None, :], chan=chan, tag=f"bias:{name}")
    idf = C.identity_f32()[0:1, 0:1]
    for i in range(nch):
        w = min(128, n - i * 128)
        pb = ps.acquire()
        P.transpose(ps.ap(pb)[0:w, 0:1], row[0:1, i * 128:i * 128 + w], idf,
                    tag="biastr")
        P.copy(t[0:w, i:i + 1], ps.ap(pb)[0:w, 0:1])
        ps.release(pb)
    sb.release()
    return [t[:, i:i + 1] for i in range(nch)]



def build_L3_ffn(C, xn_hbm, acc):
    P, sb, ps, nc = C.P, C.sb, C.ps, C.nc
    sb.mark()
    wch = []
    for r0 in range(0, 1024, 128):
        t = sb.alloc(1024, F32, f"l3w_{r0}")
        P.dma(SP, t, C.dram["L3_ffn_weight"][r0:r0 + 128, :], chan="w0")
        wch.append(t)
    rs = sb.alloc(8, F32, "l3rs")
    for i, t in enumerate(wch):
        def em(e, out=rs, i=i, t=t):
            return e.tensor_reduce(out[:, i:i + 1], t, AX.X, ALU.add,
                                   apply_absolute_value=True)
        P.custom(DVE, em, reads=[t], writes=[rs[:, i:i + 1]], tag="l3abs")
    rssum = sb.alloc(1, F32, "l3rss")
    P.red(rssum, rs, ALU.add)
    pb = ps.acquire()
    P.mm(ps.ap(pb)[0:1, 0:1], C.ones_col_f32(), rssum, tag="l3sum")
    delta = sb.alloc(1, F32, "l3delta")
    P.ts(delta[0:1, :], ps.ap(pb)[0:1, 0:1], 0.7 / (1024.0 * 1024.0), ALU.mult)
    ps.release(pb)
    pb = ps.acquire()
    deltaB_ps = C.bcast_row(delta[0:1, :], pb, 1)
    deltaB = sb.alloc(1, F32, "l3deltaB")
    P.copy(deltaB, deltaB_ps)
    ps.release(pb)
    ndeltaB = sb.alloc(1, F32, "l3ndeltaB")
    P.ts(ndeltaB, deltaB, -1.0, ALU.mult)
    terns = []
    hi = sb.alloc(1024, F32, "l3hi")
    lo = sb.alloc(1024, F32, "l3lo")
    for i, t in enumerate(wch):
        P.ts(hi, t, deltaB, ALU.is_gt)
        P.ts(lo, t, ndeltaB, ALU.is_lt)
        tb = sb.alloc(1024, BF16, f"l3tern_{i}")
        P.tt(tb, hi, lo, ALU.subtract)
        terns.append(tb)
    wT = [sb.alloc(1024, BF16, f"l3wT_{c0}") for c0 in range(0, 1024, 128)]
    idb = C.identity_bf16()
    for oc in range(8):
        for cc in range(8):
            pb = ps.acquire()
            pv = ps.ap(pb).bitcast(BF16)
            P.transpose(pv[:, 0:128], terns[oc][:, cc * 128:(cc + 1) * 128],
                        idb, tag="l3tr")
            P.copy(wT[cc][:, oc * 128:(oc + 1) * 128], pv[:, 0:128])
            ps.release(pb)
    al = sb.alloc(1, F32, "l3al")
    P.dma(SP, al[0:1, :], C.dram["L3_ffn_alpha"][None, :], chan="w0")
    pb = ps.acquire()
    alB_ps = C.bcast_row(al[0:1, :], pb, 1)
    alB = sb.alloc(1, F32, "l3alB")
    P.copy(alB, alB_ps)
    ps.release(pb)
    biases = fm_bias_col(C, "L3_ffn_bias", "w0")
    xt = load_xn_fm(C, xn_hbm, "x0")
    obs = [sb.alloc(T, F32, f"l3out{i}") for i in range(2)]

    def epi(pap, jc, s, pbi):
        ob = obs[(jc // 128) % 2]
        P.ts2(ob[:, s * SEG:(s + 1) * SEG], pap, alB, biases[jc // 128],
              ALU.mult, ALU.add, tag="l3epi")
        if s == NSEG - 1:
            P.dma(POOL, acc[jc:jc + 128, :], ob, chan="acc0",
                  accum_op=ALU.add, tag="l3acc")

    mm_fm(C, wT, xt, 0, NSEG, 1024, epi, tag="l3mm")
    sb.release()



def build_L2_ffn(C, xn_hbm, acc):
    P, sb, ps = C.P, C.sb, C.ps
    sb.mark()
    WIN = 1024
    wrT = C.load_wT("L2_ffn_W_r", "w0")
    wkT = C.load_wT("L2_ffn_W_k", "w0")
    wvT = C.load_wT("L2_ffn_W_v", "w0")
    mur = fm_bias_col(C, "L2_ffn_mu_r", "w1")
    muk = fm_bias_col(C, "L2_ffn_mu_k", "w1")
    xt = [sb.alloc(WIN + 1, BF16, f"l2x_{i}") for i in range(8)]
    xr = [sb.alloc(WIN, BF16, f"l2xr_{i}") for i in range(8)]
    xk = [sb.alloc(WIN, BF16, f"l2xk_{i}") for i in range(8)]
    tmp = sb.alloc(WIN, F32, "l2tmp")
    r = [sb.alloc(WIN, BF16, f"l2r_{i}") for i in range(8)]
    ka = [sb.alloc(WIN, BF16, f"l2ka_{i}") for i in range(8)]
    obs = [sb.alloc(SEG, F32, f"l2out{i}") for i in range(4)]

    for w0 in range(0, T, WIN):
        if w0 == 0:
            for i in range(8):
                P.dma(SP, xt[i][:, 1:WIN + 1],
                      xn_hbm[i * 128:(i + 1) * 128, 0:WIN], chan="x0")
                P.memset(xt[i][:, 0:1], 0.0, tag="l2z")
        else:
            for i in range(8):
                P.dma(SP, xt[i],
                      xn_hbm[i * 128:(i + 1) * 128, w0 - 1:w0 + WIN],
                      chan="x0")
        for i in range(8):
            for dst, mu in ((xr[i], mur[i]), (xk[i], muk[i])):
                P.tt(tmp, xt[i][:, 1:WIN + 1], xt[i][:, 0:WIN], ALU.subtract,
                     tag="l2sh1")
                P.stt(dst, tmp, mu, xt[i][:, 0:WIN], ALU.mult, ALU.add,
                      tag="l2sh2")

        def epi_r(pap, jc, s, pbi):
            P.act(r[jc // 128][:, s * SEG:(s + 1) * SEG], pap, AF.Sigmoid,
                  tag="l2sig")

        mm_fm(C, wrT, xr, 0, WIN // SEG, 1024, epi_r, tag="l2mmr")

        def epi_k(pap, jc, s, pbi):
            sl = slice(s * SEG, (s + 1) * SEG)
            t = ka[jc // 128]
            P.act(t[:, sl], pap, AF.Relu, tag="l2relu")
            P.act(t[:, sl], t[:, sl], AF.Square, tag="l2sq")

        mm_fm(C, wkT, xk, 0, WIN // SEG, 1024, epi_k, tag="l2mmk")

        def epi_v(pap, jc, s, pbi, w0=w0):
            sl = slice(s * SEG, (s + 1) * SEG)
            ob = obs[(jc // 128 * (WIN // SEG) + s) % 4]
            P.tt(ob, pap, r[jc // 128][:, sl], ALU.mult, tag="l2mul")
            P.dma(POOL, acc[jc:jc + 128, w0 + s * SEG:w0 + (s + 1) * SEG],
                  ob, chan="acc" + str((jc // 128) % 2), accum_op=ALU.add,
                  tag="l2acc")

        mm_fm(C, wvT, ka, 0, WIN // SEG, 1024, epi_v, tag="l2mmv")
    sb.release()



class RopeGen:

    def __init__(self, C, name, W=SEG):
        P, sb = C.P, C.sb
        self.C = C
        self.W = W
        self.cos = sb.alloc(W, BF16, f"cos_{name}")
        self.sin = sb.alloc(W, BF16, f"sin_{name}")
        self.frow = sb.alloc(128, F32, f"ropefr_{name}")
        P.iota(self.frow[0:1, :], [[0, 4], [1, 32]], base=0,
               channel_multiplier=0)
        P.act(self.frow[0:1, :], self.frow[0:1, :], AF.Exp,
              scale=-math.log(10000.0) / 32.0)
        self.pos0r = sb.alloc(W, F32, f"ropep0_{name}")
        P.iota(self.pos0r[0:1, :], [[1, W]], base=0, channel_multiplier=0)
        self.posr = sb.alloc(SEG, F32, f"ropepos_{name}")
        self.ang = sb.alloc(SEG, F32, f"ropeang_{name}")
        self.qi = sb.alloc(SEG, mybir.dt.int32, f"ropeqi_{name}")
        self.qf = sb.alloc(SEG, F32, f"ropeqf_{name}")
        self.r = sb.alloc(SEG, F32, f"roper_{name}")

    def gen(self, pos0):
        C = self.C
        P, ps = C.P, C.ps
        CHI = 6.28125
        CLO = 2 * math.pi - CHI
        posr, ang, qi, qf, r = self.posr, self.ang, self.qi, self.qf, self.r
        for p0 in range(0, self.W, SEG):
            Wc = min(SEG, self.W - p0)
            P.ts(posr[0:1, 0:Wc], self.pos0r[0:1, p0:p0 + Wc], float(pos0),
                 ALU.add, eng=POOL, tag="ropepos")
            pb = ps.acquire()
            P.mm(ps.ap(pb)[:, 0:Wc], self.frow[0:1, :], posr[0:1, 0:Wc],
                 tag="ropeouter")
            P.copy(ang[:, 0:Wc], ps.ap(pb)[:, 0:Wc])
            ps.release(pb)
            for out, shift in ((self.sin, 0.0), (self.cos, math.pi / 2)):
                ra = r[:, 0:Wc]
                P.ts(ra, ang[:, 0:Wc], shift, ALU.add)
                P.ts(ra, ra, 1.0 / (2 * math.pi), ALU.mult)
                P.copy(qi[:, 0:Wc], ra)
                P.copy(qf[:, 0:Wc], qi[:, 0:Wc])
                P.ts(ra, ang[:, 0:Wc], shift, ALU.add)
                P.stt(ra, qf[:, 0:Wc], -CHI, ra, ALU.mult, ALU.add)
                P.stt(ra, qf[:, 0:Wc], -CLO, ra, ALU.mult, ALU.add)
                P.act(out[:, p0:p0 + Wc], ra, AF.Sin, tag="ropesin")


def build_rrot(C):
    if "rrot" in C.const:
        return C.const["rrot"]
    P, sb = C.P, C.sb
    out = sb.alloc(128, BF16, "c_rrot")
    sb.mark()
    io = sb.alloc(128, F32, "rr_io")
    m1 = sb.alloc(128, F32, "rr_m1")
    m2 = sb.alloc(128, F32, "rr_m2")
    acc = sb.alloc(128, F32, "rr_acc")
    P.iota(io, [[1, 128]], base=32, channel_multiplier=-1)
    P.ts(m1, io, 0.0, ALU.is_equal)
    P.iota(io, [[1, 128]], base=-32, channel_multiplier=-1)
    P.ts(m2, io, 0.0, ALU.is_equal)
    qm = sb.alloc(1, F32, "rr_qm")
    P.iota(qm, [[0, 1]], base=0, channel_multiplier=1)
    ge64 = sb.alloc(1, F32, "rr_ge")
    P.ts(ge64, qm, 63.5, ALU.is_gt)
    P.stt(qm, ge64, -64.0, qm, ALU.mult, ALU.add)
    s1 = sb.alloc(1, F32, "rr_s1")
    P.ts(s1, qm, 31.5, ALU.is_gt)
    s2 = sb.alloc(1, F32, "rr_s2")
    P.ts(s2, s1, -1.0, ALU.mult)
    P.ts(s2, s2, 1.0, ALU.add)
    P.ts(m1, m1, s1, ALU.mult)
    P.ts(m1, m1, -1.0, ALU.mult)
    P.ts(m2, m2, s2, ALU.mult)
    P.tt(acc, m1, m2, ALU.add)
    P.copy(out, acc)
    sb.release()
    C.const["rrot"] = out
    return out


def build_band_mask(C):
    if "maskA" in C.const:
        return C.const["maskA"]
    P, sb = C.P, C.sb
    out = sb.alloc(256, BF16, "c_maskA")
    sb.mark()
    io = sb.alloc(256, F32, "ma_io")
    m1 = sb.alloc(256, F32, "ma_m1")
    P.iota(io, [[1, 256]], base=0, channel_multiplier=-1)
    P.ts(m1, io, -0.5, ALU.is_gt)
    P.ts(io, io, 128.5, ALU.is_lt)
    P.tt(m1, m1, io, ALU.mult)
    P.copy(out, m1)
    sb.release()
    C.const["maskA"] = out
    return out


def build_causal_mask(C):
    if "maskC" in C.const:
        return C.const["maskC"]
    P, sb = C.P, C.sb
    out = sb.alloc(1024, BF16, "c_maskC")
    sb.mark()
    io = sb.alloc(1024, F32, "mc_io")
    P.iota(io, [[1, 1024]], base=-384, channel_multiplier=-1)
    P.ts(io, io, -0.5, ALU.is_gt)
    P.copy(out, io)
    sb.release()
    C.const["maskC"] = out
    return out


def ln_window(C, xtiles, W, out_tiles, gcols=None, bcols=None, eps=1e-5,
              out_dtype=BF16, tag="ln"):
    P, sb, ps = C.P, C.sb, C.ps
    sb.mark()
    is_f32 = xtiles[0].dtype == F32
    xb2 = [sb.alloc(SEG, BF16, f"lnxb{j}") for j in range(2)] if is_f32 \
        else None
    sq2 = [sb.alloc(SEG, BF16, f"lnsq{j}") for j in range(2)]
    mu = sb.alloc(SEG, F32, "lnmu")
    var = sb.alloc(SEG, F32, "lnvar")
    rstd = sb.alloc(SEG, F32, "lnrstd")
    nb = sb.alloc(SEG, F32, "lnnb")
    tmp = sb.alloc(SEG, F32, "lntmp")
    ones = C.ones_col_bf16()
    for s0 in range(0, W, SEG):
        Wc = min(SEG, W - s0)
        pb_s = ps.acquire()
        pb_q = ps.acquire()
        for ci in range(8):
            xs = xtiles[ci][:, s0:s0 + Wc]
            sq = sq2[ci % 2]
            if is_f32:
                xb = xb2[ci % 2]
                P.copy(xb[:, 0:Wc], xs, tag=tag + "cast")
                xin = xb[:, 0:Wc]
            else:
                xin = xs
            P.act(sq[:, 0:Wc], xs, AF.Square, tag=tag + "sq")
            P.mm(ps.ap(pb_s)[0:1, 0:Wc], ones, xin,
                 start=(ci == 0), stop=(ci == 7), tag=tag + "s")
            P.mm(ps.ap(pb_q)[0:1, 0:Wc], ones, sq[:, 0:Wc],
                 start=(ci == 0), stop=(ci == 7), tag=tag + "q")
        mu_r = mu[0:1, 0:Wc]
        var_r = var[0:1, 0:Wc]
        rs_r = rstd[0:1, 0:Wc]
        nb_r = nb[0:1, 0:Wc]
        P.ts(mu_r, ps.ap(pb_s)[0:1, 0:Wc], 1.0 / D, ALU.mult, tag=tag)
        ps.release(pb_s)
        P.tt(var_r, mu_r, mu_r, ALU.mult, tag=tag)
        P.stt(var_r, ps.ap(pb_q)[0:1, 0:Wc], 1.0 / D, var_r, ALU.mult,
              ALU.subtract, tag=tag)
        ps.release(pb_q)
        P.ts(var_r, var_r, eps, ALU.add, tag=tag)
        P.act(rs_r, var_r, AF.Sqrt, tag=tag)
        P.recip(rs_r, rs_r, tag=tag)
        P.tt(nb_r, mu_r, rs_r, ALU.mult, tag=tag)
        P.ts(nb_r, nb_r, -1.0, ALU.mult, tag=tag)
        pb_a = ps.acquire()
        pb_b = ps.acquire()
        sB = C.bcast_row(rs_r, pb_a, Wc)
        bB = C.bcast_row(nb_r, pb_b, Wc)
        for ci in range(8):
            xs = xtiles[ci][:, s0:s0 + Wc]
            ot = out_tiles[ci][:, s0:s0 + Wc]
            P.tt(tmp[:, 0:Wc], xs, sB, ALU.mult, tag=tag + "ap1")
            if gcols is None:
                P.tt(ot, tmp[:, 0:Wc], bB, ALU.add, tag=tag + "ap2")
            else:
                P.tt(tmp[:, 0:Wc], tmp[:, 0:Wc], bB, ALU.add, tag=tag + "ap2")
                P.ts2(ot, tmp[:, 0:Wc], gcols[ci], bcols[ci], ALU.mult,
                      ALU.add, tag=tag + "gb")
        ps.release(pb_a)
        ps.release(pb_b)
    sb.release()



def rms_window(C, xtiles, W, out_tiles, gcols, eps, tag="rms"):
    P, sb, ps = C.P, C.sb, C.ps
    sb.mark()
    sq = sb.alloc(W, BF16, "rmssq")
    var = sb.alloc(W, F32, "rmsvar")
    ones = C.ones_col_bf16()
    pb_q = ps.acquire()
    for ci in range(8):
        P.act(sq, xtiles[ci][:, 0:W], AF.Square, tag=tag + "sq")
        P.mm(ps.ap(pb_q)[0:1, 0:W], ones, sq, start=(ci == 0),
             stop=(ci == 7), tag=tag + "mm")
    vr = var[0:1, :]
    P.ts(vr, ps.ap(pb_q)[0:1, 0:W], 1.0 / D, ALU.mult, tag=tag)
    ps.release(pb_q)
    P.ts(vr, vr, eps, ALU.add, tag=tag)
    P.act(vr, vr, AF.Sqrt, tag=tag)
    P.recip(vr, vr, tag=tag)
    pb_a = ps.acquire()
    sB = C.bcast_row(vr, pb_a, W)
    tmp = sb.alloc(W, F32, "rmstmp")
    for ci in range(8):
        P.tt(tmp, xtiles[ci][:, 0:W], sB, ALU.mult, tag=tag + "ap")
        P.ts(out_tiles[ci][:, 0:W], tmp, gcols[ci], ALU.mult, tag=tag + "g")
    ps.release(pb_a)
    sb.release()


def build_L2_attn(C, xn_hbm, acc):
    P, sb, ps, nc = C.P, C.sb, C.ps, C.nc
    sb.mark()
    h2_hbm = nc.dram_tensor("l2_h2", [D, T], BF16,
                            kind="ExternalOutput" if globals().get("_L2_DEBUG_H2")
                            else "Internal").ap()
    idb = C.identity_bf16()
    ln1 = fm_bias_col(C, "L2_attn_ln1", "w1")
    ln2 = fm_bias_col(C, "L2_attn_ln2", "w1")

    msk = {}
    for name in ("L", "Us", "Ui"):
        msk[name] = sb.alloc(1024, BF16, f"c_l2m{name}")
    sb.mark()
    a1 = sb.alloc(1024, F32, "l2a1")
    P.iota(a1, [[0, 16], [1, 64]], base=0, channel_multiplier=0)
    pcol = sb.alloc(1, F32, "l2pc")
    P.iota(pcol, [[0, 1]], base=0, channel_multiplier=1)
    ge = sb.alloc(1, F32, "l2ge")
    P.ts(ge, pcol, 63.5, ALU.is_gt)
    P.stt(pcol, ge, -64.0, pcol, ALU.mult, ALU.add)
    mf = sb.alloc(1024, F32, "l2mf")
    for name, op in (("L", ALU.is_lt), ("Us", ALU.is_gt), ("Ui", ALU.is_ge)):
        P.ts(mf, a1, pcol, op)
        P.copy(msk[name], mf)
    sb.release()

    sb.mark()
    wq = C.load_w("L2_attn_Wq", "w0")
    wk = C.load_w("L2_attn_Wk", "w0b")
    wv = C.load_w("L2_attn_Wv", "w0c")
    wo = C.load_w("L2_attn_Wo", "w0")
    wb = C.load_w("L2_attn_Wb", "w1")

    S_buf = sb.alloc(SEG, F32, "l2S")
    Sb = sb.alloc(SEG, BF16, "l2Sb")
    P.memset(S_buf, 0.0)
    P.memset(Sb, 0.0)

    xnw = [sb.alloc(SEG, BF16, f"l2xn_{i}") for i in range(8)]
    x2w = [sb.alloc(SEG, BF16, f"l2x2_{i}") for i in range(8)]
    qtm = [sb.alloc(D, BF16, f"l2q_{i}") for i in range(4)]
    ktm = [sb.alloc(D, BF16, f"l2k_{i}") for i in range(4)]
    kbtm = [sb.alloc(D, BF16, f"l2kb_{i}") for i in range(4)]
    vtm = [sb.alloc(D, BF16, f"l2v_{i}") for i in range(4)]
    btm = [sb.alloc(16, F32, f"l2b_{i}") for i in range(4)]
    Qfm = [sb.alloc(SEG, BF16, f"l2Qf_{i}") for i in range(8)]
    Kfm = [sb.alloc(SEG, BF16, f"l2Kf_{i}") for i in range(8)]
    Kbfm = [sb.alloc(SEG, BF16, f"l2Kbf_{i}") for i in range(8)]
    ofm = [sb.alloc(SEG, BF16, f"l2of_{i}") for i in range(8)]
    v0 = sb.alloc(D, BF16, "l2v0")
    kb0 = sb.alloc(D, BF16, "l2kb0")
    obuf = sb.alloc(D, BF16, "l2obuf")
    A_sb = sb.alloc(D, BF16, "l2A")
    AT_sb = sb.alloc(D, BF16, "l2AT")
    PW = [sb.alloc(D, BF16, f"l2PW{i}") for i in range(2)]
    PWT = [sb.alloc(D, BF16, f"l2PWT{i}") for i in range(5)]
    Vw = sb.alloc(D, BF16, "l2Vw")
    sqs = sb.alloc(D, BF16, "l2sqs")
    nrm = sb.alloc(16, F32, "l2nrm")
    h2w = [sb.alloc(SEG, BF16, f"l2h2_{i}") for i in range(8)]
    hstg = sb.alloc(SEG, F32, "l2hstg")

    for w in (range(NSEG) if not globals().get("_L2_SKIP_P1") else []):
        ws = w * SEG
        for i in range(8):
            P.dma(SP, xnw[i], xn_hbm[i * 128:(i + 1) * 128, ws:ws + SEG],
                  chan="x" + str(i % 2), tag="l2ldx")
        rms_window(C, xnw, SEG, x2w, ln1, 1e-6, tag="l2rms1")
        for tt in range(4):
            tsl = slice(tt * 128, (tt + 1) * 128)
            for dst, wmat, act in ((qtm, wq, AF.Silu), (ktm, wk, AF.Silu),
                                   (vtm, wv, AF.Silu)):
                for nh in range(2):
                    pb = ps.acquire()
                    for ci in range(8):
                        P.mm(ps.ap(pb)[:, 0:SEG], x2w[ci][:, tsl],
                             wmat[ci][:, nh * SEG:(nh + 1) * SEG],
                             start=(ci == 0), stop=(ci == 7), tag="l2proj")
                    P.act(dst[tt][:, nh * SEG:(nh + 1) * SEG],
                          ps.ap(pb)[:, 0:SEG], act, tag="l2act")
                    ps.release(pb)
            pb = ps.acquire()
            for ci in range(8):
                P.mm(ps.ap(pb)[:, 0:16], x2w[ci][:, tsl], wb[ci],
                     start=(ci == 0), stop=(ci == 7), tag="l2bproj")
            P.act(btm[tt], ps.ap(pb)[:, 0:16], AF.Sigmoid, tag="l2bsig")
            ps.release(pb)
            for dst in (qtm, ktm):
                tile = dst[tt]
                P.act(sqs[:, 0:1024], tile, AF.Square, tag="l2sq")
                def rd(e, out=nrm, in_=sqs):
                    import concourse.ap as _c
                    v = _c.AP(in_.tensor, offset=in_.offset,
                              ap=[in_.ap[0], [64, 16], [1, 64]])
                    return e.tensor_reduce(out, v, AX.X, ALU.add)
                P.custom(DVE, rd, reads=[sqs], writes=[nrm], tag="l2red")
                P.ts(nrm, nrm, 1e-6, ALU.add, tag="l2eps")
                P.act(nrm, nrm, AF.Sqrt, tag="l2sqrt")
                P.recip(nrm, nrm, tag="l2rec")
                for h in range(16):
                    P.ts(tile[:, h * 64:(h + 1) * 64],
                         tile[:, h * 64:(h + 1) * 64], nrm[:, h:h + 1],
                         ALU.mult, tag="l2nap")
            for h in range(16):
                P.ts(kbtm[tt][:, h * 64:(h + 1) * 64],
                     ktm[tt][:, h * 64:(h + 1) * 64], btm[tt][:, h:h + 1],
                     ALU.mult, tag="l2kb")
            for src, dstf in ((qtm, Qfm), (ktm, Kfm), (kbtm, Kbfm)):
                for cc in range(8):
                    pb = ps.acquire()
                    pv = ps.ap(pb).bitcast(BF16)
                    P.transpose(pv[:, 0:128],
                                src[tt][:, cc * 128:(cc + 1) * 128], idb,
                                tag="l2tr")
                    P.copy(dstf[cc][:, tsl], pv[:, 0:128], eng=ACT,
                           tag="l2tre")
                    ps.release(pb)
        for ch in (range(8) if not globals().get("_L2_SKIP_DELTA") else []):
            pbase = (ch % 2) * 64
            tt = ch // 2
            csl = slice(ch * 64, (ch + 1) * 64)
            psl = slice(pbase, pbase + 64)
            exsel = C.shift64_sel(pbase)

            def scol(h):
                return (h % 2) * 512 + (h // 2) * 64

            def sslot(buf, h):
                return buf[0:64, scol(h):scol(h) + 64]

            def halves(buf):
                return (buf[0:64, 0:512], buf[0:64, 512:1024])

            def headmm(banks, lhs_of, rhs_of, start=True, stop=True, tag="",
                       colpos=0):
                if banks is None:
                    banks = nextpair()
                headmm.last = banks
                for h in range(16):
                    bk = banks[h % 2]
                    out = ps.ap(bk)[colpos:colpos + 64,
                                    (h // 2) * 64:(h // 2) * 64 + 64]
                    lhs = lhs_of(h)
                    P.mm(out, lhs, rhs_of(h),
                         start=(start and h // 2 == 0),
                         stop=(stop and h // 2 == 7), tag=tag,
                         tile_position=(lhs.base_partition(), colpos))

            def kslice(fmbuf, h):
                return fmbuf[h // 2][(h % 2) * 64:(h % 2) * 64 + 64, csl]

            def sbslice(h):
                hb = (h % 2) * 64
                return Sb[hb:hb + 64, (h // 2) * 64:(h // 2) * 64 + 64]

            bA = (ps.acquire(), ps.acquire())
            bB = (ps.acquire(), ps.acquire())
            _pair = [bA, bB]
            _pc = [0]

            def nextpair():
                _pc[0] += 1
                return _pair[_pc[0] % 2]
            for j in range(2):
                P.mm(ps.ap(bA[j])[0:64, 0:512], exsel,
                     vtm[tt][psl, j * 512:(j + 1) * 512], tag="l2exv")
                P.copy(v0[0:64, j * 512:(j + 1) * 512],
                       ps.ap(bA[j])[0:64, 0:512], tag="l2exve")
                P.mm(ps.ap(bA[j])[0:64, 0:512], exsel,
                     kbtm[tt][psl, j * 512:(j + 1) * 512], tag="l2exk")
                P.copy(kb0[0:64, j * 512:(j + 1) * 512],
                       ps.ap(bA[j])[0:64, 0:512], eng=ACT, tag="l2exke")
            headmm(None, lambda h: kslice(Kfm, h), lambda h: kslice(Kbfm, h),
                   tag="l2mA")
            for j in range(2):
                P.tt(halves(A_sb)[j], ps.ap(headmm.last[j])[0:64, 0:512],
                     msk["L"][0:64, 0:512], ALU.mult, tag="l2Amask")
            headmm(None, lambda h: kslice(Kbfm, h), lambda h: kslice(Kfm, h),
                   tag="l2mAT")
            for j in range(2):
                P.tt(halves(AT_sb)[j], ps.ap(headmm.last[j])[0:64, 0:512],
                     msk["Us"][0:64, 0:512], ALU.mult, tag="l2ATmask")
            headmm(None, lambda h: kslice(Kfm, h), lambda h: sbslice(h),
                   tag="l2mKS")
            def parity_view(buf, j):
                import concourse.ap as _cc
                return _cc.AP(buf.tensor, offset=buf.offset + j * 64,
                              ap=[[buf.ap[0][0], 64], [128, 8], [1, 64]])
            for j in range(2):
                P.tt(halves(Vw)[j], parity_view(v0, j),
                     ps.ap(headmm.last[j])[0:64, 0:512], ALU.subtract,
                     tag="l2R")
            curP, curPT = A_sb, AT_sb
            for lvl in range(5):
                nP = PW[lvl % 2]
                nPT = PWT[lvl]
                headmm(None, lambda h: sslot(curPT, h),
                       lambda h: sslot(curP, h), tag="l2sq1")
                for j in range(2):
                    P.copy(halves(nP)[j], ps.ap(headmm.last[j])[0:64, 0:512],
                           tag="l2sqe")
                headmm(None, lambda h: sslot(curP, h),
                       lambda h: sslot(curPT, h), tag="l2sq2")
                for j in range(2):
                    P.copy(halves(nPT)[j],
                           ps.ap(headmm.last[j])[0:64, 0:512],
                           eng=ACT, tag="l2sqe2")
                curP, curPT = nP, nPT
            for lvl in range(4, -1, -1):
                pt = PWT[lvl]
                headmm(None, lambda h: sslot(pt, h), lambda h: sslot(Vw, h),
                       tag="l2ap")
                for j in range(2):
                    P.tt(halves(Vw)[j], halves(Vw)[j],
                         ps.ap(headmm.last[j])[0:64, 0:512], ALU.add,
                         tag="l2apa")
            headmm(bA, lambda h: sslot(AT_sb, h), lambda h: sslot(Vw, h),
                   tag="l2fin")
            for j in range(2):
                P.tt(halves(Vw)[j], halves(Vw)[j],
                     ps.ap(bA[j])[0:64, 0:512], ALU.subtract, tag="l2fins")
            headmm(None, lambda h: kslice(Kbfm, h), lambda h: kslice(Qfm, h),
                   tag="l2mBT")
            for j in range(2):
                P.tt(halves(A_sb)[j], ps.ap(headmm.last[j])[0:64, 0:512],
                     msk["Ui"][0:64, 0:512], ALU.mult, tag="l2BTmask")
            headmm(None, lambda h: kslice(Qfm, h), lambda h: sbslice(h),
                   tag="l2mQS")
            bQS = headmm.last
            headmm(None, lambda h: sslot(A_sb, h), lambda h: sslot(Vw, h),
                   tag="l2mBU")
            bBU = headmm.last
            for j in range(2):
                pw = parity_view(obuf, j)
                P.copy(pw, ps.ap(bQS[j])[0:64, 0:512], tag="l2oq")
                P.tt(pw, pw, ps.ap(bBU[j])[0:64, 0:512], ALU.add, tag="l2ou")
            pS = (ps.acquire(), ps.acquire())
            for h in range(16):
                hb = (h % 2) * 64
                P.mm(ps.ap(pS[h % 2])[hb:hb + 64,
                                      (h // 2) * 64:(h // 2) * 64 + 64],
                     kb0[0:64, h * 64:(h + 1) * 64], sslot(Vw, h),
                     start=(h // 2 == 0), stop=(h // 2 == 7), tag="l2mS",
                     tile_position=(0, hb))
            P.tt(S_buf[0:64, :], S_buf[0:64, :], ps.ap(pS[0])[0:64, :],
                 ALU.add, tag="l2Sadd")
            P.tt(S_buf[64:128, :], S_buf[64:128, :], ps.ap(pS[1])[64:128, :],
                 ALU.add, tag="l2Sadd2")
            ps.release(pS[0])
            ps.release(pS[1])
            P.copy(Sb, S_buf, eng=ACT, tag="l2Sb")
            for cc in range(8):
                pb2 = ps.acquire()
                pv = ps.ap(pb2).bitcast(BF16)
                P.transpose(pv[:, 0:64], obuf[0:64, cc * 128:(cc + 1) * 128],
                            idb[0:64, 0:64], tag="l2otr")
                P.copy(ofm[cc][:, ch * 64:(ch + 1) * 64], pv[:, 0:64],
                       eng=ACT, tag="l2otre")
                ps.release(pb2)
            ps.release(bA[0])
            ps.release(bA[1])
            ps.release(bB[0])
            ps.release(bB[1])
        for jc in range(8):
            pb = ps.acquire()
            for ci in range(8):
                P.mm(ps.ap(pb)[:, 0:SEG], wo[ci][:, jc * 128:(jc + 1) * 128],
                     ofm[ci], start=(ci == 0), stop=(ci == 7), tag="l2wo")
            P.tt(h2w[jc], ps.ap(pb)[:, 0:SEG], xnw[jc], ALU.add, tag="l2h2")
            ps.release(pb)
            P.copy(hstg, h2w[jc], eng=ACT, tag="l2hstg")
            P.dma(POOL, acc[jc * 128:(jc + 1) * 128, ws:ws + SEG], hstg,
                  chan="acc0", accum_op=ALU.add, tag="l2hacc")
            P.dma(SP, h2_hbm[jc * 128:(jc + 1) * 128, ws:ws + SEG], h2w[jc],
                  chan="y" + str(jc % 2), tag="l2h2w")
    sb.release()

    NHC = 16
    for half in range(2):
        sb.mark()
        wg = C.load_w("L2_attn_Wg", "w0", cols=2048, col0=half * 2048)
        wu = C.load_w("L2_attn_Wu", "w0b", cols=2048, col0=half * 2048)
        wd = C.load_w("L2_attn_Wd", "w0c", rows=2048, row0=half * 2048)
        hw2 = [[sb.alloc(SEG, BF16, f"l2ghw2_{j}_{i}") for i in range(8)]
               for j in range(2)]
        yw = [sb.alloc(SEG, BF16, f"l2gy_{i}") for i in range(8)]
        tb = [sb.alloc(SEG, BF16, f"l2gt_{i}") for i in range(NHC)]
        sil = sb.alloc(SEG, BF16, "l2gsil")
        obs = [sb.alloc(SEG, F32, f"l2gob{i}") for i in range(2)]
        for s in range(NSEG):
            hw = hw2[s % 2]
            for i in range(8):
                P.dma(SP, hw[i], h2_hbm[i * 128:(i + 1) * 128,
                                        s * SEG:(s + 1) * SEG],
                      chan="x" + str(i % 2))
            rms_window(C, hw, SEG, yw, ln2, 1e-6, tag="l2rms2")
            if globals().get("_L2_DBG2") is not None and s == 0 and half == 0:
                dbg = globals()["_L2_DBG2"]
                P.dma(SP, dbg[0:128, 0:SEG], hw[0], chan="dbg")
                P.dma(SP, dbg[128:256, 0:SEG], yw[0], chan="dbg")
                lncol = ln2[0].tensor.ap() if hasattr(ln2[0], "tensor") else None
                dbgf = sb.alloc(8, F32, "dbgln")
                for ii in range(8):
                    P.copy(dbgf[:, ii:ii + 1], ln2[ii])
                dbgb = sb.alloc(8, BF16, "dbgbb")
                P.copy(dbgb, dbgf)
                P.dma(SP, dbg[256:384, 0:8], dbgb, chan="dbg")
            for hc in range(NHC):
                pa = ps.acquire()
                for ci in range(8):
                    P.mm(ps.ap(pa)[:, 0:SEG],
                         wg[ci][:, hc * 128:(hc + 1) * 128],
                         yw[ci], start=(ci == 0), stop=(ci == 7), tag="l2g1")
                pbk = ps.acquire()
                for ci in range(8):
                    P.mm(ps.ap(pbk)[:, 0:SEG],
                         wu[ci][:, hc * 128:(hc + 1) * 128],
                         yw[ci], start=(ci == 0), stop=(ci == 7), tag="l2g3")
                P.act(sil, ps.ap(pa)[:, 0:SEG], AF.Silu, tag="l2gsil")
                ps.release(pa)
                P.tt(tb[hc], sil, ps.ap(pbk)[:, 0:SEG], ALU.mult, tag="l2gh")
                ps.release(pbk)
                if (globals().get("_L2_DBG2") is not None and s == 0
                        and half == 0 and hc == 0):
                    P.dma(SP, globals()["_L2_DBG2"][256:384, 0:SEG], tb[0],
                          chan="dbg")
            for dc in range(8):
                pb = ps.acquire()
                for hc in range(NHC):
                    P.mm(ps.ap(pb)[:, 0:SEG],
                         wd[hc][:, dc * 128:(dc + 1) * 128],
                         tb[hc], start=(hc == 0), stop=(hc == NHC - 1),
                         tag="l2g2")
                ob = obs[dc % 2]
                P.copy(ob, ps.ap(pb)[:, 0:SEG])
                ps.release(pb)
                P.dma(POOL, acc[dc * 128:(dc + 1) * 128,
                                s * SEG:(s + 1) * SEG], ob,
                      chan="acc" + str(dc % 2), accum_op=ALU.add,
                      tag="l2gacc")
        sb.release()
    sb.release()



def build_L1_attn(C, xn_hbm, acc):
    import concourse.ap as _capmod
    P, sb, ps, nc = C.P, C.sb, C.ps, C.nc
    sb.mark()
    s0_hbm = nc.dram_tensor("l1_s0", [D, T], BF16, kind="Internal").ap()
    vp_hbm = nc.dram_tensor("l1_vp", [D, T], BF16, kind="Internal").ap()
    hpad = nc.dram_tensor("l1_hpad", [2048], F32, kind="Internal").ap()
    idb = C.identity_bf16()
    rrev = C.reversal_bf16()

    sb.mark()
    b1col = fm_bias_col(C, "L1_attn_b1", "w1")[0]
    hm = sb.alloc(1, F32, "l1hm")
    P.act(hm[0:64, :], b1col[0:64, :], AF.Silu, tag="l1hmlp")
    hmb = sb.alloc(1, BF16, "l1hmb")
    P.copy(hmb[0:64, :], hm[0:64, :])
    w2t = sb.alloc(64, BF16, "l1w2")
    P.dma(POOL, w2t[0:64, :], C.dram["L1_attn_w2"], chan="w1")
    pb = ps.acquire()
    P.mm(ps.ap(pb)[0:1, 0:64], hmb[0:64, 0:1], w2t[0:64, :], tag="l1mm2")
    r2 = sb.alloc(64, F32, "l1r2")
    b2row = sb.alloc(64, F32, "l1b2r")
    P.dma(SP, b2row[0:1, :], C.dram["L1_attn_b2"][None, :], chan="w1")
    P.tt(r2[0:1, :], ps.ap(pb)[0:1, 0:64], b2row[0:1, :], ALU.add)
    ps.release(pb)
    P.act(r2[0:1, :], r2[0:1, :], AF.Silu, tag="l1silu2")
    r2b = sb.alloc(64, BF16, "l1r2b")
    P.copy(r2b[0:1, :], r2[0:1, :])
    pb = ps.acquire()
    pv = ps.ap(pb).bitcast(BF16)
    P.transpose(pv[0:64, 0:1], r2b[0:1, 0:64], idb[0:1, 0:1], tag="l1tr")
    r2c = sb.alloc(1, BF16, "l1r2c")
    P.copy(r2c[0:64, :], pv[0:64, 0:1])
    ps.release(pb)
    w3t = sb.alloc(1024, BF16, "l1w3")
    P.dma(POOL, w3t[0:64, :], C.dram["L1_attn_w3"], chan="w1")
    hrow = sb.alloc(1024, F32, "l1hrow")
    for half in range(2):
        pb = ps.acquire()
        P.mm(ps.ap(pb)[0:1, 0:SEG], r2c[0:64, :],
             w3t[0:64, half * SEG:(half + 1) * SEG], tag="l1mm3")
        P.copy(hrow[0:1, half * SEG:(half + 1) * SEG],
               ps.ap(pb)[0:1, 0:SEG])
        ps.release(pb)
    zr = sb.alloc(1024, F32, "l1zr")
    P.memset(zr[0:1, :], 0.0)
    P.dma(SP, hpad[None, 0:512], zr[0:1, 0:512], chan="hp")
    P.dma(SP, hpad[None, 1536:2048], zr[0:1, 0:512], chan="hp")
    P.dma(SP, hpad[None, 512:1536], hrow[0:1, :], chan="hp", tag="l1hpad")
    sb.release()

    sb.mark()
    win0 = C.load_w("L1_attn_w_in", "w0", cols=1024, col0=0)
    win2 = C.load_w("L1_attn_w_in", "w0b", cols=1024, col0=2048)
    bin0 = fm_bias_col(C, "L1_attn_b_in", "w1")
    wc_t = []
    for grp, base in ((0, 0), (1, 2048)):
        for r0 in range(0, 1024, 128):
            t = sb.alloc(3, F32, f"l1wc_{grp}_{r0}")
            P.dma(SP, t, C.dram["L1_attn_w_conv"][base + r0:base + r0 + 128,
                                                  0, :], chan="w1")
            wc_t.append(t)
    bc = fm_bias_col(C, "L1_attn_b_conv", "w1")
    xnw = [sb.alloc(514, BF16, f"l1xn_{i}") for i in range(8)]
    ub = sb.alloc(514, F32, "l1u")
    uc = sb.alloc(SEG, BF16, "l1uc")
    s0seg = [sb.alloc(SEG, BF16, f"l1s0_{i}") for i in range(8)]
    for s in range(NSEG):
        t0 = s * SEG
        lo = max(0, t0 - 2)
        pad = (t0 - 2) - lo
        for i in range(8):
            if s == 0:
                P.memset(xnw[i][:, 0:2], 0.0, tag="l1z")
                P.dma(SP, xnw[i][:, 2:514],
                      xn_hbm[i * 128:(i + 1) * 128, 0:SEG],
                      chan="x" + str(i % 2))
            else:
                P.dma(SP, xnw[i][:, 0:514],
                      xn_hbm[i * 128:(i + 1) * 128, t0 - 2:t0 + SEG],
                      chan="x" + str(i % 2))
        for grp, wt, coff in ((0, win0, 0), (1, win2, 16)):
            for jc in range(8):
                gi = grp * 8 + jc
                pb = ps.acquire()
                for ci in range(8):
                    P.mm(ps.ap(pb)[:, 0:2], wt[ci][:, jc * 128:(jc + 1) * 128],
                         xnw[ci][:, 0:2], start=(ci == 0), stop=(ci == 7),
                         tag="l1ue")
                pb2 = ps.acquire()
                for ci in range(8):
                    P.mm(ps.ap(pb2)[:, 0:SEG],
                         wt[ci][:, jc * 128:(jc + 1) * 128],
                         xnw[ci][:, 2:514], start=(ci == 0), stop=(ci == 7),
                         tag="l1um")
                bcol = bin0[coff + jc]
                P.ts(ub[:, 0:2], ps.ap(pb)[:, 0:2], bcol, ALU.add, tag="l1ub")
                ps.release(pb)
                P.ts(ub[:, 2:514], ps.ap(pb2)[:, 0:SEG], bcol, ALU.add,
                     tag="l1ub2")
                ps.release(pb2)
                wct = wc_t[grp * 8 + jc]
                P.ts2(uc, ub[:, 2:514], wct[:, 2:3], bc[coff + jc],
                      ALU.mult, ALU.add, tag="l1c3a")
                P.stt(uc, ub[:, 1:513], wct[:, 1:2], uc, ALU.mult, ALU.add,
                      tag="l1c3b")
                P.stt(uc, ub[:, 0:512], wct[:, 0:1], uc, ALU.mult, ALU.add,
                      tag="l1c3c")
                if grp == 0:
                    P.copy(s0seg[jc], uc, eng=POOL, tag="l1s0c")
                    P.dma(SP, s0_hbm[jc * 128:(jc + 1) * 128, t0:t0 + SEG],
                          s0seg[jc], chan="y0", tag="l1s0w")
                else:
                    P.tt(uc, uc, s0seg[jc], ALU.mult, tag="l1vp")
                    P.dma(SP, vp_hbm[jc * 128:(jc + 1) * 128, t0:t0 + SEG],
                          uc, chan="y1", tag="l1vpw")
    sb.release()

    sb.mark()
    tstage = sb.alloc(SEG, BF16, "l1tstage")
    tblocks = {}
    avals = list(range(-384, 1025, 128))
    for a in avals:
        tb = sb.alloc(SEG, BF16, f"l1T_{a}")
        tblocks[a] = tb
        hv = _capmod.AP(hpad.tensor, offset=512 + a - 127,
                        ap=[[1, 128], [1, 512]])
        P.dma(POOL, tstage, hv, chan="tb", tag="l1trev")
        pb = ps.acquire()
        P.mm(ps.ap(pb)[:, 0:SEG], rrev, tstage, tag="l1rev")
        P.copy(tb, ps.ap(pb)[:, 0:SEG], eng=ACT, tag="l1tb")
        ps.release(pb)

    wout = C.load_w("L1_attn_w_out", "w0")
    bout = fm_bias_col(C, "L1_attn_b_out", "w1")
    fb = fm_bias_col(C, "L1_attn_filt_bias", "w1")
    vtm = [sb.alloc(D, BF16, f"l1vt_{i}") for i in range(12)]
    vpw = [sb.alloc(SEG, BF16, f"l1vw_{i}") for i in range(8)]
    s0w = [sb.alloc(SEG, BF16, f"l1sw_{i}") for i in range(8)]
    zw = [sb.alloc(SEG, BF16, f"l1zw_{i}") for i in range(8)]
    obs = [sb.alloc(SEG, F32, f"l1ob{i}") for i in range(2)]
    for lw in range(NSEG):
        t0 = lw * SEG
        for i in range(8):
            P.dma(SP, vpw[i], vp_hbm[i * 128:(i + 1) * 128, t0:t0 + SEG],
                  chan="x" + str(i % 2), tag="l1ldvp")
            P.dma(SP, s0w[i], s0_hbm[i * 128:(i + 1) * 128, t0:t0 + SEG],
                  chan="y" + str(i % 2), tag="l1lds0")
        for j in range(4):
            mt = lw * 4 + j
            dst = vtm[mt % 12]
            for ci in range(8):
                pb = ps.acquire()
                pv = ps.ap(pb).bitcast(BF16)
                P.transpose(pv[:, 0:128], vpw[ci][:, j * 128:(j + 1) * 128],
                            idb, tag="l1vtr")
                P.copy(dst[:, ci * 128:(ci + 1) * 128], pv[:, 0:128],
                       eng=ACT, tag="l1vte")
                ps.release(pb)
        mts = [m for m in range(max(0, lw * 4 - 8), lw * 4 + 4)]
        for cc in range(8):
            pb = ps.acquire()
            for mi, mt in enumerate(mts):
                a = SEG * lw - 128 * mt
                P.mm(ps.ap(pb)[:, 0:SEG],
                     vtm[mt % 12][:, cc * 128:(cc + 1) * 128], tblocks[a],
                     start=(mi == 0), stop=(mi == len(mts) - 1), tag="l1cv")
            P.stt(zw[cc], vpw[cc], fb[cc], ps.ap(pb)[:, 0:SEG], ALU.mult,
                  ALU.add, tag="l1fb")
            ps.release(pb)
            P.tt(zw[cc], zw[cc], s0w[cc], ALU.mult, eng=POOL, tag="l1zs0")
        for jc in range(8):
            pb = ps.acquire()
            for ci in range(8):
                P.mm(ps.ap(pb)[:, 0:SEG], wout[ci][:, jc * 128:(jc + 1) * 128],
                     zw[ci], start=(ci == 0), stop=(ci == 7), tag="l1out")
            ob = obs[jc % 2]
            P.ts(ob, ps.ap(pb)[:, 0:SEG], bout[jc], ALU.add, tag="l1bo")
            ps.release(pb)
            P.dma(POOL, acc[jc * 128:(jc + 1) * 128, t0:t0 + SEG], ob,
                  chan="acc" + str(jc % 2), accum_op=ALU.add, tag="l1acc")
    sb.release()
    sb.release()



def build_L0_attn(C, xn_hbm, acc):
    P, sb, ps, nc = C.P, C.sb, C.ps, C.nc
    sb.mark()
    y1_hbm = nc.dram_tensor("l0_y1", [D, T], BF16, kind="Internal").ap()
    y2_hbm = nc.dram_tensor("l0_y2", [D, T], BF16, kind="Internal").ap()
    rope = RopeGen(C, "l0", W=640)
    rrot = build_rrot(C)
    maskA = build_band_mask(C)
    g1 = fm_bias_col(C, "L0_attn_ln1_g", "w1")
    b1 = fm_bias_col(C, "L0_attn_ln1_b", "w1")
    g2 = fm_bias_col(C, "L0_attn_ln2_g", "w1")
    b2 = fm_bias_col(C, "L0_attn_ln2_b", "w1")

    sb.mark()
    xwA = [sb.alloc(SEG, BF16, f"l0xa_{i}") for i in range(8)]
    ywA = [sb.alloc(SEG, BF16, f"l0ya_{i}") for i in range(8)]
    for s in range(NSEG):
        for i in range(8):
            P.dma(SP, xwA[i], xn_hbm[i * 128:(i + 1) * 128,
                                     s * SEG:(s + 1) * SEG],
                  chan="x" + str(i % 2))
        ln_window(C, xwA, SEG, ywA, g1, b1, tag="l0ln1")
        for i in range(8):
            P.dma(SP, y1_hbm[i * 128:(i + 1) * 128, s * SEG:(s + 1) * SEG],
                  ywA[i], chan="y" + str(i % 2), tag="l0y1w")
    sb.release()

    import concourse.ap as _capmod
    wperm = []
    sb.mark()
    wnat = [sb.alloc(1024, BF16, f"l0wn_{i}") for i in range(8)]
    for wname in ("L0_attn_wq", "L0_attn_wk"):
        tiles = []
        for r0 in range(0, D, 128):
            t = sb.alloc(1024, BF16, f"l0wp_{wname}_{r0}")
            tiles.append(t)
        wperm.append(tiles)
    for wi, wname in enumerate(("L0_attn_wq", "L0_attn_wk")):
        for r0 in range(0, D, 128):
            P.dma(POOL, wnat[r0 // 128], C.dram[wname][r0:r0 + 128, :],
                  chan="w0", tag="l0wload")
        for r0 in range(0, D, 128):
            src_v = _capmod.AP(wnat[r0 // 128].tensor, offset=0,
                               ap=[[1024, 128], [64, 16], [1, 2], [2, 32]])
            P.copy(wperm[wi][r0 // 128], src_v, tag="l0perm")
    wv = wnat
    for r0 in range(0, D, 128):
        P.dma(POOL, wv[r0 // 128], C.dram["L0_attn_wv"][r0:r0 + 128, :],
              chan="w0", tag="l0wv")
    wo = [sb.alloc(1024, BF16, f"l0wo_{i}") for i in range(8)]
    for r0 in range(0, D, 128):
        P.dma(POOL, wo[r0 // 128], C.dram["L0_attn_wo"][r0:r0 + 128, :],
              chan="w0", tag="l0wo")
    wqp, wkp = wperm

    y1w = [sb.alloc(640, BF16, f"l0y1w_{i}") for i in range(8)]
    Kd = [sb.alloc(640, BF16, f"l0K_{i}") for i in range(8)]
    Qd = [sb.alloc(SEG, BF16, f"l0Q_{i}") for i in range(8)]
    V_tm = [sb.alloc(D, BF16, f"l0V_{i}") for i in range(5)]
    raw = sb.alloc(640, BF16, "l0raw")
    rot = sb.alloc(640, BF16, "l0rot")
    t1 = sb.alloc(640, BF16, "l0t1")
    hW = [sb.alloc(SEG, BF16, f"l0h_{i}") for i in range(8)]
    y2W = [sb.alloc(SEG, BF16, f"l0y2_{i}") for i in range(8)]
    xnw = [sb.alloc(SEG, BF16, f"l0xn_{i}") for i in range(8)]
    rsum = sb.alloc(SEG, F32, "l0rsum")
    recB = sb.alloc(SEG, BF16, "l0recB")
    Pkm = sb.alloc(SEG, BF16, "l0P")
    hstg = sb.alloc(SEG, F32, "l0hstg")

    def proj_rope_piece(dst_tiles, wT, c0, wlen, coff):
        for jc in range(8):
            pb = ps.acquire()
            for ci in range(8):
                P.mm(ps.ap(pb)[:, 0:wlen],
                     wT[ci][:, jc * 128:(jc + 1) * 128],
                     y1w[ci][:, c0:c0 + wlen],
                     start=(ci == 0), stop=(ci == 7), tag="l0qk")
            P.copy(raw[:, 0:wlen], ps.ap(pb)[:, 0:wlen], eng=ACT,
                   tag="l0raw")
            ps.release(pb)
            pb = ps.acquire()
            P.mm(ps.ap(pb)[:, 0:wlen], rrot, raw[:, 0:wlen], tag="l0rot")
            P.tt(t1[:, 0:wlen], raw[:, 0:wlen],
                 rope.cos[:, coff:coff + wlen], ALU.mult, eng=POOL,
                 tag="l0rc")
            P.tt(rot[:, 0:wlen], ps.ap(pb)[:, 0:wlen],
                 rope.sin[:, coff:coff + wlen], ALU.mult, tag="l0rs")
            ps.release(pb)
            P.tt(dst_tiles[jc][:, c0:c0 + wlen], t1[:, 0:wlen],
                 rot[:, 0:wlen], ALU.add, eng=POOL, tag="l0radd")

    for w in range(NSEG):
        ws = w * SEG
        kb = max(0, ws - 128)
        kvlen = ws + SEG - kb
        qoff = ws - kb
        for i in range(8):
            P.dma(SP, y1w[i][:, 0:kvlen],
                  y1_hbm[i * 128:(i + 1) * 128, kb:kb + kvlen],
                  chan="x" + str(i % 2), tag="l0ldy1")
            P.dma(SP, xnw[i], xn_hbm[i * 128:(i + 1) * 128, ws:ws + SEG],
                  chan="y" + str(i % 2), tag="l0ldxn")
        rope.gen(kb)
        for (c0, wlen) in ((0, min(kvlen, SEG)),) + (
                ((SEG, kvlen - SEG),) if kvlen > SEG else ()):
            proj_rope_piece(Kd, wkp, c0, wlen, c0)
        proj_rope_piece(Qd, wqp, 0, SEG, qoff)
        nkt = (kvlen + 127) // 128
        for tt in range(nkt):
            for nh in range(2):
                pb = ps.acquire()
                for ci in range(8):
                    P.mm(ps.ap(pb)[:, 0:SEG],
                         y1w[ci][:, tt * 128:(tt + 1) * 128],
                         wv[ci][:, nh * SEG:(nh + 1) * SEG],
                         start=(ci == 0), stop=(ci == 7), tag="l0v")
                P.copy(V_tm[tt][:, nh * SEG:(nh + 1) * SEG],
                       ps.ap(pb)[:, 0:SEG], eng=ACT, tag="l0vc")
                ps.release(pb)
        for h in range(16):
            hc, hr = h // 2, (h % 2) * 64
            pb_o = ps.acquire()
            pb_r = ps.acquire()
            for kt in range(nkt):
                kglob = kb + kt * 128
                q0 = max(0, kglob - ws)
                q1 = min(SEG, kglob + 256 - ws)
                wq = q1 - q0
                a = (ws + q0) - kglob
                pb_s = ps.acquire()
                P.mm(ps.ap(pb_s)[:, 0:wq],
                     Kd[hc][hr:hr + 64, kt * 128:(kt + 1) * 128],
                     Qd[hc][hr:hr + 64, q0:q1], tag="l0sc")
                P.act(Pkm[:, 0:wq], ps.ap(pb_s)[:, 0:wq], AF.Exp,
                      scale=0.125, tag="l0exp")
                ps.release(pb_s)
                P.tt(Pkm[:, 0:wq], Pkm[:, 0:wq], maskA[:, a:a + wq],
                     ALU.mult, eng=POOL, tag="l0mask")
                P.mm(ps.ap(pb_r)[0:1, q0:q1], C.ones_col_bf16(),
                     Pkm[:, 0:wq], start=(kt == 0), stop=(kt == nkt - 1),
                     tag="l0rs")
                P.mm(ps.ap(pb_o)[hr:hr + 64, q0:q1],
                     V_tm[kt][:, h * 64:(h + 1) * 64], Pkm[:, 0:wq],
                     start=(kt == 0), stop=(kt == nkt - 1), tag="l0pv",
                     tile_position=(0, hr))
            P.recip(rsum[0:1, :], ps.ap(pb_r)[0:1, 0:SEG], tag="l0rec")
            ps.release(pb_r)
            pb_b = ps.acquire()
            rb = C.bcast_row(rsum[0:1, :], pb_b, SEG)
            P.copy(recB, rb, tag="l0recB")
            ps.release(pb_b)
            P.tt(Qd[hc][hr:hr + 64, :], ps.ap(pb_o)[hr:hr + 64, 0:SEG],
                 recB[hr:hr + 64, :], ALU.mult, tag="l0norm")
            ps.release(pb_o)
        for jc in range(8):
            pb = ps.acquire()
            for ci in range(8):
                P.mm(ps.ap(pb)[:, 0:SEG], wo[ci][:, jc * 128:(jc + 1) * 128],
                     Qd[ci], start=(ci == 0), stop=(ci == 7), tag="l0om")
            P.tt(hW[jc], ps.ap(pb)[:, 0:SEG], xnw[jc], ALU.add, tag="l0hadd")
            ps.release(pb)
            P.copy(hstg, hW[jc], eng=ACT, tag="l0hstg")
            P.dma(POOL, acc[jc * 128:(jc + 1) * 128, ws:ws + SEG], hstg,
                  chan="acc0", accum_op=ALU.add, tag="l0hacc")
        ln_window(C, hW, SEG, y2W, g2, b2, tag="l0ln2")
        for i in range(8):
            P.dma(SP, y2_hbm[i * 128:(i + 1) * 128, ws:ws + SEG], y2W[i],
                  chan="y" + str(i % 2), tag="l0y2w")
    sb.release()

    sb.mark()
    NHC = 2816 // 128
    w1 = C.load_w("L0_attn_w1", "w0")
    w3 = C.load_w("L0_attn_w3", "w0b")
    w2 = C.load_w("L0_attn_w2", "w0c")
    xw2 = [[sb.alloc(SEG, BF16, f"l0fx2_{j}_{i}") for i in range(8)]
           for j in range(2)]
    hT = [sb.alloc(SEG, BF16, f"l0ft_{i}") for i in range(NHC)]
    sil = sb.alloc(SEG, BF16, "l0fsil")
    obs = [sb.alloc(SEG, F32, f"l0fob{i}") for i in range(2)]
    for s in range(NSEG):
        xw = xw2[s % 2]
        for i in range(8):
            P.dma(SP, xw[i], y2_hbm[i * 128:(i + 1) * 128,
                                    s * SEG:(s + 1) * SEG],
                  chan="x" + str(i % 2))
        for hc in range(NHC):
            pa = ps.acquire()
            for ci in range(8):
                P.mm(ps.ap(pa)[:, 0:SEG], w1[ci][:, hc * 128:(hc + 1) * 128],
                     xw[ci], start=(ci == 0), stop=(ci == 7), tag="l0f1")
            pbk = ps.acquire()
            for ci in range(8):
                P.mm(ps.ap(pbk)[:, 0:SEG], w3[ci][:, hc * 128:(hc + 1) * 128],
                     xw[ci], start=(ci == 0), stop=(ci == 7), tag="l0f3")
            P.act(sil, ps.ap(pa)[:, 0:SEG], AF.Silu, tag="l0fsil")
            ps.release(pa)
            P.tt(hT[hc], sil, ps.ap(pbk)[:, 0:SEG], ALU.mult, tag="l0fh")
            ps.release(pbk)
        for dc in range(8):
            pb = ps.acquire()
            for hc in range(NHC):
                P.mm(ps.ap(pb)[:, 0:SEG], w2[hc][:, dc * 128:(dc + 1) * 128],
                     hT[hc], start=(hc == 0), stop=(hc == NHC - 1),
                     tag="l0f2")
            ob = obs[dc % 2]
            P.copy(ob, ps.ap(pb)[:, 0:SEG])
            ps.release(pb)
            P.dma(POOL, acc[dc * 128:(dc + 1) * 128,
                            s * SEG:(s + 1) * SEG], ob,
                  chan="acc" + str(dc % 2), accum_op=ALU.add, tag="l0facc")
    sb.release()
    sb.release()



def build_L3_attn(C, xn_hbm, acc):
    P, sb, ps, nc = C.P, C.sb, C.ps, C.nc
    sb.mark()
    rope = RopeGen(C, "l3")
    cos, sin = rope.cos, rope.sin
    rrot = build_rrot(C)
    maskC = build_causal_mask(C)
    idb = C.identity_bf16()
    bq = fm_bias_col(C, "L3_attn_bq", "w1")
    bk = fm_bias_col(C, "L3_attn_bk", "w1")
    bo = fm_bias_col(C, "L3_attn_bo", "w1")
    _vbias_b(C)

    Kd = [sb.alloc(T, BF16, f"l3K_{i}") for i in range(8)]
    V_tm = [sb.alloc(D, BF16, f"l3V_{i}") for i in range(32)]
    xw = [sb.alloc(SEG, BF16, f"l3xw_{i}") for i in range(8)]
    raw = rope.r.bitcast(BF16)[:, 0:SEG]
    rot = sb.alloc(SEG, BF16, "l3rot")
    t1 = sb.alloc(SEG, BF16, "l3t1")
    sb.mark()
    wkT = C.load_wT("L3_attn_Wk", "w0b")
    wvT = C.load_wT("L3_attn_Wv", "w0c")

    def project_rope(dst_tiles, wT, bcolv, dst0, pos0):
        rope.gen(pos0)
        sl = slice(dst0, dst0 + SEG)
        psl = slice(0, SEG)
        for jc in range(8):
            pb = ps.acquire()
            for ci in range(8):
                P.mm(ps.ap(pb)[:, 0:SEG], wT[ci][:, jc * 128:(jc + 1) * 128],
                     xw[ci], start=(ci == 0), stop=(ci == 7), tag="l3qk")
            P.ts(raw, ps.ap(pb)[:, 0:SEG], bcolv[jc], ALU.add, tag="l3b")
            ps.release(pb)
            pb = ps.acquire()
            P.mm(ps.ap(pb)[:, 0:SEG], rrot, raw, tag="l3rot")
            P.tt(t1, raw, cos[:, psl], ALU.mult, eng=POOL, tag="l3rc")
            P.tt(rot, ps.ap(pb)[:, 0:SEG], sin[:, psl], ALU.mult, tag="l3rs")
            ps.release(pb)
            P.tt(dst_tiles[jc][:, sl], t1, rot, ALU.add, eng=POOL,
                 tag="l3radd")

    for s in range(NSEG):
        for i in range(8):
            P.dma(SP, xw[i], xn_hbm[i * 128:(i + 1) * 128,
                                    s * SEG:(s + 1) * SEG],
                  chan="x" + str(i % 2))
        project_rope(Kd, wkT, bk, s * SEG, 1 + s * SEG)
        for tt in range(4):
            t0 = s * SEG + tt * 128
            for nh in range(2):
                pb = ps.acquire()
                for ci in range(8):
                    P.mm(ps.ap(pb)[:, 0:SEG],
                         xw[ci][:, tt * 128:(tt + 1) * 128],
                         wvT[ci][:, nh * SEG:(nh + 1) * SEG],
                         start=(ci == 0), stop=(ci == 7), tag="l3v")
                P.copy(raw, ps.ap(pb)[:, 0:SEG], tag="l3vc")
                ps.release(pb)
                P.tt(V_tm[t0 // 128][:, nh * SEG:(nh + 1) * SEG], raw,
                     _vbias_b(C)[:, nh * SEG:(nh + 1) * SEG], ALU.add,
                     eng=POOL, tag="l3vb")

    sb.release()
    wqT = C.load_wT("L3_attn_Wq", "w0")
    woT = C.load_wT("L3_attn_Wo", "w0")
    Qd = [sb.alloc(SEG, BF16, f"l3Q_{i}") for i in range(8)]
    Pkm = t1
    recB = rot
    rsum = rope.qf
    Ow = [sb.alloc(SEG, BF16, f"l3O_{i}") for i in range(8)]
    obf = [rsum]
    for w in range(NSEG):
        ws = w * SEG
        for i in range(8):
            P.dma(SP, xw[i], xn_hbm[i * 128:(i + 1) * 128, ws:ws + SEG],
                  chan="x" + str(i % 2))
        project_rope(Qd, wqT, bq, 0, 1 + ws)
        for h in range(16):
            hc, hr = h // 2, (h % 2) * 64
            pb_o = ps.acquire()
            pb_r = ps.acquire()
            nkt = (ws + SEG) // 128
            for kt in range(nkt):
                pb_s = ps.acquire()
                P.mm(ps.ap(pb_s)[:, 0:SEG],
                     Kd[hc][hr:hr + 64, kt * 128:(kt + 1) * 128],
                     Qd[hc][hr:hr + 64, 0:SEG], tag="l3sc")
                P.act(Pkm, ps.ap(pb_s)[:, 0:SEG], AF.Exp, scale=0.125,
                      tag="l3exp")
                ps.release(pb_s)
                a = kt * 128 - ws
                if a >= 0:
                    P.tt(Pkm, Pkm, maskC[:, 384 - a:896 - a], ALU.mult,
                         eng=POOL, tag="l3mask")
                P.mm(ps.ap(pb_r)[0:1, 0:SEG], C.ones_col_bf16(), Pkm,
                     start=(kt == 0), stop=(kt == nkt - 1), tag="l3rs")
                P.mm(ps.ap(pb_o)[hr:hr + 64, 0:SEG],
                     V_tm[kt][:, h * 64:(h + 1) * 64], Pkm,
                     start=(kt == 0), stop=(kt == nkt - 1), tag="l3pv",
                     tile_position=(0, hr))
            P.recip(rsum[0:1, :], ps.ap(pb_r)[0:1, 0:SEG], tag="l3rec")
            ps.release(pb_r)
            pb_b = ps.acquire()
            rb = C.bcast_row(rsum[0:1, :], pb_b, SEG)
            P.copy(recB, rb, tag="l3recB")
            ps.release(pb_b)
            P.tt(Ow[hc][hr:hr + 64, :], ps.ap(pb_o)[hr:hr + 64, 0:SEG],
                 recB[hr:hr + 64, :], ALU.mult, tag="l3norm")
            ps.release(pb_o)
        for jc in range(8):
            pb = ps.acquire()
            for ci in range(8):
                P.mm(ps.ap(pb)[:, 0:SEG], woT[ci][:, jc * 128:(jc + 1) * 128],
                     Ow[ci], start=(ci == 0), stop=(ci == 7), tag="l3out")
            ob = obf[0]
            P.ts(ob, ps.ap(pb)[:, 0:SEG], bo[jc], ALU.add, tag="l3bo")
            ps.release(pb)
            P.dma(POOL, acc[jc * 128:(jc + 1) * 128, ws:ws + SEG], ob,
                  chan="acc0", accum_op=ALU.add, tag="l3acc")
    sb.release()


def _vbias_b(C):
    if "l3bvB" not in C.const:
        P, sb, ps = C.P, C.sb, C.ps
        t = sb.alloc(1024, BF16, "c_bvB")
        sb.mark()
        row = sb.alloc(1024, BF16, "c_bvrow")
        rowf = sb.alloc(1024, F32, "c_bvrowf")
        P.dma(SP, rowf[0:1, :], C.dram["L3_attn_bv"][None, :], chan="w1")
        P.copy(row[0:1, :], rowf[0:1, :])
        for half in range(2):
            pb = ps.acquire()
            bB = C.bcast_row(row[0:1, half * SEG:(half + 1) * SEG], pb, SEG)
            P.copy(t[:, half * SEG:(half + 1) * SEG], bB)
            ps.release(pb)
        sb.release()
        C.const["l3bvB"] = t
    return C.const["l3bvB"]



def build_L0_moe_sparse(C, xn_hbm, acc):
    from concourse.bass_core_rust import IndirectOffsetOnAxis as IOA
    P, sb, ps, nc = C.P, C.sb, C.ps, C.nc
    I32 = mybir.dt.int32
    U32 = mybir.dt.uint32
    sb.mark()
    NE = 7
    HID = 2816
    NHC = HID // 128
    CAP = 1024
    NB = CAP // 128
    EXT = 1032
    idb = C.identity_bf16()
    idf = C.identity_f32()
    for e in range(NE):
        C.stage_wT("L0_ffn_ew1", idx=e)
        C.stage_wT("L0_ffn_ew3", idx=e)
        C.stage_wT("L0_ffn_ew2", idx=e)
    C.stage_wT("L0_ffn_sw1")
    C.stage_wT("L0_ffn_sw3")
    C.stage_wT("L0_ffn_sw2")
    XROWS = T + NE * CAP + 8
    IDXN = NE * CAP + T + 8
    wrows = nc.dram_tensor("moe_wrows", [NE + 1, T], BF16,
                           kind="Internal").ap()
    xn_tm = nc.dram_tensor("moe_xn_tm", [XROWS, EXT], BF16,
                           kind="Internal").ap()
    idxmap = nc.dram_tensor("moe_idxmap", [IDXN], F32,
                            kind="Internal").ap()
    moe_out = nc.dram_tensor("moe_out", [XROWS, D], BF16,
                             kind="Internal").ap()

    sb.mark()
    srow = sb.alloc(SEG, F32, "moesent")
    for off in range(0, IDXN, SEG):
        w = min(SEG, IDXN - off)
        b0 = T + (off % (NE * CAP))

        def iop(e, out=srow[0:1, 0:w], b=b0, w=w):
            return e.iota(out, [[1, w]], base=b, channel_multiplier=0,
                          allow_small_or_imprecise_dtypes=True)
        P.custom(POOL, iop, writes=[srow[0:1, 0:w]], tag="moesent")
        P.dma(SP, idxmap[None, off:off + w], srow[0:1, 0:w], chan="wr",
              tag="moesentw")
    sb.release()

    rwT = C.load_wT_small("L0_ffn_router_w", "w1")
    rb = sb.alloc(NE, F32, "rbias")
    P.dma(SP, rb[0:1, :], C.dram["L0_ffn_routing_bias"][None, :], chan="w1")
    xw2 = [[sb.alloc(SEG, BF16, f"moexw_{j}_{i}") for i in range(8)]
           for j in range(2)]
    sb.mark()
    probs = [sb.alloc(SEG, F32, f"moepr_{e}") for e in range(NE)]
    pmax = sb.alloc(SEG, F32, "moepmax")
    pclip = sb.alloc(SEG, F32, "moepclip")
    rtw = sb.alloc(SEG, F32, "moertw")
    wrow = sb.alloc(SEG, BF16, "moewrow")
    mrow = sb.alloc(SEG, F32, "moemrow")
    rkrow = sb.alloc(SEG, F32, "moerk")
    t1row = sb.alloc(SEG, F32, "moet1")
    onesr = sb.alloc(SEG, F32, "moeones")
    t2row = sb.alloc(SEG, F32, "moet2")
    tokrow = sb.alloc(SEG, F32, "moetokr")
    dacc = sb.alloc(SEG, F32, "moedacc")
    base = sb.alloc(NE, F32, "moebase")
    dcol = sb.alloc(4, F32, "moedcol")
    dcoli = sb.alloc(4, I32, "moedcoli")
    tkcol = sb.alloc(1, F32, "moetk")
    P.memset(base[0:1, :], 0.0)
    P.memset(onesr[0:1, :], 1.0)
    for s in range(NSEG):
        xw = xw2[s % 2]
        for i in range(8):
            P.dma(SP, xw[i], xn_hbm[i * 128:(i + 1) * 128,
                                    s * SEG:(s + 1) * SEG],
                  chan="x" + str(i % 2))
        for e in range(NE):
            pb = ps.acquire()
            for ci in range(8):
                P.mm(ps.ap(pb)[0:1, 0:SEG], rwT[ci][:, e:e + 1], xw[ci],
                     start=(ci == 0), stop=(ci == 7), tag="moegate")
            pr = probs[e][0:1, :]
            P.ts(pr, ps.ap(pb)[0:1, 0:SEG], rb[0:1, e:e + 1], ALU.add)
            ps.release(pb)
            P.ts(pr, pr, 50.0, ALU.min)
            P.ts(pr, pr, -50.0, ALU.max)
            P.act(pr, pr, AF.Sigmoid, tag="moesig")
        pm = pmax[0:1, :]
        P.tt(pm, probs[0][0:1, :], probs[1][0:1, :], ALU.max)
        for e in range(2, NE):
            P.tt(pm, pm, probs[e][0:1, :], ALU.max)
        pc = pclip[0:1, :]
        P.ts(pc, pm, 1e-8, ALU.max)
        P.ts(pc, pc, 1.0 - 1e-8, ALU.min)
        rt = rtw[0:1, :]
        P.ts(rt, pc, 0.5 + 1e-8, ALU.add)
        P.ts(rt, rt, 0.5, ALU.max)
        P.ts(rt, rt, 2.0, ALU.min)
        P.recip(rt, rt)
        P.memset(dacc[0:1, :], 0.0, tag="moedz")

        def toki(e, out=tokrow[0:1, :], b0=s * SEG):
            return e.iota(out, [[1, SEG]], base=b0, channel_multiplier=0,
                          allow_small_or_imprecise_dtypes=True)
        P.custom(POOL, toki, writes=[tokrow[0:1, :]], tag="moetoki")
        for e in range(NE):
            m = mrow[0:1, :]
            P.tt(m, probs[e][0:1, :], pm, ALU.is_ge, tag="moemask")
            P.tt(wrow[0:1, :], m, pc, ALU.mult)
            P.tt(wrow[0:1, :], wrow[0:1, :], rt, ALU.mult)
            P.dma(SP, wrows[e:e + 1, s * SEG:(s + 1) * SEG], wrow[0:1, :],
                  chan="wr", tag="moewr")
            rk = rkrow[0:1, :]

            def scn(eng, out=rk, m=m, o1=onesr[0:1, :]):
                return eng.tensor_tensor_scan(out, o1, m, 0.0, ALU.mult,
                                              ALU.add)
            P.custom(DVE, scn, reads=[m, onesr], writes=[rk], tag="moescan")
            P.ts(rk, rk, base[0:1, e:e + 1], ALU.add, tag="moerkb")
            P.copy(base[0:1, e:e + 1], rk[0:1, SEG - 1:SEG], tag="moebup")
            t1 = t1row[0:1, :]
            P.ts(t1, rk, float(e * CAP - 1), ALU.add)
            P.ts(rk, rk, CAP + 0.5, ALU.is_gt)
            P.tt(t2row[0:1, :], tokrow[0:1, :], t1, ALU.subtract)
            P.ts(t2row[0:1, :], t2row[0:1, :], float(NE * CAP), ALU.add)
            P.tt(t2row[0:1, :], t2row[0:1, :], rk, ALU.mult)
            P.tt(t1, t1, t2row[0:1, :], ALU.add)
            P.tt(t1, t1, m, ALU.mult, tag="moedm")
            P.tt(dacc[0:1, :], dacc[0:1, :], t1, ALU.add, tag="moeda")
        P.ts(wrow[0:1, :], rt, 0.5, ALU.mult)
        P.dma(SP, wrows[NE:NE + 1, s * SEG:(s + 1) * SEG], wrow[0:1, :],
              chan="wr", tag="moewrs")
        P.ts(dacc[0:1, :], dacc[0:1, :], float(NE * CAP + T), ALU.min,
             tag="moeclamp")
        for b in range(4):
            pb = ps.acquire()
            P.transpose(ps.ap(pb)[:, 0:1],
                        dacc[0:1, b * 128:(b + 1) * 128], idf[0:1, 0:1],
                        tag="moedtr")
            P.copy(dcol[:, b:b + 1], ps.ap(pb)[:, 0:1])
            ps.release(pb)
        P.copy(dcoli, dcol, tag="moedci")
        if globals().get("_MOE_DBG") is not None and s == 0:
            P.dma(SP, globals()["_MOE_DBG"][0:1, 0:SEG], dacc[0:1, :],
                  chan="dbg")
            P.dma(SP, globals()["_MOE_DBG"][1:129, 0:4], dcol, chan="dbg")
        for b in range(4):
            def iop2(e, out=tkcol, b0=s * SEG + b * 128):
                return e.iota(out, [[0, 1]], base=b0, channel_multiplier=1,
                              allow_small_or_imprecise_dtypes=True)
            P.custom(POOL, iop2, writes=[tkcol], tag="moetk")

            im2d = idxmap.rearrange("(a b) -> a b", b=1)

            breg = C.bound_reg(IDXN - 1)

            def sc(e, idx=dcoli[:, b:b + 1].bitcast(U32), src=tkcol,
                   out=im2d, br=breg):
                return e.indirect_dma_start(out, IOA(idx, 0), src, None,
                                            bounds_check=br)
            P._add(POOL, sc, reads=[tkcol, dcoli[:, b:b + 1]],
                   writes=[idxmap], is_dma=True, chan="wr2", tag="moescat")
    sb.release()

    _MP2 = globals().get("_MOE_PHASES", 9)
    sb.mark()
    rowbuf = sb.alloc(EXT, BF16, "moerow")
    wrseg = sb.alloc(SEG, BF16, "moewrl")
    for s in (range(NSEG) if _MP2 >= 2 else []):
        xw = xw2[s % 2]
        for i in range(8):
            P.dma(SP, xw[i], xn_hbm[i * 128:(i + 1) * 128,
                                    s * SEG:(s + 1) * SEG],
                  chan="x" + str(i % 2))
        P.dma(SP, wrseg[0:NE, :], wrows[0:NE, s * SEG:(s + 1) * SEG],
              chan="wr2", tag="moewrl")
        for tt in range(4):
            for cc in range(8):
                pb = ps.acquire()
                pv = ps.ap(pb).bitcast(BF16)
                P.transpose(pv[:, 0:128],
                            xw[cc][:, tt * 128:(tt + 1) * 128], idb,
                            tag="moextr")
                P.copy(rowbuf[:, cc * 128:(cc + 1) * 128], pv[:, 0:128],
                       eng=ACT, tag="moextre")
                ps.release(pb)
            pb = ps.acquire()
            pv = ps.ap(pb).bitcast(BF16)
            P.transpose(pv[:, 0:NE],
                        wrseg[0:NE, tt * 128:(tt + 1) * 128], idb[0:NE, 0:NE],
                        tag="moewtr")
            P.copy(rowbuf[:, 1024:1024 + NE], pv[:, 0:NE], tag="moewtre")
            P.memset(rowbuf[:, 1024 + NE:EXT], 0.0, tag="moepadz")
            ps.release(pb)
            t0 = s * SEG + tt * 128
            P.dma(SP, xn_tm[t0:t0 + 128, :], rowbuf, chan="y" + str(tt % 2),
                  tag="moexnw")
    sb.release()

    sb.mark()
    zt = sb.alloc(D, BF16, "moezero")
    P.memset(zt, 0.0)
    for tt in range(32):
        P.dma(SP, moe_out[tt * 128:(tt + 1) * 128, :], zt,
              chan="y" + str(tt % 2), tag="moezw")
    sb.release()

    _MP = globals().get("_MOE_PHASES", 9)
    sb.mark()
    h = [sb.alloc(SEG, BF16, f"moeh_{i}") for i in range(NHC)]
    xg_tm = [sb.alloc(EXT, BF16, f"moexg_{j}") for j in range(2)]
    xg_fm = [sb.alloc(SEG, BF16, f"moexf_{i}") for i in range(8)]
    wEc = sb.alloc(NB, F32, "moewec")
    idxrow = sb.alloc(CAP // 2, F32, "moeidxr")
    idxf = sb.alloc(NB, F32, "moeidxf")
    idxi = sb.alloc(NB, I32, "moeidxi")
    ybuf = [sb.alloc(SEG, F32, "moeyb_0")]
    sil = ybuf[0].bitcast(BF16)[:, 0:SEG]
    y_tm = [sb.alloc(D, BF16, f"moeyt_{j}") for j in range(4)]
    segs = [(s0, min(SEG, CAP - s0)) for s0 in range(0, CAP, SEG)]
    for e in (range(NE) if _MP >= 3 else []):
        sb.mark()
        w1T = C.load_wT("L0_ffn_ew1", "w0", idx=e)
        w3T = C.load_wT("L0_ffn_ew3", "w0b", idx=e)
        w2T = C.load_wT("L0_ffn_ew2", "w0c", idx=e)
        for hf in range(2):
            o0 = e * CAP + hf * (CAP // 2)
            P.dma(SP, idxrow[0:1, :], idxmap[None, o0:o0 + CAP // 2],
                  chan="wr2", tag="moeidxl")
            for bb in range(NB // 2):
                b = hf * (NB // 2) + bb
                pb = ps.acquire()
                P.transpose(ps.ap(pb)[:, 0:1],
                            idxrow[0:1, bb * 128:(bb + 1) * 128],
                            idf[0:1, 0:1], tag="moeidxt")
                P.copy(idxf[:, b:b + 1], ps.ap(pb)[:, 0:1])
                ps.release(pb)
        P.copy(idxi, idxf, tag="moeidxc")
        breg2 = C.bound_reg(XROWS - 1)
        for si, (s0, sw) in enumerate(segs):
            for b in range(s0 // 128, (s0 + sw) // 128):
                xg = xg_tm[b % 2]
                P.memset(xg, 0.0, tag="moexgz")

                def ga(eng, out=xg, idx=idxi[:, b:b + 1].bitcast(U32),
                       br=breg2):
                    return eng.indirect_dma_start(out, None, xn_tm,
                                                  IOA(idx, 0),
                                                  bounds_check=br)
                P._add(POOL, ga, reads=[xn_tm, idxi[:, b:b + 1]],
                       writes=[xg], is_dma=True, chan="g" + str(b % 2),
                       tag="moega")
                P.copy(wEc[:, b:b + 1], xg[:, 1024 + e:1024 + e + 1],
                       tag="moewec")
                bl = b - s0 // 128
                for cc in range(8):
                    pb = ps.acquire()
                    pv = ps.ap(pb).bitcast(BF16)
                    P.transpose(pv[:, 0:128],
                                xg[:, cc * 128:(cc + 1) * 128], idb,
                                tag="moexgt")
                    P.copy(xg_fm[cc][:, bl * 128:(bl + 1) * 128],
                           pv[:, 0:128], eng=ACT, tag="moexgte")
                    ps.release(pb)
            for hc in range(NHC):
                pa = ps.acquire()
                for ci in range(8):
                    P.mm(ps.ap(pa)[:, 0:sw],
                         w1T[ci][:, hc * 128:(hc + 1) * 128],
                         xg_fm[ci][:, 0:sw],
                         start=(ci == 0), stop=(ci == 7), tag="moea")
                pbk = ps.acquire()
                for ci in range(8):
                    P.mm(ps.ap(pbk)[:, 0:sw],
                         w3T[ci][:, hc * 128:(hc + 1) * 128],
                         xg_fm[ci][:, 0:sw],
                         start=(ci == 0), stop=(ci == 7), tag="moeb")
                P.act(sil[:, 0:sw], ps.ap(pa)[:, 0:sw], AF.Silu,
                      tag="moesilu")
                ps.release(pa)
                P.tt(h[hc][:, 0:sw], sil[:, 0:sw], ps.ap(pbk)[:, 0:sw],
                     ALU.mult, tag="moeh")
                ps.release(pbk)
            for dc in range(8):
                pb = ps.acquire()
                for hc in range(NHC):
                    P.mm(ps.ap(pb)[:, 0:sw],
                         w2T[hc][:, dc * 128:(dc + 1) * 128],
                         h[hc][:, 0:sw],
                         start=(hc == 0), stop=(hc == NHC - 1), tag="moey")
                yb = ybuf[0]
                P.copy(yb[:, 0:sw], ps.ap(pb)[:, 0:sw], tag="moeye")
                ps.release(pb)
                for bl in range(sw // 128):
                    pb2 = ps.acquire()
                    P.transpose(ps.ap(pb2)[:, 0:128],
                                yb[:, bl * 128:(bl + 1) * 128], idf,
                                tag="moeytr")
                    P.ts(y_tm[bl][:, dc * 128:(dc + 1) * 128],
                         ps.ap(pb2)[:, 0:128],
                         wEc[:, s0 // 128 + bl:s0 // 128 + bl + 1],
                         ALU.mult, tag="moeyw")
                    ps.release(pb2)
            breg3 = C.bound_reg(XROWS - 1)
            for bl in range(sw // 128):
                b = s0 // 128 + bl

                def sc2(eng, src=y_tm[bl],
                        idx=idxi[:, b:b + 1].bitcast(U32), br=breg3):
                    return eng.indirect_dma_start(moe_out, IOA(idx, 0), src,
                                                  None, bounds_check=br)
                P._add(POOL, sc2, reads=[y_tm[bl], idxi[:, b:b + 1]],
                       writes=[moe_out], is_dma=True, chan="g" + str(b % 2),
                       tag="moeysc")
        sb.release()
    sb.release()

    sb.mark()
    h2 = [sb.alloc(SEG, BF16, f"moesh_{i}") for i in range(NHC)]
    sil2 = sb.alloc(SEG, BF16, "moesil2")
    wEB = sb.alloc(SEG, BF16, "moewEB")
    wrseg2 = sb.alloc(SEG, BF16, "moewrseg")
    obs = [sb.alloc(SEG, F32, f"moeob{i}") for i in range(4)]
    w1T = C.load_wT("L0_ffn_sw1", "w0")
    w3T = C.load_wT("L0_ffn_sw3", "w0b")
    w2T = C.load_wT("L0_ffn_sw2", "w0c")
    for s in range(NSEG):
        xw = xw2[s % 2]
        for i in range(8):
            P.dma(SP, xw[i], xn_hbm[i * 128:(i + 1) * 128,
                                    s * SEG:(s + 1) * SEG],
                  chan="x" + str(i % 2))
        P.dma(SP, wrseg2[0:1, :], wrows[NE:NE + 1, s * SEG:(s + 1) * SEG],
              chan="wr2")
        pb = ps.acquire()
        P.copy(wEB, C.bcast_row(wrseg2[0:1, :], pb, SEG), tag="moewEB")
        ps.release(pb)
        for hc in range(NHC):
            pa = ps.acquire()
            for ci in range(8):
                P.mm(ps.ap(pa)[:, 0:SEG],
                     w1T[ci][:, hc * 128:(hc + 1) * 128], xw[ci],
                     start=(ci == 0), stop=(ci == 7), tag="moesa")
            pbk = ps.acquire()
            for ci in range(8):
                P.mm(ps.ap(pbk)[:, 0:SEG],
                     w3T[ci][:, hc * 128:(hc + 1) * 128], xw[ci],
                     start=(ci == 0), stop=(ci == 7), tag="moesb")
            P.act(sil2, ps.ap(pa)[:, 0:SEG], AF.Silu, tag="moessilu")
            ps.release(pa)
            P.tt(h2[hc], sil2, ps.ap(pbk)[:, 0:SEG], ALU.mult, tag="moesh")
            ps.release(pbk)
        for dc in range(8):
            pb = ps.acquire()
            for hc in range(NHC):
                P.mm(ps.ap(pb)[:, 0:SEG],
                     w2T[hc][:, dc * 128:(dc + 1) * 128], h2[hc],
                     start=(hc == 0), stop=(hc == NHC - 1), tag="moesy")
            ob = obs[dc % 4]
            P.tt(ob, ps.ap(pb)[:, 0:SEG], wEB, ALU.mult, tag="moesw")
            ps.release(pb)
            P.dma(POOL, acc[dc * 128:(dc + 1) * 128,
                            s * SEG:(s + 1) * SEG], ob,
                  chan="acc" + str(dc % 2), accum_op=ALU.add, tag="moesacc")
    sb.release()

    sb.mark()
    mtile = [sb.alloc(D, BF16, f"moeft_{j}") for j in range(2)]
    stg = [sb.alloc(SEG, BF16, f"moefs_{i}") for i in range(8)]
    for g in range(8):
        for j in range(4):
            tt = g * 4 + j
            mt = mtile[tt % 2]
            P.dma(SP, mt, moe_out[tt * 128:(tt + 1) * 128, :],
                  chan="x" + str(tt % 2), tag="moefl")
            for cc in range(8):
                pb = ps.acquire()
                pv = ps.ap(pb).bitcast(BF16)
                P.transpose(pv[:, 0:128],
                            mt[:, cc * 128:(cc + 1) * 128], idb,
                            tag="moeftr")
                P.copy(stg[cc][:, j * 128:(j + 1) * 128], pv[:, 0:128],
                       eng=(ACT if cc % 2 else DVE), tag="moefe")
                ps.release(pb)
        for cc in range(8):
            P.dma(POOL, acc[cc * 128:(cc + 1) * 128,
                            g * SEG:(g + 1) * SEG], stg[cc],
                  chan="acc" + str(cc % 2), accum_op=ALU.add, tag="moefacc")
    sb.release()
    sb.release()


def build_L0_moe(C, xn_hbm, acc):
    P, sb, ps, nc = C.P, C.sb, C.ps, C.nc
    sb.mark()
    NE = 7
    HID = 2816
    NHC = HID // 128
    for e in range(NE):
        C.stage_wT("L0_ffn_ew1", idx=e)
        C.stage_wT("L0_ffn_ew3", idx=e)
        C.stage_wT("L0_ffn_ew2", idx=e)
    C.stage_wT("L0_ffn_sw1")
    C.stage_wT("L0_ffn_sw3")
    C.stage_wT("L0_ffn_sw2")

    wrows = nc.dram_tensor("moe_wrows", [NE + 1, T], BF16, kind="Internal").ap()
    rwT = C.load_wT_small("L0_ffn_router_w", "w1")
    rb = sb.alloc(NE, F32, "rbias")
    P.dma(SP, rb[0:1, :], C.dram["L0_ffn_routing_bias"][None, :], chan="w1")
    xw = [sb.alloc(SEG, BF16, f"moexw_{i}") for i in range(8)]
    sb.mark()
    probs = [sb.alloc(SEG, F32, f"moepr_{e}") for e in range(NE)]
    pmax = sb.alloc(SEG, F32, "moepmax")
    pclip = sb.alloc(SEG, F32, "moepclip")
    rtw = sb.alloc(SEG, F32, "moertw")
    wrow = sb.alloc(SEG, BF16, "moewrow")
    for s in range(NSEG):
        for i in range(8):
            P.dma(SP, xw[i], xn_hbm[i * 128:(i + 1) * 128,
                                    s * SEG:(s + 1) * SEG], chan="x0")
        for e in range(NE):
            pb = ps.acquire()
            for ci in range(8):
                P.mm(ps.ap(pb)[0:1, 0:SEG], rwT[ci][:, e:e + 1], xw[ci],
                     start=(ci == 0), stop=(ci == 7), tag="moegate")
            pr = probs[e][0:1, :]
            P.ts(pr, ps.ap(pb)[0:1, 0:SEG], rb[0:1, e:e + 1], ALU.add)
            ps.release(pb)
            P.ts(pr, pr, 50.0, ALU.min)
            P.ts(pr, pr, -50.0, ALU.max)
            P.act(pr, pr, AF.Sigmoid, tag="moesig")
        pm = pmax[0:1, :]
        P.tt(pm, probs[0][0:1, :], probs[1][0:1, :], ALU.max)
        for e in range(2, NE):
            P.tt(pm, pm, probs[e][0:1, :], ALU.max)
        pc = pclip[0:1, :]
        P.ts(pc, pm, 1e-8, ALU.max)
        P.ts(pc, pc, 1.0 - 1e-8, ALU.min)
        rt = rtw[0:1, :]
        P.ts(rt, pc, 0.5 + 1e-8, ALU.add)
        P.ts(rt, rt, 0.5, ALU.max)
        P.ts(rt, rt, 2.0, ALU.min)
        P.recip(rt, rt)
        for e in range(NE):
            wr = wrow[0:1, :]
            P.tt(wr, probs[e][0:1, :], pm, ALU.is_ge, tag="moemask")
            P.tt(wr, wr, pc, ALU.mult)
            P.tt(wr, wr, rt, ALU.mult)
            P.dma(SP, wrows[e:e + 1, s * SEG:(s + 1) * SEG], wr,
                  chan="wr", tag="moewr")
        P.ts(wrow[0:1, :], rt, 0.5, ALU.mult)
        P.dma(SP, wrows[NE:NE + 1, s * SEG:(s + 1) * SEG], wrow[0:1, :],
              chan="wr", tag="moewr")
    sb.release()

    xw2 = [[sb.alloc(SEG, BF16, f"moexw2_{j}_{i}") for i in range(8)]
           for j in range(2)]
    h = [sb.alloc(SEG, BF16, f"moeh_{i}") for i in range(NHC)]
    sil = sb.alloc(SEG, BF16, "moesil")
    wEB = sb.alloc(SEG, BF16, "moewEB")
    wrseg = sb.alloc(SEG, BF16, "moewrseg")
    obs = [sb.alloc(SEG, F32, f"moeob{i}") for i in range(4)]
    for e in range(NE + 1):
        sb.mark()
        if e < NE:
            w1T = C.load_wT("L0_ffn_ew1", "w0", idx=e)
            w3T = C.load_wT("L0_ffn_ew3", "w0b", idx=e)
            w2T = C.load_wT("L0_ffn_ew2", "w0c", idx=e)
        else:
            w1T = C.load_wT("L0_ffn_sw1", "w0")
            w3T = C.load_wT("L0_ffn_sw3", "w0b")
            w2T = C.load_wT("L0_ffn_sw2", "w0c")
        for s in range(NSEG):
            xws = xw2[s % 2]
            for i in range(8):
                P.dma(SP, xws[i], xn_hbm[i * 128:(i + 1) * 128,
                                         s * SEG:(s + 1) * SEG],
                      chan="x" + str(i % 2))
            P.dma(SP, wrseg[0:1, :], wrows[e:e + 1, s * SEG:(s + 1) * SEG],
                  chan="wr2")
            pb = ps.acquire()
            P.copy(wEB, C.bcast_row(wrseg[0:1, :], pb, SEG), tag="moewEB")
            ps.release(pb)
            for hc in range(NHC):
                pa = ps.acquire()
                for ci in range(8):
                    P.mm(ps.ap(pa)[:, 0:SEG],
                         w1T[ci][:, hc * 128:(hc + 1) * 128], xws[ci],
                         start=(ci == 0), stop=(ci == 7), tag="moea")
                pbk = ps.acquire()
                for ci in range(8):
                    P.mm(ps.ap(pbk)[:, 0:SEG],
                         w3T[ci][:, hc * 128:(hc + 1) * 128], xws[ci],
                         start=(ci == 0), stop=(ci == 7), tag="moeb")
                P.act(sil, ps.ap(pa)[:, 0:SEG], AF.Silu, tag="moesilu")
                ps.release(pa)
                P.tt(h[hc], sil, ps.ap(pbk)[:, 0:SEG], ALU.mult, tag="moeh")
                ps.release(pbk)
            for dc in range(8):
                pb = ps.acquire()
                for hc in range(NHC):
                    P.mm(ps.ap(pb)[:, 0:SEG],
                         w2T[hc][:, dc * 128:(dc + 1) * 128], h[hc],
                         start=(hc == 0), stop=(hc == NHC - 1), tag="moey")
                ob = obs[dc % 4]
                P.tt(ob, ps.ap(pb)[:, 0:SEG], wEB, ALU.mult, tag="moew")
                ps.release(pb)
                P.dma(POOL, acc[dc * 128:(dc + 1) * 128,
                                s * SEG:(s + 1) * SEG], ob,
                      chan="acc" + str(dc % 2), accum_op=ALU.add,
                      tag="moeacc")
        sb.release()
    sb.release()



def build_L1_ffn(C, xn_hbm, acc):
    P, sb, ps, nc = C.P, C.sb, C.ps, C.nc
    sb.mark()
    HID = 4096
    gwT = C.load_wT_small("L1_ffn_gate_w", "w1")
    xw = [sb.alloc(SEG, BF16, f"l1xw_{i}") for i in range(8)]
    m0 = sb.alloc(T, BF16, "l1m0")
    s0row = sb.alloc(SEG, F32, "l1s0")
    for s in range(NSEG):
        for i in range(8):
            P.dma(SP, xw[i], xn_hbm[i * 128:(i + 1) * 128,
                                    s * SEG:(s + 1) * SEG], chan="x0")
        pbs = [ps.acquire(), ps.acquire()]
        for e in range(2):
            for ci in range(8):
                P.mm(ps.ap(pbs[e])[0:1, 0:SEG], gwT[ci][:, e:e + 1], xw[ci],
                     start=(ci == 0), stop=(ci == 7), tag="l1gate")
        P.copy(s0row[0:1, :], ps.ap(pbs[0])[0:1, 0:SEG])
        P.tt(m0[0:1, s * SEG:(s + 1) * SEG], s0row[0:1, :],
             ps.ap(pbs[1])[0:1, 0:SEG], ALU.is_ge, tag="l1m0")
        ps.release(pbs[0])
        ps.release(pbs[1])

    b20 = fm_bias_col(C, "L1_ffn_b2", "w1", idx=0)
    b21 = fm_bias_col(C, "L1_ffn_b2", "w1", idx=1)
    sb2 = fm_bias_col(C, "L1_ffn_sb2", "w1")
    b2s = sb.alloc(8, F32, "l1b2s")
    for i in range(8):
        P.tt(b2s[:, i:i + 1], b20[i], b21[i], ALU.add)
        P.tt(b2s[:, i:i + 1], b2s[:, i:i + 1], sb2[i], ALU.add)
    b2cols = [b2s[:, i:i + 1] for i in range(8)]

    h = [sb.alloc(SEG, BF16, f"l1h_{i}") for i in range(32)]
    xw1 = [sb.alloc(SEG, BF16, f"l1xwb_{i}") for i in range(8)]
    xw2 = [xw1, xw1]
    xe1 = [sb.alloc(SEG, BF16, f"l1xe_{i}") for i in range(8)]
    xe2 = [xe1, xe1]
    obs = [sb.alloc(SEG, F32, f"l1ob{i}") for i in range(2)]

    for ei, (w1n, w2n, b1n, idx) in enumerate((
            ("L1_ffn_w1", "L1_ffn_w2", "L1_ffn_b1", 0),
            ("L1_ffn_w1", "L1_ffn_w2", "L1_ffn_b1", 1),
            ("L1_ffn_sw1", "L1_ffn_sw2", "L1_ffn_sb1", None))):
        sb.mark()
        b1c = fm_bias_col(C, b1n, "w1", idx=idx)
        if idx is None:
            w1 = C.load_w(w1n, "w0")
            w2 = C.load_w(w2n, "w0")
        else:
            w1 = []
            for r0 in range(0, D, 128):
                t = sb.alloc(HID, BF16, f"w_{w1n}{idx}_{r0}")
                P.dma(POOL, t, C.dram[w1n][idx, r0:r0 + 128, :], chan="w0")
                w1.append(t)
            w2 = []
            for r0 in range(0, HID, 128):
                t = sb.alloc(D, BF16, f"w_{w2n}{idx}_{r0}")
                P.dma(POOL, t, C.dram[w2n][idx, r0:r0 + 128, :], chan="w0")
                w2.append(t)
        for s in range(NSEG):
            xw = xw2[s % 2]
            xe = xe2[s % 2]
            for i in range(8):
                P.dma(SP, xw[i], xn_hbm[i * 128:(i + 1) * 128,
                                        s * SEG:(s + 1) * SEG],
                      chan="x" + str(i % 2))
            if ei < 2:
                pb = ps.acquire()
                mB = C.bcast_row(m0[0:1, s * SEG:(s + 1) * SEG], pb, SEG)
                for i in range(8):
                    if ei == 0:
                        P.tt(xe[i], xw[i], mB, ALU.mult, tag="l1mask")
                    else:
                        P.tt(xe[i], xw[i], mB, ALU.mult, tag="l1mask")
                        P.tt(xe[i], xw[i], xe[i], ALU.subtract, tag="l1mask2")
                ps.release(pb)
                xin = xe
            else:
                xin = xw

            for hc in range(32):
                pb = ps.acquire()
                for ci in range(8):
                    P.mm(ps.ap(pb)[:, 0:SEG],
                         w1[ci][:, hc * 128:(hc + 1) * 128], xin[ci],
                         start=(ci == 0), stop=(ci == 7), tag="l1h")
                P.act(h[hc], ps.ap(pb)[:, 0:SEG], AF.Relu,
                      bias=b1c[hc], tag="l1relu")
                ps.release(pb)
            for dc in range(8):
                pb = ps.acquire()
                for hc in range(32):
                    P.mm(ps.ap(pb)[:, 0:SEG],
                         w2[hc][:, dc * 128:(dc + 1) * 128], h[hc],
                         start=(hc == 0), stop=(hc == 31), tag="l1y")
                ob = obs[dc % 2]
                if ei == 2:
                    P.ts(ob, ps.ap(pb)[:, 0:SEG], b2cols[dc], ALU.add,
                         tag="l1badd")
                else:
                    P.copy(ob, ps.ap(pb)[:, 0:SEG])
                P.dma(POOL, acc[dc * 128:(dc + 1) * 128,
                                s * SEG:(s + 1) * SEG], ob,
                      chan="acc" + str(dc % 2), accum_op=ALU.add, tag="l1acc")
                ps.release(pb)
        sb.release()
    sb.release()


def b1c_ap(b1c, hc):
    return b1c[hc]



def ln_outer(C, X, xn_hbm):
    P, sb = C.P, C.sb
    sb.mark()
    xw2 = [[sb.alloc(SEG, F32, f"lno_x{j}_{i}") for i in range(8)]
           for j in range(2)]
    ow2 = [[sb.alloc(SEG, BF16, f"lno_o{j}_{i}") for i in range(8)]
           for j in range(2)]
    for s in range(NSEG):
        xw = xw2[s % 2]
        ow = ow2[s % 2]
        for i in range(8):
            P.dma(SP, xw[i], X[i * 128:(i + 1) * 128, s * SEG:(s + 1) * SEG],
                  chan="x" + str(i % 2), tag="lnoload")
        ln_window(C, xw, SEG, ow, None, None, eps=1e-5, tag="lno")
        for i in range(8):
            P.dma(SP, xn_hbm[i * 128:(i + 1) * 128, s * SEG:(s + 1) * SEG],
                  ow[i], chan="y" + str(i % 2), tag="lnostore")
    sb.release()


_BUILDERS = {
    0: (None, None),
}


def prewarm_shared(C):
    build_rrot(C)
    build_band_mask(C)
    build_causal_mask(C)
    C.reversal_bf16()


def build_layer(C, lyr, xn_hbm, X):
    ln_outer(C, X, xn_hbm)
    attn, ffn = {
        0: (build_L0_attn, build_L0_moe_sparse),
        1: (build_L1_attn, build_L1_ffn),
        2: (build_L2_attn, build_L2_ffn),
        3: (build_L3_attn, build_L3_ffn),
    }[lyr]
    ffn(C, xn_hbm, X)
    attn(C, xn_hbm, X)


def _declare_weights(nc, lyr=None):
    dram = {}
    pref = None if lyr is None else f"L{lyr}_"
    for nm, shp in _INPUT_SPECS:
        if nm == "L0_attn_x":
            continue
        if pref is None or nm.startswith(pref):
            dram[nm] = nc.dram_tensor(nm, list(shp), F32,
                                      kind="ExternalInput").ap()
    return dram


def build_layer_nc(lyr):
    nc = bass.Bass("TRN2", target_bir_lowering=False)
    P = Prog(nc)
    sb = SbufAlloc(nc)
    ps = PsumAlloc(nc)
    dram = _declare_weights(nc, lyr)
    x_in = nc.dram_tensor("x", [D, T], F32, kind="ExternalInput").ap()
    out = nc.dram_tensor("out", [D, T], F32, kind="ExternalOutput").ap()
    xn_hbm = nc.dram_tensor("xn_hbm", [D, T], BF16, kind="Internal").ap()
    C = Ctx(nc, P, sb, ps, dram)
    prewarm_shared(C)
    P.dma(SP, out, x_in, chan="xcopy", tag="xcopy")
    build_layer(C, lyr, xn_hbm, out)
    stats = P.finalize()
    return nc, stats


def build_full_nc():
    nc = bass.Bass("TRN2", target_bir_lowering=False)
    P = Prog(nc)
    sb = SbufAlloc(nc)
    ps = PsumAlloc(nc)
    dram = _declare_weights(nc, None)
    x_in = nc.dram_tensor("x", [D, T], F32, kind="ExternalInput").ap()
    out = nc.dram_tensor("out", [D, T], F32, kind="ExternalOutput").ap()
    xn_hbm = nc.dram_tensor("xn_hbm", [D, T], BF16, kind="Internal").ap()
    C = Ctx(nc, P, sb, ps, dram)
    prewarm_shared(C)
    P.dma(SP, out, x_in, chan="xcopy", tag="xcopy")
    for lyr in range(4):
        build_layer(C, lyr, xn_hbm, out)
    stats = P.finalize()
    return nc, stats


_FUSED = True


def kernel(**inputs):
    x = np.asarray(inputs["L0_attn_x"], dtype=np.float32)
    xs = [np.ascontiguousarray(x[c].T) for c in range(NCORES)]
    w_all = {nm: np.ascontiguousarray(np.asarray(inputs[nm], np.float32))
             for nm, _ in _INPUT_SPECS if nm != "L0_attn_x"}
    if _FUSED:
        nc, stats = build_full_nc()
        in_maps = []
        for c in range(NCORES):
            m = dict(w_all)
            m["x"] = xs[c]
            in_maps.append(m)
        res = run_bass_kernel_spmd(nc, in_maps, core_ids=list(range(NCORES)))
        outs = [res.results[c]["out"].T for c in range(NCORES)]
        return np.ascontiguousarray(np.stack(outs, 0))
    cur = xs
    for lyr in range(4):
        nc, stats = build_layer_nc(lyr)
        pref = f"L{lyr}_"
        in_maps = []
        for c in range(NCORES):
            m = {nm: w for nm, w in w_all.items() if nm.startswith(pref)}
            m["x"] = cur[c]
            in_maps.append(m)
        res = run_bass_kernel_spmd(nc, in_maps, core_ids=list(range(NCORES)))
        cur = [res.results[c]["out"] for c in range(NCORES)]
    return np.ascontiguousarray(np.stack([c.T for c in cur], 0))


# version 18 (final)
# speedup vs baseline: 1.0112x; 1.0112x over previous
import math
import numpy as np

import concourse.bass as bass
import concourse.mybir as mybir
from concourse.bass_utils import run_bass_kernel_spmd

F32 = mybir.dt.float32
BF16 = mybir.dt.bfloat16
AF = mybir.ActivationFunctionType
ALU = mybir.AluOpType
AX = mybir.AxisListType

NCORES = 8
T = 4096
D = 1024
NSEG = 8
SEG = 512

PART_BYTES = 207 * 1024

PE, DVE, ACT, POOL, SP = "PE", "DVE", "ACT", "POOL", "SP"
ENGINES = [PE, DVE, ACT, POOL, SP]


_DT_SIZE = {F32: 4, BF16: 2, mybir.dt.float16: 2, mybir.dt.int32: 4,
            mybir.dt.uint32: 4, mybir.dt.uint8: 1, mybir.dt.int8: 1}

_SBUF_BASE = {}


def spans_of_ap(ap):
    esz = _DT_SIZE.get(ap.dtype, 4)
    sp = str(ap.space)
    dims = ap.ap
    onchip = ("SBUF" in sp) or ("PSUM" in sp)
    if onchip:
        pstride = dims[0][0] if dims and dims[0][0] > 0 else None
        off = ap.offset % pstride if pstride else ap.offset
        dims = dims[1:]
        base = _SBUF_BASE.get(ap.name, 0)
    else:
        off = ap.offset
        base = 0
    lo = 0
    hi = 0
    for step, count in dims:
        if count <= 1:
            continue
        d = step * (count - 1)
        if d >= 0:
            hi += d
        else:
            lo += d
    start = base + (off + lo) * esz
    end = base + (off + hi) * esz + esz
    return start, end


class Op:
    __slots__ = ("idx", "eng", "emit", "deps", "is_dma", "chan", "clock",
                 "inc_needed", "tag", "chan_seq")

    def __init__(self, idx, eng, emit, is_dma=False, chan=None, tag=""):
        self.idx = idx
        self.eng = eng
        self.emit = emit
        self.deps = []
        self.is_dma = is_dma
        self.chan = chan
        self.clock = None
        self.inc_needed = False
        self.tag = tag


CHAN_DEPTH = 4


class Chan:

    def __init__(self, name, depth=CHAN_DEPTH):
        self.name = name
        self.depth = depth
        self.count = 0
        self.last_ops = [None] * depth


class Seg:
    __slots__ = ("start", "end", "writer", "readers")

    def __init__(self, start, end, writer, readers):
        self.start = start
        self.end = end
        self.writer = writer
        self.readers = readers


class IntervalMap:

    def __init__(self):
        self.segs = []

    def read(self, start, end, op_idx):
        deps = []
        for s in self.segs:
            if s.end <= start or s.start >= end:
                continue
            if s.writer is not None:
                deps.append(s.writer)
            s.readers.append(op_idx)
        return deps

    def write(self, start, end, op_idx):
        deps = []
        new = []
        for s in self.segs:
            if s.end <= start or s.start >= end:
                new.append(s)
                continue
            if s.writer is not None:
                deps.append(s.writer)
            deps.extend(s.readers)
            if s.start < start:
                new.append(Seg(s.start, start, s.writer, list(s.readers)))
            if s.end > end:
                new.append(Seg(end, s.end, s.writer, list(s.readers)))
        new.append(Seg(start, end, op_idx, []))
        new.sort(key=lambda s: s.start)
        self.segs = new
        return deps


class Prog:
    def __init__(self, nc, same_engine_sync=True):
        self.nc = nc
        self.ops = []
        self.maps = {"sbuf": IntervalMap(), "psum": IntervalMap()}
        self.chans = {}
        self.same_engine_sync = same_engine_sync
        self.wait_edges = []
        self._final_waits = []

    def chan(self, name):
        if name not in self.chans:
            hot = ("x0", "x1", "y0", "y1", "acc0", "acc1")
            med = ("w0", "w0b", "w0c")
            d = 8 if name in hot else (6 if name in med else 2)
            self.chans[name] = Chan(name, depth=d)
        return self.chans[name]

    def _space_of(self, ap):
        t = ap.tensor if hasattr(ap, "tensor") else None
        sp = str(ap.space)
        if "SB" in sp or "sb" in sp:
            return "sbuf"
        if "PSUM" in sp or "Psum" in sp or "psum" in sp:
            return "psum"
        key = "dram:" + ap.name
        if key not in self.maps:
            self.maps[key] = IntervalMap()
        return key

    def _add(self, eng, emit, reads=(), writes=(), is_dma=False, chan=None,
             tag=""):
        idx = len(self.ops)
        op = Op(idx, eng, emit, is_dma, chan, tag)
        deps = set()
        for ap in reads:
            m = self.maps[self._space_of(ap)]
            s, e = spans_of_ap(ap)
            deps.update(m.read(s, e, idx))
        for ap in writes:
            m = self.maps[self._space_of(ap)]
            s, e = spans_of_ap(ap)
            deps.update(m.write(s, e, idx))
        if is_dma:
            c = self.chan(chan)
            slot = c.count % c.depth
            if c.last_ops[slot] is not None:
                deps.add(c.last_ops[slot])
            c.last_ops[slot] = idx
            c.count += 1
            op.chan_seq = c.count
        deps.discard(idx)
        op.deps = sorted(deps)
        self.ops.append(op)
        return idx

    def dma(self, qeng, out, in_, chan, tag="", **kw):
        if "accum_op" in kw and "max_dma_last_dim" not in kw:
            kw["max_dma_last_dim"] = 2048

        def emit(eng):
            return eng.dma_start(out, in_, **kw)
        return self._add(qeng, emit, reads=[in_], writes=[out], is_dma=True,
                         chan=chan, tag=tag)

    def dma_transpose(self, qeng, out, in_, chan, tag=""):
        def emit(eng):
            return eng.dma_start_transpose(out, in_)
        return self._add(qeng, emit, reads=[in_], writes=[out], is_dma=True,
                         chan=chan, tag=tag)

    def mm(self, out, lhsT, rhs, start=True, stop=True, tag="", **kw):
        def emit(eng):
            return eng.matmul(out, lhsT, rhs, start=start, stop=stop, **kw)
        return self._add(PE, emit, reads=[lhsT, rhs], writes=[out], tag=tag)

    def transpose(self, out, in_, identity, tag="", **kw):
        def emit(eng):
            return eng.transpose(out, in_, identity, **kw)
        return self._add(PE, emit, reads=[in_, identity], writes=[out], tag=tag)

    def act(self, out, in_, func, bias=0.0, scale=1.0, accum_out=None, tag=""):
        extra_r = [b for b in (bias, scale) if hasattr(b, "ap")]
        extra_w = [accum_out] if accum_out is not None else []

        def emit(eng):
            return eng.activation(out, in_, func, bias=bias, scale=scale,
                                  accum_out=accum_out)
        return self._add(ACT, emit, reads=[in_] + extra_r,
                         writes=[out] + extra_w, tag=tag)

    def act_copy(self, out, in_, bias=0.0, scale=1.0, eng=ACT, tag=""):
        return self.act(out, in_, AF.Copy, bias=bias, scale=scale, tag=tag)

    def tt(self, out, a, b, alu, eng=DVE, tag=""):
        def emit(e):
            return e.tensor_tensor(out, a, b, alu)
        return self._add(eng, emit, reads=[a, b], writes=[out], tag=tag)

    def ts(self, out, in_, scalar, alu, eng=DVE, tag=""):
        extra = [scalar] if hasattr(scalar, "ap") else []

        def emit(e):
            return e.tensor_single_scalar(out, in_, scalar, alu)
        return self._add(eng, emit, reads=[in_] + extra, writes=[out], tag=tag)

    def ts2(self, out, in_, s1, s2, op0, op1, eng=DVE, tag=""):
        extra = [s for s in (s1, s2) if hasattr(s, "ap")]

        def emit(e):
            return e.tensor_scalar(out, in_, s1, s2, op0, op1)
        return self._add(eng, emit, reads=[in_] + extra, writes=[out], tag=tag)

    def stt(self, out, in0, scalar, in1, op0, op1, eng=DVE, tag=""):
        extra = [scalar] if hasattr(scalar, "ap") else []

        def emit(e):
            return e.scalar_tensor_tensor(out, in0, scalar, in1, op0, op1)
        return self._add(eng, emit, reads=[in0, in1] + extra, writes=[out],
                         tag=tag)

    def copy(self, out, in_, eng=DVE, tag=""):
        if eng == ACT:
            def emit(e):
                return e.copy(out, in_)
        else:
            def emit(e):
                return e.tensor_copy(out, in_)
        return self._add(eng, emit, reads=[in_], writes=[out], tag=tag)

    def red(self, out, in_, alu, eng=DVE, negate=None, tag=""):
        def emit(e):
            return e.tensor_reduce(out, in_, AX.X, alu, negate=negate)
        return self._add(eng, emit, reads=[in_], writes=[out], tag=tag)

    def recip(self, out, in_, tag=""):
        def emit(e):
            return e.reciprocal(out, in_)
        return self._add(DVE, emit, reads=[in_], writes=[out], tag=tag)

    def memset(self, ap, v, eng=DVE, tag=""):
        def emit(e):
            return e.memset(ap, v)
        return self._add(eng, emit, reads=[], writes=[ap], tag=tag)

    def iota(self, ap, pattern, base=0, channel_multiplier=0, tag=""):
        def emit(e):
            return e.iota(ap, pattern, base=base,
                          channel_multiplier=channel_multiplier,
                          allow_small_or_imprecise_dtypes=True)
        return self._add(POOL, emit, reads=[], writes=[ap], tag=tag)

    def custom(self, eng, emit, reads=(), writes=(), tag=""):
        return self._add(eng, emit, reads=reads, writes=writes, tag=tag)

    def finalize(self):
        nc = self.nc
        eng_sem = {e: nc.alloc_semaphore("sem_" + e) for e in ENGINES}
        chan_sem = {c: [nc.alloc_semaphore(f"dma_{c}_{d}")
                        for d in range(ch.depth)]
                    for c, ch in self.chans.items()}

        clocks = {e: {} for e in ENGINES}
        pos_in_eng = {}
        eng_count = {e: 0 for e in ENGINES}
        for op in self.ops:
            pos_in_eng[op.idx] = eng_count[op.eng]
            eng_count[op.eng] += 1

        waits_per_op = {}
        for op in self.ops:
            clk = clocks[op.eng]
            waits = []
            for d in sorted(op.deps):
                dop = self.ops[d]
                if dop.eng == op.eng and not dop.is_dma:
                    if dop.eng == PE:
                        continue
                    if not self.same_engine_sync:
                        continue
                if dop.is_dma:
                    dd = self.chans[dop.chan].depth
                    key = ("c", dop.chan, (dop.chan_seq - 1) % dd)
                    val = (dop.chan_seq - 1) // dd + 1
                else:
                    key = ("e", dop.eng)
                    val = pos_in_eng[d] + 1
                if clk.get(key, 0) >= val:
                    continue
                waits.append(d)
                tclk = dop.clock
                if tclk:
                    for k, v in tclk.items():
                        if clk.get(k, 0) < v:
                            clk[k] = v
                clk[key] = max(clk.get(key, 0), val)
                dop.inc_needed = True
            waits_per_op[op.idx] = waits
            snap = dict(clk)
            if op.is_dma:
                dd = self.chans[op.chan].depth
                snap[("c", op.chan, (op.chan_seq - 1) % dd)] = \
                    (op.chan_seq - 1) // dd + 1
            else:
                snap[("e", op.eng)] = pos_in_eng[op.idx] + 1
            op.clock = snap

        inc_no = {}
        cnt = {e: 0 for e in ENGINES}
        for op in self.ops:
            if not op.is_dma and op.inc_needed:
                cnt[op.eng] += 1
                inc_no[op.idx] = cnt[op.eng]
            elif not op.is_dma:
                inc_no[op.idx] = None

        final_chan_waits = []
        for c in self.chans.values():
            for d in range(c.depth):
                ncomp = (c.count - d + c.depth - 1) // c.depth
                if ncomp > 0:
                    final_chan_waits.append((c.name, d, ncomp))

        streams = {e: [] for e in ENGINES}
        for op in self.ops:
            streams[op.eng].append(op)

        def emit_stream(eng_obj, eng_name):
            my_clkmax = {}
            for op in streams[eng_name]:
                for t in waits_per_op[op.idx]:
                    top = self.ops[t]
                    if top.is_dma:
                        dd = self.chans[top.chan].depth
                        sem = chan_sem[top.chan][(top.chan_seq - 1) % dd]
                        val = 16 * ((top.chan_seq - 1) // dd + 1)
                    else:
                        sem = eng_sem[top.eng]
                        val = inc_no[t]
                        assert val is not None
                    k = sem.num if hasattr(sem, "num") else id(sem)
                    if my_clkmax.get(k, -1) >= val:
                        continue
                    my_clkmax[k] = val
                    eng_obj.wait_ge(sem, val)
                inst = op.emit(eng_obj)
                if op.is_dma:
                    dd = self.chans[op.chan].depth
                    inst.then_inc(chan_sem[op.chan][(op.chan_seq - 1) % dd],
                                  16)
                elif op.inc_needed:
                    inst.then_inc(eng_sem[op.eng], 1)
            if eng_name == SP:
                for cname, d, ncomp in final_chan_waits:
                    eng_obj.wait_ge(chan_sem[cname][d], 16 * ncomp)

        with nc.Block() as block:
            block.tensor(lambda e: emit_stream(e, PE))
            block.vector(lambda e: emit_stream(e, DVE))
            block.scalar(lambda e: emit_stream(e, ACT))
            block.gpsimd(lambda e: emit_stream(e, POOL))
            block.sync(lambda e: emit_stream(e, SP))

        n_waits = sum(len(w) for w in waits_per_op.values())
        n_incs = sum(1 for o in self.ops if (o.is_dma or o.inc_needed))
        return dict(n_ops=len(self.ops), n_waits=n_waits, n_incs=n_incs)



class SbufAlloc:
    def __init__(self, nc, limit=None):
        self.nc = nc
        self.off = (nc.sbuf_base + 31) // 32 * 32
        self.limit = limit or nc.sbuf_top
        self.marks = []
        self.n = 0

    def alloc(self, cols, dtype, name=None, parts=128):
        esz = 4 if dtype == F32 else 2
        sz = cols * esz
        off = (self.off + 31) // 32 * 32
        assert off + sz <= self.limit, (
            f"sbuf overflow: {off}+{sz} > {self.limit} ({name})")
        self.off = off + sz
        self.n += 1
        nm = (name or "sb") + f"_{self.n}"
        h = self.nc.alloc_sbuf_tensor_at(nm, [parts, cols], dtype, offset=off)
        ap = h.ap()
        _SBUF_BASE[ap.name] = off
        return ap

    def mark(self):
        self.marks.append(self.off)

    def release(self):
        self.off = self.marks.pop()


class PsumAlloc:

    def __init__(self, nc):
        self.tiles = []
        for i in range(8):
            h = nc.alloc_psum_tensor(f"pb{i}", [128, 512], F32)
            ap = h.ap()
            _SBUF_BASE[ap.name] = i * 2048
            self.tiles.append(ap)
        self.free = list(range(8))

    def acquire(self):
        return self.free.pop(0)

    def release(self, i):
        self.free.append(i)

    def ap(self, i):
        return self.tiles[i]



def build_model_nc(dense_moe=True, layers=(0, 1, 2, 3), same_engine_sync=True):
    nc = bass.Bass("TRN2", target_bir_lowering=False)
    P = Prog(nc, same_engine_sync=same_engine_sync)
    sb = SbufAlloc(nc)
    ps = PsumAlloc(nc)

    dram = {}

    def din(name, shape, dtype=F32):
        dram[name] = nc.dram_tensor(name, list(shape), dtype,
                                    kind="ExternalInput").ap()
        return dram[name]

    def dscratch(name, shape, dtype=F32):
        dram[name] = nc.dram_tensor(name, list(shape), dtype,
                                    kind="Internal").ap()
        return dram[name]

    x_in = din("x", (T, D))
    out = nc.dram_tensor("out", [T, D], F32, kind="ExternalOutput").ap()

    import json
    spec_inputs = _INPUT_SPECS
    for nm, shp in spec_inputs:
        if nm != "L0_attn_x":
            din(nm, shp)

    X = dscratch("X_resid", (T, D))
    XN_HBM = dscratch("XN_hbm", (T, D), BF16)

    meta = dict(P=P, sb=sb, ps=ps, dram=dram, nc=nc, x_in=x_in, out=out,
                X=X, XN_HBM=XN_HBM)

    build_program(meta, dense_moe=dense_moe, layers=layers)
    stats = P.finalize()
    return nc, stats


_INPUT_SPECS = [
    ("L0_attn_x", (8, 4096, 1024)),
    ("L0_attn_wq", (1024, 1024)), ("L0_attn_wk", (1024, 1024)),
    ("L0_attn_wv", (1024, 1024)), ("L0_attn_wo", (1024, 1024)),
    ("L0_attn_w1", (1024, 2816)), ("L0_attn_w2", (2816, 1024)),
    ("L0_attn_w3", (1024, 2816)),
    ("L0_attn_ln1_g", (1024,)), ("L0_attn_ln1_b", (1024,)),
    ("L0_attn_ln2_g", (1024,)), ("L0_attn_ln2_b", (1024,)),
    ("L0_ffn_router_w", (7, 1024)), ("L0_ffn_routing_bias", (7,)),
    ("L0_ffn_sw1", (2816, 1024)), ("L0_ffn_sw2", (1024, 2816)),
    ("L0_ffn_sw3", (2816, 1024)),
    ("L0_ffn_ew1", (7, 2816, 1024)), ("L0_ffn_ew2", (7, 1024, 2816)),
    ("L0_ffn_ew3", (7, 2816, 1024)),
    ("L1_attn_w_in", (1024, 3072)), ("L1_attn_b_in", (3072,)),
    ("L1_attn_w_conv", (3072, 1, 3)), ("L1_attn_b_conv", (3072,)),
    ("L1_attn_w1", (3, 64)), ("L1_attn_b1", (64,)),
    ("L1_attn_w2", (64, 64)), ("L1_attn_b2", (64,)),
    ("L1_attn_w3", (64, 1024)), ("L1_attn_filt_bias", (1024,)),
    ("L1_attn_w_out", (1024, 1024)), ("L1_attn_b_out", (1024,)),
    ("L1_ffn_gate_w", (2, 1024)),
    ("L1_ffn_w1", (2, 1024, 4096)), ("L1_ffn_b1", (2, 4096)),
    ("L1_ffn_w2", (2, 4096, 1024)), ("L1_ffn_b2", (2, 1024)),
    ("L1_ffn_sw1", (1024, 4096)), ("L1_ffn_sb1", (4096,)),
    ("L1_ffn_sw2", (4096, 1024)), ("L1_ffn_sb2", (1024,)),
    ("L2_attn_Wq", (1024, 1024)), ("L2_attn_Wk", (1024, 1024)),
    ("L2_attn_Wv", (1024, 1024)), ("L2_attn_Wb", (1024, 16)),
    ("L2_attn_Wo", (1024, 1024)),
    ("L2_attn_ln1", (1024,)), ("L2_attn_ln2", (1024,)),
    ("L2_attn_Wg", (1024, 4096)), ("L2_attn_Wu", (1024, 4096)),
    ("L2_attn_Wd", (4096, 1024)),
    ("L2_ffn_W_r", (1024, 1024)), ("L2_ffn_W_k", (1024, 1024)),
    ("L2_ffn_W_v", (1024, 1024)),
    ("L2_ffn_mu_r", (1024,)), ("L2_ffn_mu_k", (1024,)),
    ("L3_attn_Wq", (1024, 1024)), ("L3_attn_bq", (1024,)),
    ("L3_attn_Wk", (1024, 1024)), ("L3_attn_bk", (1024,)),
    ("L3_attn_Wv", (1024, 1024)), ("L3_attn_bv", (1024,)),
    ("L3_attn_Wo", (1024, 1024)), ("L3_attn_bo", (1024,)),
    ("L3_ffn_weight", (1024, 1024)), ("L3_ffn_alpha", (1,)),
    ("L3_ffn_bias", (1024,)),
]



class Ctx:

    def __init__(self, nc, P, sb, ps, dram):
        self.nc = nc
        self.P = P
        self.sb = sb
        self.ps = ps
        self.dram = dram
        self.const = {}
        self.identity_bf16()
        self.identity_f32()
        self.ones_col_bf16()
        self.ones_col_f32()
        self.ones_row_f32()
        self.ones_row_bf16()

    def w(self, name):
        return self.dram[name]

    def identity_bf16(self):
        if "id_bf16" not in self.const:
            P, sb = self.P, self.sb
            io = sb.alloc(128, F32, "c_io")
            P.iota(io, [[1, 128]], base=0, channel_multiplier=-1)
            idn = sb.alloc(128, BF16, "c_idbf")
            P.ts(idn, io, 0.0, ALU.is_equal)
            self.const["id_bf16"] = idn
        return self.const["id_bf16"]

    def identity_f32(self):
        if "id_f32" not in self.const:
            P, sb = self.P, self.sb
            io = sb.alloc(128, F32, "c_iof")
            P.iota(io, [[1, 128]], base=0, channel_multiplier=-1)
            idn = sb.alloc(128, F32, "c_idf")
            P.ts(idn, io, 0.0, ALU.is_equal)
            self.const["id_f32"] = idn
        return self.const["id_f32"]

    def reversal_bf16(self):
        if "rev_bf16" not in self.const:
            P, sb = self.P, self.sb
            io = sb.alloc(128, F32, "c_ior")
            P.iota(io, [[1, 128]], base=-127, channel_multiplier=1)
            idn = sb.alloc(128, BF16, "c_rev")
            P.ts(idn, io, 0.0, ALU.is_equal)
            self.const["rev_bf16"] = idn
        return self.const["rev_bf16"]

    def ones_col_bf16(self):
        if "ones_col" not in self.const:
            t = self.sb.alloc(1, BF16, "c_ones")
            self.P.memset(t, 1.0)
            self.const["ones_col"] = t
        return self.const["ones_col"]

    def ones_col_f32(self):
        if "ones_colf" not in self.const:
            t = self.sb.alloc(1, F32, "c_onesf")
            self.P.memset(t, 1.0)
            self.const["ones_colf"] = t
        return self.const["ones_colf"]

    def ones_row_f32(self):
        if "ones_row" not in self.const:
            t = self.sb.alloc(128, F32, "c_onesr")
            self.P.memset(t[0:1, :], 1.0)
            self.const["ones_row"] = t[0:1, :]
        return self.const["ones_row"]

    def load_w(self, name, chan, rows=None, cols=None, col0=0, row0=0,
               dtype=BF16):
        w = self.dram[name]
        shp = _shape_of(name)
        R = rows or shp[-2]
        Cc = cols or shp[-1]
        tiles = []
        for r0 in range(0, R, 128):
            t = self.sb.alloc(Cc, dtype, f"w_{name}_{r0}")
            src = w[row0 + r0:row0 + r0 + 128, col0:col0 + Cc]
            self.P.dma(POOL, t, src, chan=chan, tag=f"ldw:{name}")
            tiles.append(t)
        return tiles

    def stage_wT(self, name, idx=None):
        key = f"wTstage:{name}:{idx}"
        if key not in self.const:
            w = self.dram[name]
            shp = list(_shape_of(name))
            if idx is not None:
                w = w[idx]
                shp = shp[1:]
            st = self.nc.dram_tensor(f"stg_{name}_{idx}", shp, BF16,
                                     kind="Internal").ap()
            self.P.dma(POOL, st, w, chan="wstage", tag=f"stage:{name}")
            self.const[key] = st
        return self.const[key]

    def load_wT(self, name, chan, idx=None, rows=None, cols=None, col0=0):
        st = self.stage_wT(name, idx)
        shp = _shape_of(name)
        if idx is not None:
            shp = shp[1:]
        R = rows or shp[0]
        Cc = cols or shp[1]
        tiles = []
        for c0 in range(0, Cc, 128):
            t = self.sb.alloc(R, BF16, f"wt_{name}_{c0}")
            src = st[0:R, col0 + c0:col0 + c0 + 128]
            self.P.dma_transpose(SP, t, src, chan=chan, tag=f"ldwT:{name}")
            tiles.append(t)
        return tiles

    def ones_row_bf16(self):
        if "ones_rowb" not in self.const:
            t = self.sb.alloc(128, BF16, "c_onesrb")
            self.P.memset(t[0:1, :], 1.0)
            self.const["ones_rowb"] = t[0:1, :]
        return self.const["ones_rowb"]

    def load_wT_small(self, name, chan, idx=None):
        P, sb, ps = self.P, self.sb, self.ps
        shp = _shape_of(name)
        if idx is not None:
            shp = shp[1:]
        R, Cc = shp
        src = self.dram[name] if idx is None else self.dram[name][idx]
        out = [sb.alloc(R, BF16, f"wsmT_{name}_{c0}")
               for c0 in range(0, Cc, 128)]
        idb = self.identity_bf16()[0:R, 0:R]
        sb.mark()
        rows = sb.alloc(Cc, BF16, f"wsm_{name}")
        P.dma(POOL, rows[0:R, :], src, chan=chan, tag=f"ldsm:{name}")
        for c0 in range(0, Cc, 128):
            pb = ps.acquire()
            pv = ps.ap(pb).bitcast(BF16)
            P.transpose(pv[:, 0:R], rows[0:R, c0:c0 + 128], idb, tag="wsmtr")
            P.copy(out[c0 // 128], pv[:, 0:R])
            ps.release(pb)
        sb.release()
        return out

    def shift64_sel(self, pbase):
        if pbase == 0:
            return self.identity_bf16()[0:64, 0:64]
        if "shift64" not in self.const:
            P, sb = self.P, self.sb
            t = sb.alloc(128, BF16, "c_shift64")
            sb.mark()
            io = sb.alloc(128, F32, "sh_io")
            P.iota(io, [[1, 128]], base=64, channel_multiplier=-1)
            m = sb.alloc(128, F32, "sh_m")
            P.ts(m, io, 0.0, ALU.is_equal)
            P.copy(t, m)
            sb.release()
            self.const["shift64"] = t
        return self.const["shift64"][64:128, 0:64]

    def bound_reg(self, val):
        key = f"breg:{val}"
        if key not in self.const:
            reg = self.nc.alloc_register(mybir.EngineType.Pool,
                                         f"breg{len(self.const)}")
            def em(e, r=reg, v=val):
                return e.reg_mov(r, v)
            self.P.custom(POOL, em, tag="bregmov")
            self.const[key] = reg
        return self.const[key]

    def bcast_row(self, row, psum_idx, n):
        ones = self.ones_row_bf16() if row.dtype == BF16 else self.ones_row_f32()
        pb = self.ps.ap(psum_idx)
        self.P.mm(pb[:, 0:n], ones, row, tag="bcast")
        return pb[:, 0:n]


def _shape_of(name):
    for nm, shp in _INPUT_SPECS:
        if nm == name:
            return shp
    raise KeyError(name)



def mm_fm(C, w_tiles, x_tiles, seg0, nseg, dout, epilogue, segw=SEG,
          extra_acc=None, tag=""):
    P, ps = C.P, C.ps
    ncc = len(x_tiles)
    for jc in range(0, dout, 128):
        for s in range(seg0, seg0 + nseg):
            pb = ps.acquire()
            pap = ps.ap(pb)[:, 0:segw]
            for ci, (wt, xt) in enumerate(zip(w_tiles, x_tiles)):
                P.mm(pap, wt[:, jc:jc + 128], xt[:, s * segw:(s + 1) * segw],
                     start=(ci == 0), stop=(ci == ncc - 1), tag=tag)
            epilogue(pap, jc, s, pb)
            ps.release(pb)


def load_xn_fm(C, src_hbm, chan, cols=None, dtype=BF16, name="xn"):
    cols = cols or T
    tiles = []
    for cc in range(0, D, 128):
        t = C.sb.alloc(cols, dtype, f"{name}_{cc}")
        C.P.dma(SP, t, src_hbm[cc:cc + 128, 0:cols], chan=chan, tag="ldxn")
        tiles.append(t)
    return tiles


def fm_bias_col(C, name, chan, idx=None):
    b = C.dram[name]
    if idx is not None:
        b = b[idx]
        n = _shape_of(name)[1]
    else:
        n = _shape_of(name)[0]
    nch = (n + 127) // 128
    P, sb, ps = C.P, C.sb, C.ps
    t = sb.alloc(nch, F32, f"bias_{name}")
    sb.mark()
    row = sb.alloc(n, F32, f"brow_{name}")
    P.dma(SP, row[0:1, :], b[None, :], chan=chan, tag=f"bias:{name}")
    idf = C.identity_f32()[0:1, 0:1]
    for i in range(nch):
        w = min(128, n - i * 128)
        pb = ps.acquire()
        P.transpose(ps.ap(pb)[0:w, 0:1], row[0:1, i * 128:i * 128 + w], idf,
                    tag="biastr")
        P.copy(t[0:w, i:i + 1], ps.ap(pb)[0:w, 0:1])
        ps.release(pb)
    sb.release()
    return [t[:, i:i + 1] for i in range(nch)]



def build_L3_ffn(C, xn_hbm, acc):
    P, sb, ps, nc = C.P, C.sb, C.ps, C.nc
    sb.mark()
    wch = []
    for r0 in range(0, 1024, 128):
        t = sb.alloc(1024, F32, f"l3w_{r0}")
        P.dma(SP, t, C.dram["L3_ffn_weight"][r0:r0 + 128, :], chan="w0")
        wch.append(t)
    rs = sb.alloc(8, F32, "l3rs")
    for i, t in enumerate(wch):
        def em(e, out=rs, i=i, t=t):
            return e.tensor_reduce(out[:, i:i + 1], t, AX.X, ALU.add,
                                   apply_absolute_value=True)
        P.custom(DVE, em, reads=[t], writes=[rs[:, i:i + 1]], tag="l3abs")
    rssum = sb.alloc(1, F32, "l3rss")
    P.red(rssum, rs, ALU.add)
    pb = ps.acquire()
    P.mm(ps.ap(pb)[0:1, 0:1], C.ones_col_f32(), rssum, tag="l3sum")
    delta = sb.alloc(1, F32, "l3delta")
    P.ts(delta[0:1, :], ps.ap(pb)[0:1, 0:1], 0.7 / (1024.0 * 1024.0), ALU.mult)
    ps.release(pb)
    pb = ps.acquire()
    deltaB_ps = C.bcast_row(delta[0:1, :], pb, 1)
    deltaB = sb.alloc(1, F32, "l3deltaB")
    P.copy(deltaB, deltaB_ps)
    ps.release(pb)
    ndeltaB = sb.alloc(1, F32, "l3ndeltaB")
    P.ts(ndeltaB, deltaB, -1.0, ALU.mult)
    terns = []
    hi = sb.alloc(1024, F32, "l3hi")
    lo = sb.alloc(1024, F32, "l3lo")
    for i, t in enumerate(wch):
        P.ts(hi, t, deltaB, ALU.is_gt)
        P.ts(lo, t, ndeltaB, ALU.is_lt)
        tb = sb.alloc(1024, BF16, f"l3tern_{i}")
        P.tt(tb, hi, lo, ALU.subtract)
        terns.append(tb)
    wT = [sb.alloc(1024, BF16, f"l3wT_{c0}") for c0 in range(0, 1024, 128)]
    idb = C.identity_bf16()
    for oc in range(8):
        for cc in range(8):
            pb = ps.acquire()
            pv = ps.ap(pb).bitcast(BF16)
            P.transpose(pv[:, 0:128], terns[oc][:, cc * 128:(cc + 1) * 128],
                        idb, tag="l3tr")
            P.copy(wT[cc][:, oc * 128:(oc + 1) * 128], pv[:, 0:128])
            ps.release(pb)
    al = sb.alloc(1, F32, "l3al")
    P.dma(SP, al[0:1, :], C.dram["L3_ffn_alpha"][None, :], chan="w0")
    pb = ps.acquire()
    alB_ps = C.bcast_row(al[0:1, :], pb, 1)
    alB = sb.alloc(1, F32, "l3alB")
    P.copy(alB, alB_ps)
    ps.release(pb)
    biases = fm_bias_col(C, "L3_ffn_bias", "w0")
    xt = load_xn_fm(C, xn_hbm, "x0")
    obs = [sb.alloc(T, F32, f"l3out{i}") for i in range(2)]

    def epi(pap, jc, s, pbi):
        ob = obs[(jc // 128) % 2]
        P.ts2(ob[:, s * SEG:(s + 1) * SEG], pap, alB, biases[jc // 128],
              ALU.mult, ALU.add, tag="l3epi")
        if s == NSEG - 1:
            P.dma(POOL, acc[jc:jc + 128, :], ob, chan="acc0",
                  accum_op=ALU.add, tag="l3acc")

    mm_fm(C, wT, xt, 0, NSEG, 1024, epi, tag="l3mm")
    sb.release()



def build_L2_ffn(C, xn_hbm, acc):
    P, sb, ps = C.P, C.sb, C.ps
    sb.mark()
    WIN = 1024
    wrT = C.load_wT("L2_ffn_W_r", "w0")
    wkT = C.load_wT("L2_ffn_W_k", "w0")
    wvT = C.load_wT("L2_ffn_W_v", "w0")
    mur = fm_bias_col(C, "L2_ffn_mu_r", "w1")
    muk = fm_bias_col(C, "L2_ffn_mu_k", "w1")
    xt = [sb.alloc(WIN + 1, BF16, f"l2x_{i}") for i in range(8)]
    xr = [sb.alloc(WIN, BF16, f"l2xr_{i}") for i in range(8)]
    xk = [sb.alloc(WIN, BF16, f"l2xk_{i}") for i in range(8)]
    tmp = sb.alloc(WIN, F32, "l2tmp")
    r = [sb.alloc(WIN, BF16, f"l2r_{i}") for i in range(8)]
    ka = [sb.alloc(WIN, BF16, f"l2ka_{i}") for i in range(8)]
    obs = [sb.alloc(SEG, F32, f"l2out{i}") for i in range(4)]

    for w0 in range(0, T, WIN):
        if w0 == 0:
            for i in range(8):
                P.dma(SP, xt[i][:, 1:WIN + 1],
                      xn_hbm[i * 128:(i + 1) * 128, 0:WIN], chan="x0")
                P.memset(xt[i][:, 0:1], 0.0, tag="l2z")
        else:
            for i in range(8):
                P.dma(SP, xt[i],
                      xn_hbm[i * 128:(i + 1) * 128, w0 - 1:w0 + WIN],
                      chan="x0")
        for i in range(8):
            for dst, mu in ((xr[i], mur[i]), (xk[i], muk[i])):
                P.tt(tmp, xt[i][:, 1:WIN + 1], xt[i][:, 0:WIN], ALU.subtract,
                     tag="l2sh1")
                P.stt(dst, tmp, mu, xt[i][:, 0:WIN], ALU.mult, ALU.add,
                      tag="l2sh2")

        def epi_r(pap, jc, s, pbi):
            P.act(r[jc // 128][:, s * SEG:(s + 1) * SEG], pap, AF.Sigmoid,
                  tag="l2sig")

        mm_fm(C, wrT, xr, 0, WIN // SEG, 1024, epi_r, tag="l2mmr")

        def epi_k(pap, jc, s, pbi):
            sl = slice(s * SEG, (s + 1) * SEG)
            t = ka[jc // 128]
            P.act(t[:, sl], pap, AF.Relu, tag="l2relu")
            P.act(t[:, sl], t[:, sl], AF.Square, tag="l2sq")

        mm_fm(C, wkT, xk, 0, WIN // SEG, 1024, epi_k, tag="l2mmk")

        def epi_v(pap, jc, s, pbi, w0=w0):
            sl = slice(s * SEG, (s + 1) * SEG)
            ob = obs[(jc // 128 * (WIN // SEG) + s) % 4]
            P.tt(ob, pap, r[jc // 128][:, sl], ALU.mult, tag="l2mul")
            P.dma(POOL, acc[jc:jc + 128, w0 + s * SEG:w0 + (s + 1) * SEG],
                  ob, chan="acc" + str((jc // 128) % 2), accum_op=ALU.add,
                  tag="l2acc")

        mm_fm(C, wvT, ka, 0, WIN // SEG, 1024, epi_v, tag="l2mmv")
    sb.release()



class RopeGen:

    def __init__(self, C, name, W=SEG):
        P, sb = C.P, C.sb
        self.C = C
        self.W = W
        self.cos = sb.alloc(W, BF16, f"cos_{name}")
        self.sin = sb.alloc(W, BF16, f"sin_{name}")
        self.frow = sb.alloc(128, F32, f"ropefr_{name}")
        P.iota(self.frow[0:1, :], [[0, 4], [1, 32]], base=0,
               channel_multiplier=0)
        P.act(self.frow[0:1, :], self.frow[0:1, :], AF.Exp,
              scale=-math.log(10000.0) / 32.0)
        self.pos0r = sb.alloc(W, F32, f"ropep0_{name}")
        P.iota(self.pos0r[0:1, :], [[1, W]], base=0, channel_multiplier=0)
        self.posr = sb.alloc(SEG, F32, f"ropepos_{name}")
        self.ang = sb.alloc(SEG, F32, f"ropeang_{name}")
        self.qi = sb.alloc(SEG, mybir.dt.int32, f"ropeqi_{name}")
        self.qf = sb.alloc(SEG, F32, f"ropeqf_{name}")
        self.r = sb.alloc(SEG, F32, f"roper_{name}")

    def gen(self, pos0):
        C = self.C
        P, ps = C.P, C.ps
        CHI = 6.28125
        CLO = 2 * math.pi - CHI
        posr, ang, qi, qf, r = self.posr, self.ang, self.qi, self.qf, self.r
        for p0 in range(0, self.W, SEG):
            Wc = min(SEG, self.W - p0)
            P.ts(posr[0:1, 0:Wc], self.pos0r[0:1, p0:p0 + Wc], float(pos0),
                 ALU.add, eng=POOL, tag="ropepos")
            pb = ps.acquire()
            P.mm(ps.ap(pb)[:, 0:Wc], self.frow[0:1, :], posr[0:1, 0:Wc],
                 tag="ropeouter")
            P.copy(ang[:, 0:Wc], ps.ap(pb)[:, 0:Wc])
            ps.release(pb)
            for out, shift in ((self.sin, 0.0), (self.cos, math.pi / 2)):
                ra = r[:, 0:Wc]
                P.ts(ra, ang[:, 0:Wc], shift, ALU.add)
                P.ts(ra, ra, 1.0 / (2 * math.pi), ALU.mult)
                P.copy(qi[:, 0:Wc], ra)
                P.copy(qf[:, 0:Wc], qi[:, 0:Wc])
                P.ts(ra, ang[:, 0:Wc], shift, ALU.add)
                P.stt(ra, qf[:, 0:Wc], -CHI, ra, ALU.mult, ALU.add)
                P.stt(ra, qf[:, 0:Wc], -CLO, ra, ALU.mult, ALU.add)
                P.act(out[:, p0:p0 + Wc], ra, AF.Sin, tag="ropesin")


def build_rrot(C):
    if "rrot" in C.const:
        return C.const["rrot"]
    P, sb = C.P, C.sb
    out = sb.alloc(128, BF16, "c_rrot")
    sb.mark()
    io = sb.alloc(128, F32, "rr_io")
    m1 = sb.alloc(128, F32, "rr_m1")
    m2 = sb.alloc(128, F32, "rr_m2")
    acc = sb.alloc(128, F32, "rr_acc")
    P.iota(io, [[1, 128]], base=32, channel_multiplier=-1)
    P.ts(m1, io, 0.0, ALU.is_equal)
    P.iota(io, [[1, 128]], base=-32, channel_multiplier=-1)
    P.ts(m2, io, 0.0, ALU.is_equal)
    qm = sb.alloc(1, F32, "rr_qm")
    P.iota(qm, [[0, 1]], base=0, channel_multiplier=1)
    ge64 = sb.alloc(1, F32, "rr_ge")
    P.ts(ge64, qm, 63.5, ALU.is_gt)
    P.stt(qm, ge64, -64.0, qm, ALU.mult, ALU.add)
    s1 = sb.alloc(1, F32, "rr_s1")
    P.ts(s1, qm, 31.5, ALU.is_gt)
    s2 = sb.alloc(1, F32, "rr_s2")
    P.ts(s2, s1, -1.0, ALU.mult)
    P.ts(s2, s2, 1.0, ALU.add)
    P.ts(m1, m1, s1, ALU.mult)
    P.ts(m1, m1, -1.0, ALU.mult)
    P.ts(m2, m2, s2, ALU.mult)
    P.tt(acc, m1, m2, ALU.add)
    P.copy(out, acc)
    sb.release()
    C.const["rrot"] = out
    return out


def build_band_mask(C):
    if "maskA" in C.const:
        return C.const["maskA"]
    P, sb = C.P, C.sb
    out = sb.alloc(256, BF16, "c_maskA")
    sb.mark()
    io = sb.alloc(256, F32, "ma_io")
    m1 = sb.alloc(256, F32, "ma_m1")
    P.iota(io, [[1, 256]], base=0, channel_multiplier=-1)
    P.ts(m1, io, -0.5, ALU.is_gt)
    P.ts(io, io, 128.5, ALU.is_lt)
    P.tt(m1, m1, io, ALU.mult)
    P.copy(out, m1)
    sb.release()
    C.const["maskA"] = out
    return out


def build_causal_mask(C):
    if "maskC" in C.const:
        return C.const["maskC"]
    P, sb = C.P, C.sb
    out = sb.alloc(1024, BF16, "c_maskC")
    sb.mark()
    io = sb.alloc(1024, F32, "mc_io")
    P.iota(io, [[1, 1024]], base=-384, channel_multiplier=-1)
    P.ts(io, io, -0.5, ALU.is_gt)
    P.copy(out, io)
    sb.release()
    C.const["maskC"] = out
    return out


def ln_window(C, xtiles, W, out_tiles, gcols=None, bcols=None, eps=1e-5,
              out_dtype=BF16, tag="ln"):
    P, sb, ps = C.P, C.sb, C.ps
    sb.mark()
    is_f32 = xtiles[0].dtype == F32
    xb2 = [sb.alloc(SEG, BF16, f"lnxb{j}") for j in range(2)] if is_f32 \
        else None
    sq2 = [sb.alloc(SEG, BF16, f"lnsq{j}") for j in range(2)]
    mu = sb.alloc(SEG, F32, "lnmu")
    var = sb.alloc(SEG, F32, "lnvar")
    rstd = sb.alloc(SEG, F32, "lnrstd")
    nb = sb.alloc(SEG, F32, "lnnb")
    tmp = sb.alloc(SEG, F32, "lntmp")
    ones = C.ones_col_bf16()
    for s0 in range(0, W, SEG):
        Wc = min(SEG, W - s0)
        pb_s = ps.acquire()
        pb_q = ps.acquire()
        for ci in range(8):
            xs = xtiles[ci][:, s0:s0 + Wc]
            sq = sq2[ci % 2]
            if is_f32:
                xb = xb2[ci % 2]
                P.copy(xb[:, 0:Wc], xs, tag=tag + "cast")
                xin = xb[:, 0:Wc]
            else:
                xin = xs
            P.act(sq[:, 0:Wc], xs, AF.Square, tag=tag + "sq")
            P.mm(ps.ap(pb_s)[0:1, 0:Wc], ones, xin,
                 start=(ci == 0), stop=(ci == 7), tag=tag + "s")
            P.mm(ps.ap(pb_q)[0:1, 0:Wc], ones, sq[:, 0:Wc],
                 start=(ci == 0), stop=(ci == 7), tag=tag + "q")
        mu_r = mu[0:1, 0:Wc]
        var_r = var[0:1, 0:Wc]
        rs_r = rstd[0:1, 0:Wc]
        nb_r = nb[0:1, 0:Wc]
        P.ts(mu_r, ps.ap(pb_s)[0:1, 0:Wc], 1.0 / D, ALU.mult, tag=tag)
        ps.release(pb_s)
        P.tt(var_r, mu_r, mu_r, ALU.mult, tag=tag)
        P.stt(var_r, ps.ap(pb_q)[0:1, 0:Wc], 1.0 / D, var_r, ALU.mult,
              ALU.subtract, tag=tag)
        ps.release(pb_q)
        P.ts(var_r, var_r, eps, ALU.add, tag=tag)
        P.act(rs_r, var_r, AF.Sqrt, tag=tag)
        P.recip(rs_r, rs_r, tag=tag)
        P.tt(nb_r, mu_r, rs_r, ALU.mult, tag=tag)
        P.ts(nb_r, nb_r, -1.0, ALU.mult, tag=tag)
        pb_a = ps.acquire()
        pb_b = ps.acquire()
        sB = C.bcast_row(rs_r, pb_a, Wc)
        bB = C.bcast_row(nb_r, pb_b, Wc)
        for ci in range(8):
            xs = xtiles[ci][:, s0:s0 + Wc]
            ot = out_tiles[ci][:, s0:s0 + Wc]
            P.tt(tmp[:, 0:Wc], xs, sB, ALU.mult, tag=tag + "ap1")
            if gcols is None:
                P.tt(ot, tmp[:, 0:Wc], bB, ALU.add, tag=tag + "ap2")
            else:
                P.tt(tmp[:, 0:Wc], tmp[:, 0:Wc], bB, ALU.add, tag=tag + "ap2")
                P.ts2(ot, tmp[:, 0:Wc], gcols[ci], bcols[ci], ALU.mult,
                      ALU.add, tag=tag + "gb")
        ps.release(pb_a)
        ps.release(pb_b)
    sb.release()



def rms_window(C, xtiles, W, out_tiles, gcols, eps, tag="rms"):
    P, sb, ps = C.P, C.sb, C.ps
    sb.mark()
    sq = sb.alloc(W, BF16, "rmssq")
    var = sb.alloc(W, F32, "rmsvar")
    ones = C.ones_col_bf16()
    pb_q = ps.acquire()
    for ci in range(8):
        P.act(sq, xtiles[ci][:, 0:W], AF.Square, tag=tag + "sq")
        P.mm(ps.ap(pb_q)[0:1, 0:W], ones, sq, start=(ci == 0),
             stop=(ci == 7), tag=tag + "mm")
    vr = var[0:1, :]
    P.ts(vr, ps.ap(pb_q)[0:1, 0:W], 1.0 / D, ALU.mult, tag=tag)
    ps.release(pb_q)
    P.ts(vr, vr, eps, ALU.add, tag=tag)
    P.act(vr, vr, AF.Sqrt, tag=tag)
    P.recip(vr, vr, tag=tag)
    pb_a = ps.acquire()
    sB = C.bcast_row(vr, pb_a, W)
    tmp = sb.alloc(W, F32, "rmstmp")
    for ci in range(8):
        P.tt(tmp, xtiles[ci][:, 0:W], sB, ALU.mult, tag=tag + "ap")
        P.ts(out_tiles[ci][:, 0:W], tmp, gcols[ci], ALU.mult, tag=tag + "g")
    ps.release(pb_a)
    sb.release()


def build_L2_attn(C, xn_hbm, acc):
    P, sb, ps, nc = C.P, C.sb, C.ps, C.nc
    sb.mark()
    h2_hbm = nc.dram_tensor("l2_h2", [D, T], BF16,
                            kind="ExternalOutput" if globals().get("_L2_DEBUG_H2")
                            else "Internal").ap()
    idb = C.identity_bf16()
    ln1 = fm_bias_col(C, "L2_attn_ln1", "w1")
    ln2 = fm_bias_col(C, "L2_attn_ln2", "w1")

    msk = {}
    for name in ("L", "Us", "Ui"):
        msk[name] = sb.alloc(1024, BF16, f"c_l2m{name}")
    sb.mark()
    a1 = sb.alloc(1024, F32, "l2a1")
    P.iota(a1, [[0, 16], [1, 64]], base=0, channel_multiplier=0)
    pcol = sb.alloc(1, F32, "l2pc")
    P.iota(pcol, [[0, 1]], base=0, channel_multiplier=1)
    ge = sb.alloc(1, F32, "l2ge")
    P.ts(ge, pcol, 63.5, ALU.is_gt)
    P.stt(pcol, ge, -64.0, pcol, ALU.mult, ALU.add)
    mf = sb.alloc(1024, F32, "l2mf")
    for name, op in (("L", ALU.is_lt), ("Us", ALU.is_gt), ("Ui", ALU.is_ge)):
        P.ts(mf, a1, pcol, op)
        P.copy(msk[name], mf)
    sb.release()

    sb.mark()
    wq = C.load_w("L2_attn_Wq", "w0")
    wk = C.load_w("L2_attn_Wk", "w0b")
    wv = C.load_w("L2_attn_Wv", "w0c")
    wo = C.load_w("L2_attn_Wo", "w0")
    wb = C.load_w("L2_attn_Wb", "w1")

    S_buf = sb.alloc(SEG, F32, "l2S")
    Sb = sb.alloc(SEG, BF16, "l2Sb")
    P.memset(S_buf, 0.0)
    P.memset(Sb, 0.0)

    xnw = [sb.alloc(SEG, BF16, f"l2xn_{i}") for i in range(8)]
    x2w = [sb.alloc(SEG, BF16, f"l2x2_{i}") for i in range(8)]
    qtm = [sb.alloc(D, BF16, f"l2q_{i}") for i in range(4)]
    ktm = [sb.alloc(D, BF16, f"l2k_{i}") for i in range(4)]
    kbtm = [sb.alloc(D, BF16, f"l2kb_{i}") for i in range(4)]
    vtm = [sb.alloc(D, BF16, f"l2v_{i}") for i in range(4)]
    btm = [sb.alloc(16, F32, f"l2b_{i}") for i in range(4)]
    Qfm = [sb.alloc(SEG, BF16, f"l2Qf_{i}") for i in range(8)]
    Kfm = [sb.alloc(SEG, BF16, f"l2Kf_{i}") for i in range(8)]
    Kbfm = [sb.alloc(SEG, BF16, f"l2Kbf_{i}") for i in range(8)]
    ofm = [sb.alloc(SEG, BF16, f"l2of_{i}") for i in range(8)]
    v0 = sb.alloc(D, BF16, "l2v0")
    kb0 = sb.alloc(D, BF16, "l2kb0")
    obuf = sb.alloc(D, BF16, "l2obuf")
    A_sb = sb.alloc(D, BF16, "l2A")
    AT_sb = sb.alloc(D, BF16, "l2AT")
    PW = [sb.alloc(D, BF16, f"l2PW{i}") for i in range(2)]
    PWT = [sb.alloc(D, BF16, f"l2PWT{i}") for i in range(5)]
    Vw = sb.alloc(D, BF16, "l2Vw")
    sqs = sb.alloc(D, BF16, "l2sqs")
    nrm = sb.alloc(16, F32, "l2nrm")
    h2w = [sb.alloc(SEG, BF16, f"l2h2_{i}") for i in range(8)]
    hstg = sb.alloc(SEG, F32, "l2hstg")

    for w in (range(NSEG) if not globals().get("_L2_SKIP_P1") else []):
        ws = w * SEG
        for i in range(8):
            P.dma(SP, xnw[i], xn_hbm[i * 128:(i + 1) * 128, ws:ws + SEG],
                  chan="x" + str(i % 2), tag="l2ldx")
        rms_window(C, xnw, SEG, x2w, ln1, 1e-6, tag="l2rms1")
        for tt in range(4):
            tsl = slice(tt * 128, (tt + 1) * 128)
            for dst, wmat, act in ((qtm, wq, AF.Silu), (ktm, wk, AF.Silu),
                                   (vtm, wv, AF.Silu)):
                for nh in range(2):
                    pb = ps.acquire()
                    for ci in range(8):
                        P.mm(ps.ap(pb)[:, 0:SEG], x2w[ci][:, tsl],
                             wmat[ci][:, nh * SEG:(nh + 1) * SEG],
                             start=(ci == 0), stop=(ci == 7), tag="l2proj")
                    P.act(dst[tt][:, nh * SEG:(nh + 1) * SEG],
                          ps.ap(pb)[:, 0:SEG], act, tag="l2act")
                    ps.release(pb)
            pb = ps.acquire()
            for ci in range(8):
                P.mm(ps.ap(pb)[:, 0:16], x2w[ci][:, tsl], wb[ci],
                     start=(ci == 0), stop=(ci == 7), tag="l2bproj")
            P.act(btm[tt], ps.ap(pb)[:, 0:16], AF.Sigmoid, tag="l2bsig")
            ps.release(pb)
            for dst in (qtm, ktm):
                tile = dst[tt]
                P.act(sqs[:, 0:1024], tile, AF.Square, tag="l2sq")
                def rd(e, out=nrm, in_=sqs):
                    import concourse.ap as _c
                    v = _c.AP(in_.tensor, offset=in_.offset,
                              ap=[in_.ap[0], [64, 16], [1, 64]])
                    return e.tensor_reduce(out, v, AX.X, ALU.add)
                P.custom(DVE, rd, reads=[sqs], writes=[nrm], tag="l2red")
                P.ts(nrm, nrm, 1e-6, ALU.add, tag="l2eps")
                P.act(nrm, nrm, AF.Sqrt, tag="l2sqrt")
                P.recip(nrm, nrm, tag="l2rec")
                for h in range(16):
                    P.ts(tile[:, h * 64:(h + 1) * 64],
                         tile[:, h * 64:(h + 1) * 64], nrm[:, h:h + 1],
                         ALU.mult, tag="l2nap")
            for h in range(16):
                P.ts(kbtm[tt][:, h * 64:(h + 1) * 64],
                     ktm[tt][:, h * 64:(h + 1) * 64], btm[tt][:, h:h + 1],
                     ALU.mult, tag="l2kb")
            for src, dstf in ((qtm, Qfm), (ktm, Kfm), (kbtm, Kbfm)):
                for cc in range(8):
                    pb = ps.acquire()
                    pv = ps.ap(pb).bitcast(BF16)
                    P.transpose(pv[:, 0:128],
                                src[tt][:, cc * 128:(cc + 1) * 128], idb,
                                tag="l2tr")
                    P.copy(dstf[cc][:, tsl], pv[:, 0:128], eng=ACT,
                           tag="l2tre")
                    ps.release(pb)
        for ch in (range(8) if not globals().get("_L2_SKIP_DELTA") else []):
            pbase = (ch % 2) * 64
            tt = ch // 2
            csl = slice(ch * 64, (ch + 1) * 64)
            psl = slice(pbase, pbase + 64)
            exsel = C.shift64_sel(pbase)

            def scol(h):
                return (h % 2) * 512 + (h // 2) * 64

            def sslot(buf, h):
                return buf[0:64, scol(h):scol(h) + 64]

            def halves(buf):
                return (buf[0:64, 0:512], buf[0:64, 512:1024])

            def headmm(banks, lhs_of, rhs_of, start=True, stop=True, tag="",
                       colpos=0):
                if banks is None:
                    banks = nextpair()
                headmm.last = banks
                for h in range(16):
                    bk = banks[h % 2]
                    out = ps.ap(bk)[colpos:colpos + 64,
                                    (h // 2) * 64:(h // 2) * 64 + 64]
                    lhs = lhs_of(h)
                    P.mm(out, lhs, rhs_of(h),
                         start=(start and h // 2 == 0),
                         stop=(stop and h // 2 == 7), tag=tag,
                         tile_position=(lhs.base_partition(), colpos))

            def kslice(fmbuf, h):
                return fmbuf[h // 2][(h % 2) * 64:(h % 2) * 64 + 64, csl]

            def sbslice(h):
                hb = (h % 2) * 64
                return Sb[hb:hb + 64, (h // 2) * 64:(h // 2) * 64 + 64]

            bA = (ps.acquire(), ps.acquire())
            bB = (ps.acquire(), ps.acquire())
            _pair = [bA, bB]
            _pc = [0]

            def nextpair():
                _pc[0] += 1
                return _pair[_pc[0] % 2]
            for j in range(2):
                P.mm(ps.ap(bA[j])[0:64, 0:512], exsel,
                     vtm[tt][psl, j * 512:(j + 1) * 512], tag="l2exv")
                P.copy(v0[0:64, j * 512:(j + 1) * 512],
                       ps.ap(bA[j])[0:64, 0:512], tag="l2exve")
                P.mm(ps.ap(bA[j])[0:64, 0:512], exsel,
                     kbtm[tt][psl, j * 512:(j + 1) * 512], tag="l2exk")
                P.copy(kb0[0:64, j * 512:(j + 1) * 512],
                       ps.ap(bA[j])[0:64, 0:512], eng=ACT, tag="l2exke")
            headmm(None, lambda h: kslice(Kfm, h), lambda h: kslice(Kbfm, h),
                   tag="l2mA")
            for j in range(2):
                P.tt(halves(A_sb)[j], ps.ap(headmm.last[j])[0:64, 0:512],
                     msk["L"][0:64, 0:512], ALU.mult, tag="l2Amask")
            headmm(None, lambda h: kslice(Kbfm, h), lambda h: kslice(Kfm, h),
                   tag="l2mAT")
            for j in range(2):
                P.tt(halves(AT_sb)[j], ps.ap(headmm.last[j])[0:64, 0:512],
                     msk["Us"][0:64, 0:512], ALU.mult, tag="l2ATmask")
            headmm(None, lambda h: kslice(Kfm, h), lambda h: sbslice(h),
                   tag="l2mKS")
            def parity_view(buf, j):
                import concourse.ap as _cc
                return _cc.AP(buf.tensor, offset=buf.offset + j * 64,
                              ap=[[buf.ap[0][0], 64], [128, 8], [1, 64]])
            for j in range(2):
                P.tt(halves(Vw)[j], parity_view(v0, j),
                     ps.ap(headmm.last[j])[0:64, 0:512], ALU.subtract,
                     tag="l2R")
            curP, curPT = A_sb, AT_sb
            for lvl in range(5):
                nP = PW[lvl % 2]
                nPT = PWT[lvl]
                headmm(None, lambda h: sslot(curPT, h),
                       lambda h: sslot(curP, h), tag="l2sq1")
                for j in range(2):
                    P.copy(halves(nP)[j], ps.ap(headmm.last[j])[0:64, 0:512],
                           tag="l2sqe")
                headmm(None, lambda h: sslot(curP, h),
                       lambda h: sslot(curPT, h), tag="l2sq2")
                for j in range(2):
                    P.copy(halves(nPT)[j],
                           ps.ap(headmm.last[j])[0:64, 0:512],
                           eng=ACT, tag="l2sqe2")
                curP, curPT = nP, nPT
            for lvl in range(4, -1, -1):
                pt = PWT[lvl]
                headmm(None, lambda h: sslot(pt, h), lambda h: sslot(Vw, h),
                       tag="l2ap")
                for j in range(2):
                    P.tt(halves(Vw)[j], halves(Vw)[j],
                         ps.ap(headmm.last[j])[0:64, 0:512], ALU.add,
                         tag="l2apa")
            headmm(bA, lambda h: sslot(AT_sb, h), lambda h: sslot(Vw, h),
                   tag="l2fin")
            for j in range(2):
                P.tt(halves(Vw)[j], halves(Vw)[j],
                     ps.ap(bA[j])[0:64, 0:512], ALU.subtract, tag="l2fins")
            headmm(None, lambda h: kslice(Kbfm, h), lambda h: kslice(Qfm, h),
                   tag="l2mBT")
            for j in range(2):
                P.tt(halves(A_sb)[j], ps.ap(headmm.last[j])[0:64, 0:512],
                     msk["Ui"][0:64, 0:512], ALU.mult, tag="l2BTmask")
            headmm(None, lambda h: kslice(Qfm, h), lambda h: sbslice(h),
                   tag="l2mQS")
            bQS = headmm.last
            headmm(None, lambda h: sslot(A_sb, h), lambda h: sslot(Vw, h),
                   tag="l2mBU")
            bBU = headmm.last
            for j in range(2):
                pw = parity_view(obuf, j)
                P.copy(pw, ps.ap(bQS[j])[0:64, 0:512], tag="l2oq")
                P.tt(pw, pw, ps.ap(bBU[j])[0:64, 0:512], ALU.add, tag="l2ou")
            pS = (ps.acquire(), ps.acquire())
            for h in range(16):
                hb = (h % 2) * 64
                P.mm(ps.ap(pS[h % 2])[hb:hb + 64,
                                      (h // 2) * 64:(h // 2) * 64 + 64],
                     kb0[0:64, h * 64:(h + 1) * 64], sslot(Vw, h),
                     start=(h // 2 == 0), stop=(h // 2 == 7), tag="l2mS",
                     tile_position=(0, hb))
            P.tt(S_buf[0:64, :], S_buf[0:64, :], ps.ap(pS[0])[0:64, :],
                 ALU.add, tag="l2Sadd")
            P.tt(S_buf[64:128, :], S_buf[64:128, :], ps.ap(pS[1])[64:128, :],
                 ALU.add, tag="l2Sadd2")
            ps.release(pS[0])
            ps.release(pS[1])
            P.copy(Sb, S_buf, eng=ACT, tag="l2Sb")
            for cc in range(8):
                pb2 = ps.acquire()
                pv = ps.ap(pb2).bitcast(BF16)
                P.transpose(pv[:, 0:64], obuf[0:64, cc * 128:(cc + 1) * 128],
                            idb[0:64, 0:64], tag="l2otr")
                P.copy(ofm[cc][:, ch * 64:(ch + 1) * 64], pv[:, 0:64],
                       eng=ACT, tag="l2otre")
                ps.release(pb2)
            ps.release(bA[0])
            ps.release(bA[1])
            ps.release(bB[0])
            ps.release(bB[1])
        for jc in range(8):
            pb = ps.acquire()
            for ci in range(8):
                P.mm(ps.ap(pb)[:, 0:SEG], wo[ci][:, jc * 128:(jc + 1) * 128],
                     ofm[ci], start=(ci == 0), stop=(ci == 7), tag="l2wo")
            P.tt(h2w[jc], ps.ap(pb)[:, 0:SEG], xnw[jc], ALU.add, tag="l2h2")
            ps.release(pb)
            P.copy(hstg, h2w[jc], eng=ACT, tag="l2hstg")
            P.dma(POOL, acc[jc * 128:(jc + 1) * 128, ws:ws + SEG], hstg,
                  chan="acc0", accum_op=ALU.add, tag="l2hacc")
            P.dma(SP, h2_hbm[jc * 128:(jc + 1) * 128, ws:ws + SEG], h2w[jc],
                  chan="y" + str(jc % 2), tag="l2h2w")
    sb.release()

    NHC = 16
    for half in range(2):
        sb.mark()
        wg = C.load_w("L2_attn_Wg", "w0", cols=2048, col0=half * 2048)
        wu = C.load_w("L2_attn_Wu", "w0b", cols=2048, col0=half * 2048)
        wd = C.load_w("L2_attn_Wd", "w0c", rows=2048, row0=half * 2048)
        hw2 = [[sb.alloc(SEG, BF16, f"l2ghw2_{j}_{i}") for i in range(8)]
               for j in range(2)]
        yw = [sb.alloc(SEG, BF16, f"l2gy_{i}") for i in range(8)]
        tb = [sb.alloc(SEG, BF16, f"l2gt_{i}") for i in range(NHC)]
        sil = sb.alloc(SEG, BF16, "l2gsil")
        obs = [sb.alloc(SEG, F32, f"l2gob{i}") for i in range(2)]
        for s in range(NSEG):
            hw = hw2[s % 2]
            for i in range(8):
                P.dma(SP, hw[i], h2_hbm[i * 128:(i + 1) * 128,
                                        s * SEG:(s + 1) * SEG],
                      chan="x" + str(i % 2))
            rms_window(C, hw, SEG, yw, ln2, 1e-6, tag="l2rms2")
            if globals().get("_L2_DBG2") is not None and s == 0 and half == 0:
                dbg = globals()["_L2_DBG2"]
                P.dma(SP, dbg[0:128, 0:SEG], hw[0], chan="dbg")
                P.dma(SP, dbg[128:256, 0:SEG], yw[0], chan="dbg")
                lncol = ln2[0].tensor.ap() if hasattr(ln2[0], "tensor") else None
                dbgf = sb.alloc(8, F32, "dbgln")
                for ii in range(8):
                    P.copy(dbgf[:, ii:ii + 1], ln2[ii])
                dbgb = sb.alloc(8, BF16, "dbgbb")
                P.copy(dbgb, dbgf)
                P.dma(SP, dbg[256:384, 0:8], dbgb, chan="dbg")
            for hc in range(NHC):
                pa = ps.acquire()
                for ci in range(8):
                    P.mm(ps.ap(pa)[:, 0:SEG],
                         wg[ci][:, hc * 128:(hc + 1) * 128],
                         yw[ci], start=(ci == 0), stop=(ci == 7), tag="l2g1")
                pbk = ps.acquire()
                for ci in range(8):
                    P.mm(ps.ap(pbk)[:, 0:SEG],
                         wu[ci][:, hc * 128:(hc + 1) * 128],
                         yw[ci], start=(ci == 0), stop=(ci == 7), tag="l2g3")
                P.act(sil, ps.ap(pa)[:, 0:SEG], AF.Silu, tag="l2gsil")
                ps.release(pa)
                P.tt(tb[hc], sil, ps.ap(pbk)[:, 0:SEG], ALU.mult, tag="l2gh")
                ps.release(pbk)
                if (globals().get("_L2_DBG2") is not None and s == 0
                        and half == 0 and hc == 0):
                    P.dma(SP, globals()["_L2_DBG2"][256:384, 0:SEG], tb[0],
                          chan="dbg")
            for dc in range(8):
                pb = ps.acquire()
                for hc in range(NHC):
                    P.mm(ps.ap(pb)[:, 0:SEG],
                         wd[hc][:, dc * 128:(dc + 1) * 128],
                         tb[hc], start=(hc == 0), stop=(hc == NHC - 1),
                         tag="l2g2")
                ob = obs[dc % 2]
                P.copy(ob, ps.ap(pb)[:, 0:SEG])
                ps.release(pb)
                P.dma(POOL, acc[dc * 128:(dc + 1) * 128,
                                s * SEG:(s + 1) * SEG], ob,
                      chan="acc" + str(dc % 2), accum_op=ALU.add,
                      tag="l2gacc")
        sb.release()
    sb.release()



def build_L1_attn(C, xn_hbm, acc):
    import concourse.ap as _capmod
    P, sb, ps, nc = C.P, C.sb, C.ps, C.nc
    sb.mark()
    s0_hbm = nc.dram_tensor("l1_s0", [D, T], BF16, kind="Internal").ap()
    vp_hbm = nc.dram_tensor("l1_vp", [D, T], BF16, kind="Internal").ap()
    hpad = nc.dram_tensor("l1_hpad", [2048], F32, kind="Internal").ap()
    idb = C.identity_bf16()
    rrev = C.reversal_bf16()

    sb.mark()
    b1col = fm_bias_col(C, "L1_attn_b1", "w1")[0]
    hm = sb.alloc(1, F32, "l1hm")
    P.act(hm[0:64, :], b1col[0:64, :], AF.Silu, tag="l1hmlp")
    hmb = sb.alloc(1, BF16, "l1hmb")
    P.copy(hmb[0:64, :], hm[0:64, :])
    w2t = sb.alloc(64, BF16, "l1w2")
    P.dma(POOL, w2t[0:64, :], C.dram["L1_attn_w2"], chan="w1")
    pb = ps.acquire()
    P.mm(ps.ap(pb)[0:1, 0:64], hmb[0:64, 0:1], w2t[0:64, :], tag="l1mm2")
    r2 = sb.alloc(64, F32, "l1r2")
    b2row = sb.alloc(64, F32, "l1b2r")
    P.dma(SP, b2row[0:1, :], C.dram["L1_attn_b2"][None, :], chan="w1")
    P.tt(r2[0:1, :], ps.ap(pb)[0:1, 0:64], b2row[0:1, :], ALU.add)
    ps.release(pb)
    P.act(r2[0:1, :], r2[0:1, :], AF.Silu, tag="l1silu2")
    r2b = sb.alloc(64, BF16, "l1r2b")
    P.copy(r2b[0:1, :], r2[0:1, :])
    pb = ps.acquire()
    pv = ps.ap(pb).bitcast(BF16)
    P.transpose(pv[0:64, 0:1], r2b[0:1, 0:64], idb[0:1, 0:1], tag="l1tr")
    r2c = sb.alloc(1, BF16, "l1r2c")
    P.copy(r2c[0:64, :], pv[0:64, 0:1])
    ps.release(pb)
    w3t = sb.alloc(1024, BF16, "l1w3")
    P.dma(POOL, w3t[0:64, :], C.dram["L1_attn_w3"], chan="w1")
    hrow = sb.alloc(1024, F32, "l1hrow")
    for half in range(2):
        pb = ps.acquire()
        P.mm(ps.ap(pb)[0:1, 0:SEG], r2c[0:64, :],
             w3t[0:64, half * SEG:(half + 1) * SEG], tag="l1mm3")
        P.copy(hrow[0:1, half * SEG:(half + 1) * SEG],
               ps.ap(pb)[0:1, 0:SEG])
        ps.release(pb)
    zr = sb.alloc(1024, F32, "l1zr")
    P.memset(zr[0:1, :], 0.0)
    P.dma(SP, hpad[None, 0:512], zr[0:1, 0:512], chan="hp")
    P.dma(SP, hpad[None, 1536:2048], zr[0:1, 0:512], chan="hp")
    P.dma(SP, hpad[None, 512:1536], hrow[0:1, :], chan="hp", tag="l1hpad")
    sb.release()

    sb.mark()
    win0 = C.load_w("L1_attn_w_in", "w0", cols=1024, col0=0)
    win2 = C.load_w("L1_attn_w_in", "w0b", cols=1024, col0=2048)
    bin0 = fm_bias_col(C, "L1_attn_b_in", "w1")
    wc_t = []
    for grp, base in ((0, 0), (1, 2048)):
        for r0 in range(0, 1024, 128):
            t = sb.alloc(3, F32, f"l1wc_{grp}_{r0}")
            P.dma(SP, t, C.dram["L1_attn_w_conv"][base + r0:base + r0 + 128,
                                                  0, :], chan="w1")
            wc_t.append(t)
    bc = fm_bias_col(C, "L1_attn_b_conv", "w1")
    xnw = [sb.alloc(514, BF16, f"l1xn_{i}") for i in range(8)]
    ub = sb.alloc(514, F32, "l1u")
    uc = sb.alloc(SEG, BF16, "l1uc")
    s0seg = [sb.alloc(SEG, BF16, f"l1s0_{i}") for i in range(8)]
    for s in range(NSEG):
        t0 = s * SEG
        lo = max(0, t0 - 2)
        pad = (t0 - 2) - lo
        for i in range(8):
            if s == 0:
                P.memset(xnw[i][:, 0:2], 0.0, tag="l1z")
                P.dma(SP, xnw[i][:, 2:514],
                      xn_hbm[i * 128:(i + 1) * 128, 0:SEG],
                      chan="x" + str(i % 2))
            else:
                P.dma(SP, xnw[i][:, 0:514],
                      xn_hbm[i * 128:(i + 1) * 128, t0 - 2:t0 + SEG],
                      chan="x" + str(i % 2))
        for grp, wt, coff in ((0, win0, 0), (1, win2, 16)):
            for jc in range(8):
                gi = grp * 8 + jc
                pb = ps.acquire()
                for ci in range(8):
                    P.mm(ps.ap(pb)[:, 0:2], wt[ci][:, jc * 128:(jc + 1) * 128],
                         xnw[ci][:, 0:2], start=(ci == 0), stop=(ci == 7),
                         tag="l1ue")
                pb2 = ps.acquire()
                for ci in range(8):
                    P.mm(ps.ap(pb2)[:, 0:SEG],
                         wt[ci][:, jc * 128:(jc + 1) * 128],
                         xnw[ci][:, 2:514], start=(ci == 0), stop=(ci == 7),
                         tag="l1um")
                bcol = bin0[coff + jc]
                P.ts(ub[:, 0:2], ps.ap(pb)[:, 0:2], bcol, ALU.add, tag="l1ub")
                ps.release(pb)
                P.ts(ub[:, 2:514], ps.ap(pb2)[:, 0:SEG], bcol, ALU.add,
                     tag="l1ub2")
                ps.release(pb2)
                wct = wc_t[grp * 8 + jc]
                P.ts2(uc, ub[:, 2:514], wct[:, 2:3], bc[coff + jc],
                      ALU.mult, ALU.add, tag="l1c3a")
                P.stt(uc, ub[:, 1:513], wct[:, 1:2], uc, ALU.mult, ALU.add,
                      tag="l1c3b")
                P.stt(uc, ub[:, 0:512], wct[:, 0:1], uc, ALU.mult, ALU.add,
                      tag="l1c3c")
                if grp == 0:
                    P.copy(s0seg[jc], uc, eng=POOL, tag="l1s0c")
                    P.dma(SP, s0_hbm[jc * 128:(jc + 1) * 128, t0:t0 + SEG],
                          s0seg[jc], chan="y0", tag="l1s0w")
                else:
                    P.tt(uc, uc, s0seg[jc], ALU.mult, tag="l1vp")
                    P.dma(SP, vp_hbm[jc * 128:(jc + 1) * 128, t0:t0 + SEG],
                          uc, chan="y1", tag="l1vpw")
    sb.release()

    sb.mark()
    tstage = sb.alloc(SEG, BF16, "l1tstage")
    tblocks = {}
    avals = list(range(-384, 1025, 128))
    for a in avals:
        tb = sb.alloc(SEG, BF16, f"l1T_{a}")
        tblocks[a] = tb
        hv = _capmod.AP(hpad.tensor, offset=512 + a - 127,
                        ap=[[1, 128], [1, 512]])
        P.dma(POOL, tstage, hv, chan="tb", tag="l1trev")
        pb = ps.acquire()
        P.mm(ps.ap(pb)[:, 0:SEG], rrev, tstage, tag="l1rev")
        P.copy(tb, ps.ap(pb)[:, 0:SEG], eng=ACT, tag="l1tb")
        ps.release(pb)

    wout = C.load_w("L1_attn_w_out", "w0")
    bout = fm_bias_col(C, "L1_attn_b_out", "w1")
    fb = fm_bias_col(C, "L1_attn_filt_bias", "w1")
    vtm = [sb.alloc(D, BF16, f"l1vt_{i}") for i in range(12)]
    vpw = [sb.alloc(SEG, BF16, f"l1vw_{i}") for i in range(8)]
    s0w = [sb.alloc(SEG, BF16, f"l1sw_{i}") for i in range(8)]
    zw = [sb.alloc(SEG, BF16, f"l1zw_{i}") for i in range(8)]
    obs = [sb.alloc(SEG, F32, f"l1ob{i}") for i in range(2)]
    for lw in range(NSEG):
        t0 = lw * SEG
        for i in range(8):
            P.dma(SP, vpw[i], vp_hbm[i * 128:(i + 1) * 128, t0:t0 + SEG],
                  chan="x" + str(i % 2), tag="l1ldvp")
            P.dma(SP, s0w[i], s0_hbm[i * 128:(i + 1) * 128, t0:t0 + SEG],
                  chan="y" + str(i % 2), tag="l1lds0")
        for j in range(4):
            mt = lw * 4 + j
            dst = vtm[mt % 12]
            for ci in range(8):
                pb = ps.acquire()
                pv = ps.ap(pb).bitcast(BF16)
                P.transpose(pv[:, 0:128], vpw[ci][:, j * 128:(j + 1) * 128],
                            idb, tag="l1vtr")
                P.copy(dst[:, ci * 128:(ci + 1) * 128], pv[:, 0:128],
                       eng=ACT, tag="l1vte")
                ps.release(pb)
        mts = [m for m in range(max(0, lw * 4 - 8), lw * 4 + 4)]
        for cc in range(8):
            pb = ps.acquire()
            for mi, mt in enumerate(mts):
                a = SEG * lw - 128 * mt
                P.mm(ps.ap(pb)[:, 0:SEG],
                     vtm[mt % 12][:, cc * 128:(cc + 1) * 128], tblocks[a],
                     start=(mi == 0), stop=(mi == len(mts) - 1), tag="l1cv")
            P.stt(zw[cc], vpw[cc], fb[cc], ps.ap(pb)[:, 0:SEG], ALU.mult,
                  ALU.add, tag="l1fb")
            ps.release(pb)
            P.tt(zw[cc], zw[cc], s0w[cc], ALU.mult, eng=POOL, tag="l1zs0")
        for jc in range(8):
            pb = ps.acquire()
            for ci in range(8):
                P.mm(ps.ap(pb)[:, 0:SEG], wout[ci][:, jc * 128:(jc + 1) * 128],
                     zw[ci], start=(ci == 0), stop=(ci == 7), tag="l1out")
            ob = obs[jc % 2]
            P.ts(ob, ps.ap(pb)[:, 0:SEG], bout[jc], ALU.add, tag="l1bo")
            ps.release(pb)
            P.dma(POOL, acc[jc * 128:(jc + 1) * 128, t0:t0 + SEG], ob,
                  chan="acc" + str(jc % 2), accum_op=ALU.add, tag="l1acc")
    sb.release()
    sb.release()



def build_L0_attn(C, xn_hbm, acc):
    P, sb, ps, nc = C.P, C.sb, C.ps, C.nc
    sb.mark()
    y1_hbm = nc.dram_tensor("l0_y1", [D, T], BF16, kind="Internal").ap()
    y2_hbm = nc.dram_tensor("l0_y2", [D, T], BF16, kind="Internal").ap()
    rope = RopeGen(C, "l0", W=640)
    rrot = build_rrot(C)
    maskA = build_band_mask(C)
    g1 = fm_bias_col(C, "L0_attn_ln1_g", "w1")
    b1 = fm_bias_col(C, "L0_attn_ln1_b", "w1")
    g2 = fm_bias_col(C, "L0_attn_ln2_g", "w1")
    b2 = fm_bias_col(C, "L0_attn_ln2_b", "w1")

    sb.mark()
    xwA = [sb.alloc(SEG, BF16, f"l0xa_{i}") for i in range(8)]
    ywA = [sb.alloc(SEG, BF16, f"l0ya_{i}") for i in range(8)]
    for s in range(NSEG):
        for i in range(8):
            P.dma(SP, xwA[i], xn_hbm[i * 128:(i + 1) * 128,
                                     s * SEG:(s + 1) * SEG],
                  chan="x" + str(i % 2))
        ln_window(C, xwA, SEG, ywA, g1, b1, tag="l0ln1")
        for i in range(8):
            P.dma(SP, y1_hbm[i * 128:(i + 1) * 128, s * SEG:(s + 1) * SEG],
                  ywA[i], chan="y" + str(i % 2), tag="l0y1w")
    sb.release()

    import concourse.ap as _capmod
    wperm = []
    sb.mark()
    wnat = [sb.alloc(1024, BF16, f"l0wn_{i}") for i in range(8)]
    for wname in ("L0_attn_wq", "L0_attn_wk"):
        tiles = []
        for r0 in range(0, D, 128):
            t = sb.alloc(1024, BF16, f"l0wp_{wname}_{r0}")
            tiles.append(t)
        wperm.append(tiles)
    for wi, wname in enumerate(("L0_attn_wq", "L0_attn_wk")):
        for r0 in range(0, D, 128):
            P.dma(POOL, wnat[r0 // 128], C.dram[wname][r0:r0 + 128, :],
                  chan="w0", tag="l0wload")
        for r0 in range(0, D, 128):
            src_v = _capmod.AP(wnat[r0 // 128].tensor, offset=0,
                               ap=[[1024, 128], [64, 16], [1, 2], [2, 32]])
            P.copy(wperm[wi][r0 // 128], src_v, tag="l0perm")
    wv = wnat
    for r0 in range(0, D, 128):
        P.dma(POOL, wv[r0 // 128], C.dram["L0_attn_wv"][r0:r0 + 128, :],
              chan="w0", tag="l0wv")
    wo = [sb.alloc(1024, BF16, f"l0wo_{i}") for i in range(8)]
    for r0 in range(0, D, 128):
        P.dma(POOL, wo[r0 // 128], C.dram["L0_attn_wo"][r0:r0 + 128, :],
              chan="w0", tag="l0wo")
    wqp, wkp = wperm

    y1w = [sb.alloc(640, BF16, f"l0y1w_{i}") for i in range(8)]
    Kd = [sb.alloc(640, BF16, f"l0K_{i}") for i in range(8)]
    Qd = [sb.alloc(SEG, BF16, f"l0Q_{i}") for i in range(8)]
    V_tm = [sb.alloc(D, BF16, f"l0V_{i}") for i in range(5)]
    raw = sb.alloc(640, BF16, "l0raw")
    rot = sb.alloc(640, BF16, "l0rot")
    t1 = sb.alloc(640, BF16, "l0t1")
    hW = [sb.alloc(SEG, BF16, f"l0h_{i}") for i in range(8)]
    y2W = [sb.alloc(SEG, BF16, f"l0y2_{i}") for i in range(8)]
    xnw = [sb.alloc(SEG, BF16, f"l0xn_{i}") for i in range(8)]
    rsum = sb.alloc(SEG, F32, "l0rsum")
    recB = sb.alloc(SEG, BF16, "l0recB")
    Pkm = sb.alloc(SEG, BF16, "l0P")
    hstg = sb.alloc(SEG, F32, "l0hstg")

    def proj_rope_piece(dst_tiles, wT, c0, wlen, coff):
        for jc in range(8):
            pb = ps.acquire()
            for ci in range(8):
                P.mm(ps.ap(pb)[:, 0:wlen],
                     wT[ci][:, jc * 128:(jc + 1) * 128],
                     y1w[ci][:, c0:c0 + wlen],
                     start=(ci == 0), stop=(ci == 7), tag="l0qk")
            P.copy(raw[:, 0:wlen], ps.ap(pb)[:, 0:wlen], eng=ACT,
                   tag="l0raw")
            ps.release(pb)
            pb = ps.acquire()
            P.mm(ps.ap(pb)[:, 0:wlen], rrot, raw[:, 0:wlen], tag="l0rot")
            P.tt(t1[:, 0:wlen], raw[:, 0:wlen],
                 rope.cos[:, coff:coff + wlen], ALU.mult, eng=POOL,
                 tag="l0rc")
            P.tt(rot[:, 0:wlen], ps.ap(pb)[:, 0:wlen],
                 rope.sin[:, coff:coff + wlen], ALU.mult, tag="l0rs")
            ps.release(pb)
            P.tt(dst_tiles[jc][:, c0:c0 + wlen], t1[:, 0:wlen],
                 rot[:, 0:wlen], ALU.add, eng=POOL, tag="l0radd")

    for w in range(NSEG):
        ws = w * SEG
        kb = max(0, ws - 128)
        kvlen = ws + SEG - kb
        qoff = ws - kb
        for i in range(8):
            P.dma(SP, y1w[i][:, 0:kvlen],
                  y1_hbm[i * 128:(i + 1) * 128, kb:kb + kvlen],
                  chan="x" + str(i % 2), tag="l0ldy1")
            P.dma(SP, xnw[i], xn_hbm[i * 128:(i + 1) * 128, ws:ws + SEG],
                  chan="y" + str(i % 2), tag="l0ldxn")
        rope.gen(kb)
        for (c0, wlen) in ((0, min(kvlen, SEG)),) + (
                ((SEG, kvlen - SEG),) if kvlen > SEG else ()):
            proj_rope_piece(Kd, wkp, c0, wlen, c0)
        proj_rope_piece(Qd, wqp, 0, SEG, qoff)
        nkt = (kvlen + 127) // 128
        for tt in range(nkt):
            for nh in range(2):
                pb = ps.acquire()
                for ci in range(8):
                    P.mm(ps.ap(pb)[:, 0:SEG],
                         y1w[ci][:, tt * 128:(tt + 1) * 128],
                         wv[ci][:, nh * SEG:(nh + 1) * SEG],
                         start=(ci == 0), stop=(ci == 7), tag="l0v")
                P.copy(V_tm[tt][:, nh * SEG:(nh + 1) * SEG],
                       ps.ap(pb)[:, 0:SEG], eng=ACT, tag="l0vc")
                ps.release(pb)
        for h in range(16):
            hc, hr = h // 2, (h % 2) * 64
            pb_o = ps.acquire()
            pb_r = ps.acquire()
            for kt in range(nkt):
                kglob = kb + kt * 128
                q0 = max(0, kglob - ws)
                q1 = min(SEG, kglob + 256 - ws)
                wq = q1 - q0
                a = (ws + q0) - kglob
                pb_s = ps.acquire()
                P.mm(ps.ap(pb_s)[:, 0:wq],
                     Kd[hc][hr:hr + 64, kt * 128:(kt + 1) * 128],
                     Qd[hc][hr:hr + 64, q0:q1], tag="l0sc")
                P.act(Pkm[:, 0:wq], ps.ap(pb_s)[:, 0:wq], AF.Exp,
                      scale=0.125, tag="l0exp")
                ps.release(pb_s)
                P.tt(Pkm[:, 0:wq], Pkm[:, 0:wq], maskA[:, a:a + wq],
                     ALU.mult, eng=POOL, tag="l0mask")
                P.mm(ps.ap(pb_r)[0:1, q0:q1], C.ones_col_bf16(),
                     Pkm[:, 0:wq], start=(kt == 0), stop=(kt == nkt - 1),
                     tag="l0rs")
                P.mm(ps.ap(pb_o)[hr:hr + 64, q0:q1],
                     V_tm[kt][:, h * 64:(h + 1) * 64], Pkm[:, 0:wq],
                     start=(kt == 0), stop=(kt == nkt - 1), tag="l0pv",
                     tile_position=(0, hr))
            P.recip(rsum[0:1, :], ps.ap(pb_r)[0:1, 0:SEG], tag="l0rec")
            ps.release(pb_r)
            pb_b = ps.acquire()
            rb = C.bcast_row(rsum[0:1, :], pb_b, SEG)
            P.copy(recB, rb, tag="l0recB")
            ps.release(pb_b)
            P.tt(Qd[hc][hr:hr + 64, :], ps.ap(pb_o)[hr:hr + 64, 0:SEG],
                 recB[hr:hr + 64, :], ALU.mult, tag="l0norm")
            ps.release(pb_o)
        for jc in range(8):
            pb = ps.acquire()
            for ci in range(8):
                P.mm(ps.ap(pb)[:, 0:SEG], wo[ci][:, jc * 128:(jc + 1) * 128],
                     Qd[ci], start=(ci == 0), stop=(ci == 7), tag="l0om")
            P.tt(hW[jc], ps.ap(pb)[:, 0:SEG], xnw[jc], ALU.add, tag="l0hadd")
            ps.release(pb)
            P.copy(hstg, hW[jc], eng=ACT, tag="l0hstg")
            P.dma(POOL, acc[jc * 128:(jc + 1) * 128, ws:ws + SEG], hstg,
                  chan="acc0", accum_op=ALU.add, tag="l0hacc")
        ln_window(C, hW, SEG, y2W, g2, b2, tag="l0ln2")
        for i in range(8):
            P.dma(SP, y2_hbm[i * 128:(i + 1) * 128, ws:ws + SEG], y2W[i],
                  chan="y" + str(i % 2), tag="l0y2w")
    sb.release()

    sb.mark()
    NHC = 2816 // 128
    w1 = C.load_w("L0_attn_w1", "w0")
    w3 = C.load_w("L0_attn_w3", "w0b")
    w2 = C.load_w("L0_attn_w2", "w0c")
    xw2 = [[sb.alloc(SEG, BF16, f"l0fx2_{j}_{i}") for i in range(8)]
           for j in range(2)]
    hT = [sb.alloc(SEG, BF16, f"l0ft_{i}") for i in range(NHC)]
    sil = sb.alloc(SEG, BF16, "l0fsil")
    obs = [sb.alloc(SEG, F32, f"l0fob{i}") for i in range(2)]
    for s in range(NSEG):
        xw = xw2[s % 2]
        for i in range(8):
            P.dma(SP, xw[i], y2_hbm[i * 128:(i + 1) * 128,
                                    s * SEG:(s + 1) * SEG],
                  chan="x" + str(i % 2))
        for hc in range(NHC):
            pa = ps.acquire()
            for ci in range(8):
                P.mm(ps.ap(pa)[:, 0:SEG], w1[ci][:, hc * 128:(hc + 1) * 128],
                     xw[ci], start=(ci == 0), stop=(ci == 7), tag="l0f1")
            pbk = ps.acquire()
            for ci in range(8):
                P.mm(ps.ap(pbk)[:, 0:SEG], w3[ci][:, hc * 128:(hc + 1) * 128],
                     xw[ci], start=(ci == 0), stop=(ci == 7), tag="l0f3")
            P.act(sil, ps.ap(pa)[:, 0:SEG], AF.Silu, tag="l0fsil")
            ps.release(pa)
            P.tt(hT[hc], sil, ps.ap(pbk)[:, 0:SEG], ALU.mult, tag="l0fh")
            ps.release(pbk)
        for dc in range(8):
            pb = ps.acquire()
            for hc in range(NHC):
                P.mm(ps.ap(pb)[:, 0:SEG], w2[hc][:, dc * 128:(dc + 1) * 128],
                     hT[hc], start=(hc == 0), stop=(hc == NHC - 1),
                     tag="l0f2")
            ob = obs[dc % 2]
            P.copy(ob, ps.ap(pb)[:, 0:SEG])
            ps.release(pb)
            P.dma(POOL, acc[dc * 128:(dc + 1) * 128,
                            s * SEG:(s + 1) * SEG], ob,
                  chan="acc" + str(dc % 2), accum_op=ALU.add, tag="l0facc")
    sb.release()
    sb.release()



def build_L3_attn(C, xn_hbm, acc):
    P, sb, ps, nc = C.P, C.sb, C.ps, C.nc
    sb.mark()
    rope = RopeGen(C, "l3")
    cos, sin = rope.cos, rope.sin
    rrot = build_rrot(C)
    maskC = build_causal_mask(C)
    idb = C.identity_bf16()
    bq = fm_bias_col(C, "L3_attn_bq", "w1")
    bk = fm_bias_col(C, "L3_attn_bk", "w1")
    bo = fm_bias_col(C, "L3_attn_bo", "w1")
    _vbias_b(C)

    Kd = [sb.alloc(T, BF16, f"l3K_{i}") for i in range(8)]
    V_tm = [sb.alloc(D, BF16, f"l3V_{i}") for i in range(32)]
    xw = [sb.alloc(SEG, BF16, f"l3xw_{i}") for i in range(8)]
    raw = rope.r.bitcast(BF16)[:, 0:SEG]
    rot = sb.alloc(SEG, BF16, "l3rot")
    t1 = sb.alloc(SEG, BF16, "l3t1")
    sb.mark()
    wkT = C.load_wT("L3_attn_Wk", "w0b")
    wvT = C.load_wT("L3_attn_Wv", "w0c")

    def project_rope(dst_tiles, wT, bcolv, dst0, pos0):
        rope.gen(pos0)
        sl = slice(dst0, dst0 + SEG)
        psl = slice(0, SEG)
        for jc in range(8):
            pb = ps.acquire()
            for ci in range(8):
                P.mm(ps.ap(pb)[:, 0:SEG], wT[ci][:, jc * 128:(jc + 1) * 128],
                     xw[ci], start=(ci == 0), stop=(ci == 7), tag="l3qk")
            P.ts(raw, ps.ap(pb)[:, 0:SEG], bcolv[jc], ALU.add, tag="l3b")
            ps.release(pb)
            pb = ps.acquire()
            P.mm(ps.ap(pb)[:, 0:SEG], rrot, raw, tag="l3rot")
            P.tt(t1, raw, cos[:, psl], ALU.mult, eng=POOL, tag="l3rc")
            P.tt(rot, ps.ap(pb)[:, 0:SEG], sin[:, psl], ALU.mult, tag="l3rs")
            ps.release(pb)
            P.tt(dst_tiles[jc][:, sl], t1, rot, ALU.add, eng=POOL,
                 tag="l3radd")

    for s in range(NSEG):
        for i in range(8):
            P.dma(SP, xw[i], xn_hbm[i * 128:(i + 1) * 128,
                                    s * SEG:(s + 1) * SEG],
                  chan="x" + str(i % 2))
        project_rope(Kd, wkT, bk, s * SEG, 1 + s * SEG)
        for tt in range(4):
            t0 = s * SEG + tt * 128
            for nh in range(2):
                pb = ps.acquire()
                for ci in range(8):
                    P.mm(ps.ap(pb)[:, 0:SEG],
                         xw[ci][:, tt * 128:(tt + 1) * 128],
                         wvT[ci][:, nh * SEG:(nh + 1) * SEG],
                         start=(ci == 0), stop=(ci == 7), tag="l3v")
                P.copy(raw, ps.ap(pb)[:, 0:SEG], tag="l3vc")
                ps.release(pb)
                P.tt(V_tm[t0 // 128][:, nh * SEG:(nh + 1) * SEG], raw,
                     _vbias_b(C)[:, nh * SEG:(nh + 1) * SEG], ALU.add,
                     eng=POOL, tag="l3vb")

    sb.release()
    wqT = C.load_wT("L3_attn_Wq", "w0")
    woT = C.load_wT("L3_attn_Wo", "w0")
    Qd = [sb.alloc(SEG, BF16, f"l3Q_{i}") for i in range(8)]
    Pkm = t1
    recB = rot
    rsum = rope.qf
    Ow = [sb.alloc(SEG, BF16, f"l3O_{i}") for i in range(8)]
    obf = [rsum]
    for w in range(NSEG):
        ws = w * SEG
        for i in range(8):
            P.dma(SP, xw[i], xn_hbm[i * 128:(i + 1) * 128, ws:ws + SEG],
                  chan="x" + str(i % 2))
        project_rope(Qd, wqT, bq, 0, 1 + ws)
        for h in range(16):
            hc, hr = h // 2, (h % 2) * 64
            pb_o = ps.acquire()
            pb_r = ps.acquire()
            nkt = (ws + SEG) // 128
            for kt in range(nkt):
                pb_s = ps.acquire()
                P.mm(ps.ap(pb_s)[:, 0:SEG],
                     Kd[hc][hr:hr + 64, kt * 128:(kt + 1) * 128],
                     Qd[hc][hr:hr + 64, 0:SEG], tag="l3sc")
                P.act(Pkm, ps.ap(pb_s)[:, 0:SEG], AF.Exp, scale=0.125,
                      tag="l3exp")
                ps.release(pb_s)
                a = kt * 128 - ws
                if a >= 0:
                    P.tt(Pkm, Pkm, maskC[:, 384 - a:896 - a], ALU.mult,
                         eng=POOL, tag="l3mask")
                P.mm(ps.ap(pb_r)[0:1, 0:SEG], C.ones_col_bf16(), Pkm,
                     start=(kt == 0), stop=(kt == nkt - 1), tag="l3rs")
                P.mm(ps.ap(pb_o)[hr:hr + 64, 0:SEG],
                     V_tm[kt][:, h * 64:(h + 1) * 64], Pkm,
                     start=(kt == 0), stop=(kt == nkt - 1), tag="l3pv",
                     tile_position=(0, hr))
            P.recip(rsum[0:1, :], ps.ap(pb_r)[0:1, 0:SEG], tag="l3rec")
            ps.release(pb_r)
            pb_b = ps.acquire()
            rb = C.bcast_row(rsum[0:1, :], pb_b, SEG)
            P.copy(recB, rb, tag="l3recB")
            ps.release(pb_b)
            P.tt(Ow[hc][hr:hr + 64, :], ps.ap(pb_o)[hr:hr + 64, 0:SEG],
                 recB[hr:hr + 64, :], ALU.mult, tag="l3norm")
            ps.release(pb_o)
        for jc in range(8):
            pb = ps.acquire()
            for ci in range(8):
                P.mm(ps.ap(pb)[:, 0:SEG], woT[ci][:, jc * 128:(jc + 1) * 128],
                     Ow[ci], start=(ci == 0), stop=(ci == 7), tag="l3out")
            ob = obf[0]
            P.ts(ob, ps.ap(pb)[:, 0:SEG], bo[jc], ALU.add, tag="l3bo")
            ps.release(pb)
            P.dma(POOL, acc[jc * 128:(jc + 1) * 128, ws:ws + SEG], ob,
                  chan="acc0", accum_op=ALU.add, tag="l3acc")
    sb.release()


def _vbias_b(C):
    if "l3bvB" not in C.const:
        P, sb, ps = C.P, C.sb, C.ps
        t = sb.alloc(1024, BF16, "c_bvB")
        sb.mark()
        row = sb.alloc(1024, BF16, "c_bvrow")
        rowf = sb.alloc(1024, F32, "c_bvrowf")
        P.dma(SP, rowf[0:1, :], C.dram["L3_attn_bv"][None, :], chan="w1")
        P.copy(row[0:1, :], rowf[0:1, :])
        for half in range(2):
            pb = ps.acquire()
            bB = C.bcast_row(row[0:1, half * SEG:(half + 1) * SEG], pb, SEG)
            P.copy(t[:, half * SEG:(half + 1) * SEG], bB)
            ps.release(pb)
        sb.release()
        C.const["l3bvB"] = t
    return C.const["l3bvB"]



def build_L0_moe_sparse(C, xn_hbm, acc):
    from concourse.bass_core_rust import IndirectOffsetOnAxis as IOA
    P, sb, ps, nc = C.P, C.sb, C.ps, C.nc
    I32 = mybir.dt.int32
    U32 = mybir.dt.uint32
    sb.mark()
    NE = 7
    HID = 2816
    NHC = HID // 128
    CAP = 1024
    NB = CAP // 128
    EXT = 1032
    idb = C.identity_bf16()
    idf = C.identity_f32()
    for e in range(NE):
        C.stage_wT("L0_ffn_ew1", idx=e)
        C.stage_wT("L0_ffn_ew3", idx=e)
        C.stage_wT("L0_ffn_ew2", idx=e)
    C.stage_wT("L0_ffn_sw1")
    C.stage_wT("L0_ffn_sw3")
    C.stage_wT("L0_ffn_sw2")
    XROWS = T + NE * CAP + 8
    IDXN = NE * CAP + T + 8
    wrows = nc.dram_tensor("moe_wrows", [NE + 1, T], BF16,
                           kind="Internal").ap()
    xn_tm = nc.dram_tensor("moe_xn_tm", [XROWS, EXT], BF16,
                           kind="Internal").ap()
    idxmap = nc.dram_tensor("moe_idxmap", [IDXN], F32,
                            kind="Internal").ap()
    moe_out = nc.dram_tensor("moe_out", [XROWS, D], BF16,
                             kind="Internal").ap()

    sb.mark()
    srow = sb.alloc(SEG, F32, "moesent")
    for off in range(0, IDXN, SEG):
        w = min(SEG, IDXN - off)
        b0 = T + (off % (NE * CAP))

        def iop(e, out=srow[0:1, 0:w], b=b0, w=w):
            return e.iota(out, [[1, w]], base=b, channel_multiplier=0,
                          allow_small_or_imprecise_dtypes=True)
        P.custom(POOL, iop, writes=[srow[0:1, 0:w]], tag="moesent")
        P.dma(SP, idxmap[None, off:off + w], srow[0:1, 0:w], chan="wr",
              tag="moesentw")
    sb.release()

    rwT = C.load_wT_small("L0_ffn_router_w", "w1")
    rb = sb.alloc(NE, F32, "rbias")
    P.dma(SP, rb[0:1, :], C.dram["L0_ffn_routing_bias"][None, :], chan="w1")
    xw2 = [[sb.alloc(SEG, BF16, f"moexw_{j}_{i}") for i in range(8)]
           for j in range(2)]
    sb.mark()
    probs = [sb.alloc(SEG, F32, f"moepr_{e}") for e in range(NE)]
    pmax = sb.alloc(SEG, F32, "moepmax")
    pclip = sb.alloc(SEG, F32, "moepclip")
    rtw = sb.alloc(SEG, F32, "moertw")
    wrow = sb.alloc(SEG, BF16, "moewrow")
    mrow = sb.alloc(SEG, F32, "moemrow")
    rkrow = sb.alloc(SEG, F32, "moerk")
    t1row = sb.alloc(SEG, F32, "moet1")
    onesr = sb.alloc(SEG, F32, "moeones")
    t2row = sb.alloc(SEG, F32, "moet2")
    tokrow = sb.alloc(SEG, F32, "moetokr")
    dacc = sb.alloc(SEG, F32, "moedacc")
    base = sb.alloc(NE, F32, "moebase")
    dcol = sb.alloc(4, F32, "moedcol")
    dcoli = sb.alloc(4, I32, "moedcoli")
    tkcol = sb.alloc(1, F32, "moetk")
    P.memset(base[0:1, :], 0.0)
    P.memset(onesr[0:1, :], 1.0)
    for s in range(NSEG):
        xw = xw2[s % 2]
        for i in range(8):
            P.dma(SP, xw[i], xn_hbm[i * 128:(i + 1) * 128,
                                    s * SEG:(s + 1) * SEG],
                  chan="x" + str(i % 2))
        for e in range(NE):
            pb = ps.acquire()
            for ci in range(8):
                P.mm(ps.ap(pb)[0:1, 0:SEG], rwT[ci][:, e:e + 1], xw[ci],
                     start=(ci == 0), stop=(ci == 7), tag="moegate")
            pr = probs[e][0:1, :]
            P.ts(pr, ps.ap(pb)[0:1, 0:SEG], rb[0:1, e:e + 1], ALU.add)
            ps.release(pb)
            P.ts(pr, pr, 50.0, ALU.min)
            P.ts(pr, pr, -50.0, ALU.max)
            P.act(pr, pr, AF.Sigmoid, tag="moesig")
        pm = pmax[0:1, :]
        P.tt(pm, probs[0][0:1, :], probs[1][0:1, :], ALU.max)
        for e in range(2, NE):
            P.tt(pm, pm, probs[e][0:1, :], ALU.max)
        pc = pclip[0:1, :]
        P.ts(pc, pm, 1e-8, ALU.max)
        P.ts(pc, pc, 1.0 - 1e-8, ALU.min)
        rt = rtw[0:1, :]
        P.ts(rt, pc, 0.5 + 1e-8, ALU.add)
        P.ts(rt, rt, 0.5, ALU.max)
        P.ts(rt, rt, 2.0, ALU.min)
        P.recip(rt, rt)
        P.memset(dacc[0:1, :], 0.0, tag="moedz")

        def toki(e, out=tokrow[0:1, :], b0=s * SEG):
            return e.iota(out, [[1, SEG]], base=b0, channel_multiplier=0,
                          allow_small_or_imprecise_dtypes=True)
        P.custom(POOL, toki, writes=[tokrow[0:1, :]], tag="moetoki")
        for e in range(NE):
            m = mrow[0:1, :]
            P.tt(m, probs[e][0:1, :], pm, ALU.is_ge, tag="moemask")
            P.tt(wrow[0:1, :], m, pc, ALU.mult)
            P.tt(wrow[0:1, :], wrow[0:1, :], rt, ALU.mult)
            P.dma(SP, wrows[e:e + 1, s * SEG:(s + 1) * SEG], wrow[0:1, :],
                  chan="wr", tag="moewr")
            rk = rkrow[0:1, :]

            def scn(eng, out=rk, m=m, o1=onesr[0:1, :]):
                return eng.tensor_tensor_scan(out, o1, m, 0.0, ALU.mult,
                                              ALU.add)
            P.custom(DVE, scn, reads=[m, onesr], writes=[rk], tag="moescan")
            P.ts(rk, rk, base[0:1, e:e + 1], ALU.add, tag="moerkb")
            P.copy(base[0:1, e:e + 1], rk[0:1, SEG - 1:SEG], tag="moebup")
            t1 = t1row[0:1, :]
            P.ts(t1, rk, float(e * CAP - 1), ALU.add)
            P.ts(rk, rk, CAP + 0.5, ALU.is_gt)
            P.tt(t2row[0:1, :], tokrow[0:1, :], t1, ALU.subtract)
            P.ts(t2row[0:1, :], t2row[0:1, :], float(NE * CAP), ALU.add)
            P.tt(t2row[0:1, :], t2row[0:1, :], rk, ALU.mult)
            P.tt(t1, t1, t2row[0:1, :], ALU.add)
            P.tt(t1, t1, m, ALU.mult, tag="moedm")
            P.tt(dacc[0:1, :], dacc[0:1, :], t1, ALU.add, tag="moeda")
        P.ts(wrow[0:1, :], rt, 0.5, ALU.mult)
        P.dma(SP, wrows[NE:NE + 1, s * SEG:(s + 1) * SEG], wrow[0:1, :],
              chan="wr", tag="moewrs")
        P.ts(dacc[0:1, :], dacc[0:1, :], float(NE * CAP + T), ALU.min,
             tag="moeclamp")
        for b in range(4):
            pb = ps.acquire()
            P.transpose(ps.ap(pb)[:, 0:1],
                        dacc[0:1, b * 128:(b + 1) * 128], idf[0:1, 0:1],
                        tag="moedtr")
            P.copy(dcol[:, b:b + 1], ps.ap(pb)[:, 0:1])
            ps.release(pb)
        P.copy(dcoli, dcol, tag="moedci")
        if globals().get("_MOE_DBG") is not None and s == 0:
            P.dma(SP, globals()["_MOE_DBG"][0:1, 0:SEG], dacc[0:1, :],
                  chan="dbg")
            P.dma(SP, globals()["_MOE_DBG"][1:129, 0:4], dcol, chan="dbg")
        for b in range(4):
            def iop2(e, out=tkcol, b0=s * SEG + b * 128):
                return e.iota(out, [[0, 1]], base=b0, channel_multiplier=1,
                              allow_small_or_imprecise_dtypes=True)
            P.custom(POOL, iop2, writes=[tkcol], tag="moetk")

            im2d = idxmap.rearrange("(a b) -> a b", b=1)

            breg = C.bound_reg(IDXN - 1)

            def sc(e, idx=dcoli[:, b:b + 1].bitcast(U32), src=tkcol,
                   out=im2d, br=breg):
                return e.indirect_dma_start(out, IOA(idx, 0), src, None,
                                            bounds_check=br)
            P._add(POOL, sc, reads=[tkcol, dcoli[:, b:b + 1]],
                   writes=[idxmap], is_dma=True, chan="wr2", tag="moescat")
    sb.release()

    _MP2 = globals().get("_MOE_PHASES", 9)
    sb.mark()
    rowbuf = sb.alloc(EXT, BF16, "moerow")
    wrseg = sb.alloc(SEG, BF16, "moewrl")
    for s in (range(NSEG) if _MP2 >= 2 else []):
        xw = xw2[s % 2]
        for i in range(8):
            P.dma(SP, xw[i], xn_hbm[i * 128:(i + 1) * 128,
                                    s * SEG:(s + 1) * SEG],
                  chan="x" + str(i % 2))
        P.dma(SP, wrseg[0:NE, :], wrows[0:NE, s * SEG:(s + 1) * SEG],
              chan="wr2", tag="moewrl")
        for tt in range(4):
            for cc in range(8):
                pb = ps.acquire()
                pv = ps.ap(pb).bitcast(BF16)
                P.transpose(pv[:, 0:128],
                            xw[cc][:, tt * 128:(tt + 1) * 128], idb,
                            tag="moextr")
                P.copy(rowbuf[:, cc * 128:(cc + 1) * 128], pv[:, 0:128],
                       eng=ACT, tag="moextre")
                ps.release(pb)
            pb = ps.acquire()
            pv = ps.ap(pb).bitcast(BF16)
            P.transpose(pv[:, 0:NE],
                        wrseg[0:NE, tt * 128:(tt + 1) * 128], idb[0:NE, 0:NE],
                        tag="moewtr")
            P.copy(rowbuf[:, 1024:1024 + NE], pv[:, 0:NE], tag="moewtre")
            P.memset(rowbuf[:, 1024 + NE:EXT], 0.0, tag="moepadz")
            ps.release(pb)
            t0 = s * SEG + tt * 128
            P.dma(SP, xn_tm[t0:t0 + 128, :], rowbuf, chan="y" + str(tt % 2),
                  tag="moexnw")
    sb.release()

    sb.mark()
    zt = sb.alloc(D, BF16, "moezero")
    P.memset(zt, 0.0)
    for tt in range(32):
        P.dma(SP, moe_out[tt * 128:(tt + 1) * 128, :], zt,
              chan="y" + str(tt % 2), tag="moezw")
    sb.release()

    _MP = globals().get("_MOE_PHASES", 9)
    sb.mark()
    h = [sb.alloc(SEG, BF16, f"moeh_{i}") for i in range(NHC)]
    xg_tm = [sb.alloc(EXT, BF16, f"moexg_{j}") for j in range(2)]
    xg_fm = [sb.alloc(SEG, BF16, f"moexf_{i}") for i in range(8)]
    wEc = sb.alloc(NB, F32, "moewec")
    idxrow = sb.alloc(CAP // 2, F32, "moeidxr")
    idxf = sb.alloc(NB, F32, "moeidxf")
    idxi = sb.alloc(NB, I32, "moeidxi")
    ybuf = [sb.alloc(SEG, F32, "moeyb_0")]
    sil = ybuf[0].bitcast(BF16)[:, 0:SEG]
    y_tm = [sb.alloc(D, BF16, f"moeyt_{j}") for j in range(4)]
    segs = [(s0, min(SEG, CAP - s0)) for s0 in range(0, CAP, SEG)]
    for e in (range(NE) if _MP >= 3 else []):
        sb.mark()
        w1T = C.load_wT("L0_ffn_ew1", "w0", idx=e)
        w3T = C.load_wT("L0_ffn_ew3", "w0b", idx=e)
        w2T = C.load_wT("L0_ffn_ew2", "w0c", idx=e)
        for hf in range(2):
            o0 = e * CAP + hf * (CAP // 2)
            P.dma(SP, idxrow[0:1, :], idxmap[None, o0:o0 + CAP // 2],
                  chan="wr2", tag="moeidxl")
            for bb in range(NB // 2):
                b = hf * (NB // 2) + bb
                pb = ps.acquire()
                P.transpose(ps.ap(pb)[:, 0:1],
                            idxrow[0:1, bb * 128:(bb + 1) * 128],
                            idf[0:1, 0:1], tag="moeidxt")
                P.copy(idxf[:, b:b + 1], ps.ap(pb)[:, 0:1])
                ps.release(pb)
        P.copy(idxi, idxf, tag="moeidxc")
        breg2 = C.bound_reg(XROWS - 1)
        for si, (s0, sw) in enumerate(segs):
            for b in range(s0 // 128, (s0 + sw) // 128):
                xg = xg_tm[b % 2]
                P.memset(xg, 0.0, tag="moexgz")

                def ga(eng, out=xg, idx=idxi[:, b:b + 1].bitcast(U32),
                       br=breg2):
                    return eng.indirect_dma_start(out, None, xn_tm,
                                                  IOA(idx, 0),
                                                  bounds_check=br)
                P._add(POOL, ga, reads=[xn_tm, idxi[:, b:b + 1]],
                       writes=[xg], is_dma=True, chan="g" + str(b % 2),
                       tag="moega")
                P.copy(wEc[:, b:b + 1], xg[:, 1024 + e:1024 + e + 1],
                       tag="moewec")
                bl = b - s0 // 128
                for cc in range(8):
                    pb = ps.acquire()
                    pv = ps.ap(pb).bitcast(BF16)
                    P.transpose(pv[:, 0:128],
                                xg[:, cc * 128:(cc + 1) * 128], idb,
                                tag="moexgt")
                    P.copy(xg_fm[cc][:, bl * 128:(bl + 1) * 128],
                           pv[:, 0:128], eng=ACT, tag="moexgte")
                    ps.release(pb)
            for hc in range(NHC):
                pa = ps.acquire()
                for ci in range(8):
                    P.mm(ps.ap(pa)[:, 0:sw],
                         w1T[ci][:, hc * 128:(hc + 1) * 128],
                         xg_fm[ci][:, 0:sw],
                         start=(ci == 0), stop=(ci == 7), tag="moea")
                pbk = ps.acquire()
                for ci in range(8):
                    P.mm(ps.ap(pbk)[:, 0:sw],
                         w3T[ci][:, hc * 128:(hc + 1) * 128],
                         xg_fm[ci][:, 0:sw],
                         start=(ci == 0), stop=(ci == 7), tag="moeb")
                P.act(sil[:, 0:sw], ps.ap(pa)[:, 0:sw], AF.Silu,
                      tag="moesilu")
                ps.release(pa)
                P.tt(h[hc][:, 0:sw], sil[:, 0:sw], ps.ap(pbk)[:, 0:sw],
                     ALU.mult, tag="moeh")
                ps.release(pbk)
            for dc in range(8):
                pb = ps.acquire()
                for hc in range(NHC):
                    P.mm(ps.ap(pb)[:, 0:sw],
                         w2T[hc][:, dc * 128:(dc + 1) * 128],
                         h[hc][:, 0:sw],
                         start=(hc == 0), stop=(hc == NHC - 1), tag="moey")
                yb = ybuf[0]
                P.copy(yb[:, 0:sw], ps.ap(pb)[:, 0:sw], tag="moeye")
                ps.release(pb)
                for bl in range(sw // 128):
                    pb2 = ps.acquire()
                    P.transpose(ps.ap(pb2)[:, 0:128],
                                yb[:, bl * 128:(bl + 1) * 128], idf,
                                tag="moeytr")
                    P.ts(y_tm[bl][:, dc * 128:(dc + 1) * 128],
                         ps.ap(pb2)[:, 0:128],
                         wEc[:, s0 // 128 + bl:s0 // 128 + bl + 1],
                         ALU.mult, tag="moeyw")
                    ps.release(pb2)
            breg3 = C.bound_reg(XROWS - 1)
            for bl in range(sw // 128):
                b = s0 // 128 + bl

                def sc2(eng, src=y_tm[bl],
                        idx=idxi[:, b:b + 1].bitcast(U32), br=breg3):
                    return eng.indirect_dma_start(moe_out, IOA(idx, 0), src,
                                                  None, bounds_check=br)
                P._add(POOL, sc2, reads=[y_tm[bl], idxi[:, b:b + 1]],
                       writes=[moe_out], is_dma=True, chan="g" + str(b % 2),
                       tag="moeysc")
        sb.release()
    sb.release()

    sb.mark()
    h2 = [sb.alloc(SEG, BF16, f"moesh_{i}") for i in range(NHC)]
    sil2 = sb.alloc(SEG, BF16, "moesil2")
    wEB = sb.alloc(SEG, BF16, "moewEB")
    wrseg2 = sb.alloc(SEG, BF16, "moewrseg")
    obs = [sb.alloc(SEG, F32, f"moeob{i}") for i in range(4)]
    w1T = C.load_wT("L0_ffn_sw1", "w0")
    w3T = C.load_wT("L0_ffn_sw3", "w0b")
    w2T = C.load_wT("L0_ffn_sw2", "w0c")
    for s in range(NSEG):
        xw = xw2[s % 2]
        for i in range(8):
            P.dma(SP, xw[i], xn_hbm[i * 128:(i + 1) * 128,
                                    s * SEG:(s + 1) * SEG],
                  chan="x" + str(i % 2))
        P.dma(SP, wrseg2[0:1, :], wrows[NE:NE + 1, s * SEG:(s + 1) * SEG],
              chan="wr2")
        pb = ps.acquire()
        P.copy(wEB, C.bcast_row(wrseg2[0:1, :], pb, SEG), tag="moewEB")
        ps.release(pb)
        for hc in range(NHC):
            pa = ps.acquire()
            for ci in range(8):
                P.mm(ps.ap(pa)[:, 0:SEG],
                     w1T[ci][:, hc * 128:(hc + 1) * 128], xw[ci],
                     start=(ci == 0), stop=(ci == 7), tag="moesa")
            pbk = ps.acquire()
            for ci in range(8):
                P.mm(ps.ap(pbk)[:, 0:SEG],
                     w3T[ci][:, hc * 128:(hc + 1) * 128], xw[ci],
                     start=(ci == 0), stop=(ci == 7), tag="moesb")
            P.act(sil2, ps.ap(pa)[:, 0:SEG], AF.Silu, tag="moessilu")
            ps.release(pa)
            P.tt(h2[hc], sil2, ps.ap(pbk)[:, 0:SEG], ALU.mult, tag="moesh")
            ps.release(pbk)
        for dc in range(8):
            pb = ps.acquire()
            for hc in range(NHC):
                P.mm(ps.ap(pb)[:, 0:SEG],
                     w2T[hc][:, dc * 128:(dc + 1) * 128], h2[hc],
                     start=(hc == 0), stop=(hc == NHC - 1), tag="moesy")
            ob = obs[dc % 4]
            P.tt(ob, ps.ap(pb)[:, 0:SEG], wEB, ALU.mult, tag="moesw")
            ps.release(pb)
            P.dma(POOL, acc[dc * 128:(dc + 1) * 128,
                            s * SEG:(s + 1) * SEG], ob,
                  chan="acc" + str(dc % 2), accum_op=ALU.add, tag="moesacc")
    sb.release()

    sb.mark()
    mtile = [sb.alloc(D, BF16, f"moeft_{j}") for j in range(2)]
    stg = [sb.alloc(SEG, BF16, f"moefs_{i}") for i in range(8)]
    for g in range(8):
        for j in range(4):
            tt = g * 4 + j
            mt = mtile[tt % 2]
            P.dma(SP, mt, moe_out[tt * 128:(tt + 1) * 128, :],
                  chan="x" + str(tt % 2), tag="moefl")
            for cc in range(8):
                pb = ps.acquire()
                pv = ps.ap(pb).bitcast(BF16)
                P.transpose(pv[:, 0:128],
                            mt[:, cc * 128:(cc + 1) * 128], idb,
                            tag="moeftr")
                P.copy(stg[cc][:, j * 128:(j + 1) * 128], pv[:, 0:128],
                       eng=(ACT if cc % 2 else DVE), tag="moefe")
                ps.release(pb)
        for cc in range(8):
            P.dma(POOL, acc[cc * 128:(cc + 1) * 128,
                            g * SEG:(g + 1) * SEG], stg[cc],
                  chan="acc" + str(cc % 2), accum_op=ALU.add, tag="moefacc")
    sb.release()
    sb.release()


def build_L0_moe(C, xn_hbm, acc):
    P, sb, ps, nc = C.P, C.sb, C.ps, C.nc
    sb.mark()
    NE = 7
    HID = 2816
    NHC = HID // 128
    for e in range(NE):
        C.stage_wT("L0_ffn_ew1", idx=e)
        C.stage_wT("L0_ffn_ew3", idx=e)
        C.stage_wT("L0_ffn_ew2", idx=e)
    C.stage_wT("L0_ffn_sw1")
    C.stage_wT("L0_ffn_sw3")
    C.stage_wT("L0_ffn_sw2")

    wrows = nc.dram_tensor("moe_wrows", [NE + 1, T], BF16, kind="Internal").ap()
    rwT = C.load_wT_small("L0_ffn_router_w", "w1")
    rb = sb.alloc(NE, F32, "rbias")
    P.dma(SP, rb[0:1, :], C.dram["L0_ffn_routing_bias"][None, :], chan="w1")
    xw = [sb.alloc(SEG, BF16, f"moexw_{i}") for i in range(8)]
    sb.mark()
    probs = [sb.alloc(SEG, F32, f"moepr_{e}") for e in range(NE)]
    pmax = sb.alloc(SEG, F32, "moepmax")
    pclip = sb.alloc(SEG, F32, "moepclip")
    rtw = sb.alloc(SEG, F32, "moertw")
    wrow = sb.alloc(SEG, BF16, "moewrow")
    for s in range(NSEG):
        for i in range(8):
            P.dma(SP, xw[i], xn_hbm[i * 128:(i + 1) * 128,
                                    s * SEG:(s + 1) * SEG], chan="x0")
        for e in range(NE):
            pb = ps.acquire()
            for ci in range(8):
                P.mm(ps.ap(pb)[0:1, 0:SEG], rwT[ci][:, e:e + 1], xw[ci],
                     start=(ci == 0), stop=(ci == 7), tag="moegate")
            pr = probs[e][0:1, :]
            P.ts(pr, ps.ap(pb)[0:1, 0:SEG], rb[0:1, e:e + 1], ALU.add)
            ps.release(pb)
            P.ts(pr, pr, 50.0, ALU.min)
            P.ts(pr, pr, -50.0, ALU.max)
            P.act(pr, pr, AF.Sigmoid, tag="moesig")
        pm = pmax[0:1, :]
        P.tt(pm, probs[0][0:1, :], probs[1][0:1, :], ALU.max)
        for e in range(2, NE):
            P.tt(pm, pm, probs[e][0:1, :], ALU.max)
        pc = pclip[0:1, :]
        P.ts(pc, pm, 1e-8, ALU.max)
        P.ts(pc, pc, 1.0 - 1e-8, ALU.min)
        rt = rtw[0:1, :]
        P.ts(rt, pc, 0.5 + 1e-8, ALU.add)
        P.ts(rt, rt, 0.5, ALU.max)
        P.ts(rt, rt, 2.0, ALU.min)
        P.recip(rt, rt)
        for e in range(NE):
            wr = wrow[0:1, :]
            P.tt(wr, probs[e][0:1, :], pm, ALU.is_ge, tag="moemask")
            P.tt(wr, wr, pc, ALU.mult)
            P.tt(wr, wr, rt, ALU.mult)
            P.dma(SP, wrows[e:e + 1, s * SEG:(s + 1) * SEG], wr,
                  chan="wr", tag="moewr")
        P.ts(wrow[0:1, :], rt, 0.5, ALU.mult)
        P.dma(SP, wrows[NE:NE + 1, s * SEG:(s + 1) * SEG], wrow[0:1, :],
              chan="wr", tag="moewr")
    sb.release()

    xw2 = [[sb.alloc(SEG, BF16, f"moexw2_{j}_{i}") for i in range(8)]
           for j in range(2)]
    h = [sb.alloc(SEG, BF16, f"moeh_{i}") for i in range(NHC)]
    sil = sb.alloc(SEG, BF16, "moesil")
    wEB = sb.alloc(SEG, BF16, "moewEB")
    wrseg = sb.alloc(SEG, BF16, "moewrseg")
    obs = [sb.alloc(SEG, F32, f"moeob{i}") for i in range(4)]
    for e in range(NE + 1):
        sb.mark()
        if e < NE:
            w1T = C.load_wT("L0_ffn_ew1", "w0", idx=e)
            w3T = C.load_wT("L0_ffn_ew3", "w0b", idx=e)
            w2T = C.load_wT("L0_ffn_ew2", "w0c", idx=e)
        else:
            w1T = C.load_wT("L0_ffn_sw1", "w0")
            w3T = C.load_wT("L0_ffn_sw3", "w0b")
            w2T = C.load_wT("L0_ffn_sw2", "w0c")
        for s in range(NSEG):
            xws = xw2[s % 2]
            for i in range(8):
                P.dma(SP, xws[i], xn_hbm[i * 128:(i + 1) * 128,
                                         s * SEG:(s + 1) * SEG],
                      chan="x" + str(i % 2))
            P.dma(SP, wrseg[0:1, :], wrows[e:e + 1, s * SEG:(s + 1) * SEG],
                  chan="wr2")
            pb = ps.acquire()
            P.copy(wEB, C.bcast_row(wrseg[0:1, :], pb, SEG), tag="moewEB")
            ps.release(pb)
            for hc in range(NHC):
                pa = ps.acquire()
                for ci in range(8):
                    P.mm(ps.ap(pa)[:, 0:SEG],
                         w1T[ci][:, hc * 128:(hc + 1) * 128], xws[ci],
                         start=(ci == 0), stop=(ci == 7), tag="moea")
                pbk = ps.acquire()
                for ci in range(8):
                    P.mm(ps.ap(pbk)[:, 0:SEG],
                         w3T[ci][:, hc * 128:(hc + 1) * 128], xws[ci],
                         start=(ci == 0), stop=(ci == 7), tag="moeb")
                P.act(sil, ps.ap(pa)[:, 0:SEG], AF.Silu, tag="moesilu")
                ps.release(pa)
                P.tt(h[hc], sil, ps.ap(pbk)[:, 0:SEG], ALU.mult, tag="moeh")
                ps.release(pbk)
            for dc in range(8):
                pb = ps.acquire()
                for hc in range(NHC):
                    P.mm(ps.ap(pb)[:, 0:SEG],
                         w2T[hc][:, dc * 128:(dc + 1) * 128], h[hc],
                         start=(hc == 0), stop=(hc == NHC - 1), tag="moey")
                ob = obs[dc % 4]
                P.tt(ob, ps.ap(pb)[:, 0:SEG], wEB, ALU.mult, tag="moew")
                ps.release(pb)
                P.dma(POOL, acc[dc * 128:(dc + 1) * 128,
                                s * SEG:(s + 1) * SEG], ob,
                      chan="acc" + str(dc % 2), accum_op=ALU.add,
                      tag="moeacc")
        sb.release()
    sb.release()



def build_L1_ffn(C, xn_hbm, acc):
    P, sb, ps, nc = C.P, C.sb, C.ps, C.nc
    sb.mark()
    HID = 4096
    gwT = C.load_wT_small("L1_ffn_gate_w", "w1")
    xw = [sb.alloc(SEG, BF16, f"l1xw_{i}") for i in range(8)]
    m0 = sb.alloc(T, BF16, "l1m0")
    s0row = sb.alloc(SEG, F32, "l1s0")
    for s in range(NSEG):
        for i in range(8):
            P.dma(SP, xw[i], xn_hbm[i * 128:(i + 1) * 128,
                                    s * SEG:(s + 1) * SEG], chan="x0")
        pbs = [ps.acquire(), ps.acquire()]
        for e in range(2):
            for ci in range(8):
                P.mm(ps.ap(pbs[e])[0:1, 0:SEG], gwT[ci][:, e:e + 1], xw[ci],
                     start=(ci == 0), stop=(ci == 7), tag="l1gate")
        P.copy(s0row[0:1, :], ps.ap(pbs[0])[0:1, 0:SEG])
        P.tt(m0[0:1, s * SEG:(s + 1) * SEG], s0row[0:1, :],
             ps.ap(pbs[1])[0:1, 0:SEG], ALU.is_ge, tag="l1m0")
        ps.release(pbs[0])
        ps.release(pbs[1])

    b20 = fm_bias_col(C, "L1_ffn_b2", "w1", idx=0)
    b21 = fm_bias_col(C, "L1_ffn_b2", "w1", idx=1)
    sb2 = fm_bias_col(C, "L1_ffn_sb2", "w1")
    b2s = sb.alloc(8, F32, "l1b2s")
    for i in range(8):
        P.tt(b2s[:, i:i + 1], b20[i], b21[i], ALU.add)
        P.tt(b2s[:, i:i + 1], b2s[:, i:i + 1], sb2[i], ALU.add)
    b2cols = [b2s[:, i:i + 1] for i in range(8)]

    h = [sb.alloc(SEG, BF16, f"l1h_{i}") for i in range(32)]
    xw1 = [sb.alloc(SEG, BF16, f"l1xwb_{i}") for i in range(8)]
    xw2 = [xw1, xw1]
    xe1 = [sb.alloc(SEG, BF16, f"l1xe_{i}") for i in range(8)]
    xe2 = [xe1, xe1]
    obs = [sb.alloc(SEG, F32, f"l1ob{i}") for i in range(2)]

    for ei, (w1n, w2n, b1n, idx) in enumerate((
            ("L1_ffn_w1", "L1_ffn_w2", "L1_ffn_b1", 0),
            ("L1_ffn_w1", "L1_ffn_w2", "L1_ffn_b1", 1),
            ("L1_ffn_sw1", "L1_ffn_sw2", "L1_ffn_sb1", None))):
        sb.mark()
        b1c = fm_bias_col(C, b1n, "w1", idx=idx)
        if idx is None:
            w1 = C.load_w(w1n, "w0")
            w2 = C.load_w(w2n, "w0")
        else:
            w1 = []
            for r0 in range(0, D, 128):
                t = sb.alloc(HID, BF16, f"w_{w1n}{idx}_{r0}")
                P.dma(POOL, t, C.dram[w1n][idx, r0:r0 + 128, :], chan="w0")
                w1.append(t)
            w2 = []
            for r0 in range(0, HID, 128):
                t = sb.alloc(D, BF16, f"w_{w2n}{idx}_{r0}")
                P.dma(POOL, t, C.dram[w2n][idx, r0:r0 + 128, :], chan="w0")
                w2.append(t)
        for s in range(NSEG):
            xw = xw2[s % 2]
            xe = xe2[s % 2]
            for i in range(8):
                P.dma(SP, xw[i], xn_hbm[i * 128:(i + 1) * 128,
                                        s * SEG:(s + 1) * SEG],
                      chan="x" + str(i % 2))
            if ei < 2:
                pb = ps.acquire()
                mB = C.bcast_row(m0[0:1, s * SEG:(s + 1) * SEG], pb, SEG)
                for i in range(8):
                    if ei == 0:
                        P.tt(xe[i], xw[i], mB, ALU.mult, tag="l1mask")
                    else:
                        P.tt(xe[i], xw[i], mB, ALU.mult, tag="l1mask")
                        P.tt(xe[i], xw[i], xe[i], ALU.subtract, tag="l1mask2")
                ps.release(pb)
                xin = xe
            else:
                xin = xw

            for hc in range(32):
                pb = ps.acquire()
                for ci in range(8):
                    P.mm(ps.ap(pb)[:, 0:SEG],
                         w1[ci][:, hc * 128:(hc + 1) * 128], xin[ci],
                         start=(ci == 0), stop=(ci == 7), tag="l1h")
                P.act(h[hc], ps.ap(pb)[:, 0:SEG], AF.Relu,
                      bias=b1c[hc], tag="l1relu")
                ps.release(pb)
            for dc in range(8):
                pb = ps.acquire()
                for hc in range(32):
                    P.mm(ps.ap(pb)[:, 0:SEG],
                         w2[hc][:, dc * 128:(dc + 1) * 128], h[hc],
                         start=(hc == 0), stop=(hc == 31), tag="l1y")
                ob = obs[dc % 2]
                if ei == 2:
                    P.ts(ob, ps.ap(pb)[:, 0:SEG], b2cols[dc], ALU.add,
                         tag="l1badd")
                else:
                    P.copy(ob, ps.ap(pb)[:, 0:SEG])
                P.dma(POOL, acc[dc * 128:(dc + 1) * 128,
                                s * SEG:(s + 1) * SEG], ob,
                      chan="acc" + str(dc % 2), accum_op=ALU.add, tag="l1acc")
                ps.release(pb)
        sb.release()
    sb.release()


def b1c_ap(b1c, hc):
    return b1c[hc]



def ln_outer(C, X, xn_hbm):
    P, sb = C.P, C.sb
    sb.mark()
    xw2 = [[sb.alloc(SEG, F32, f"lno_x{j}_{i}") for i in range(8)]
           for j in range(2)]
    ow2 = [[sb.alloc(SEG, BF16, f"lno_o{j}_{i}") for i in range(8)]
           for j in range(2)]
    for s in range(NSEG):
        xw = xw2[s % 2]
        ow = ow2[s % 2]
        for i in range(8):
            P.dma(SP, xw[i], X[i * 128:(i + 1) * 128, s * SEG:(s + 1) * SEG],
                  chan="x" + str(i % 2), tag="lnoload")
        ln_window(C, xw, SEG, ow, None, None, eps=1e-5, tag="lno")
        for i in range(8):
            P.dma(SP, xn_hbm[i * 128:(i + 1) * 128, s * SEG:(s + 1) * SEG],
                  ow[i], chan="y" + str(i % 2), tag="lnostore")
    sb.release()


_BUILDERS = {
    0: (None, None),
}


def prewarm_shared(C):
    build_rrot(C)
    build_band_mask(C)
    build_causal_mask(C)
    C.reversal_bf16()


def build_layer(C, lyr, xn_hbm, X):
    ln_outer(C, X, xn_hbm)
    attn, ffn = {
        0: (build_L0_attn, build_L0_moe_sparse),
        1: (build_L1_attn, build_L1_ffn),
        2: (build_L2_attn, build_L2_ffn),
        3: (build_L3_attn, build_L3_ffn),
    }[lyr]
    ffn(C, xn_hbm, X)
    attn(C, xn_hbm, X)


def _declare_weights(nc, lyr=None):
    dram = {}
    pref = None if lyr is None else f"L{lyr}_"
    for nm, shp in _INPUT_SPECS:
        if nm == "L0_attn_x":
            continue
        if pref is None or nm.startswith(pref):
            dram[nm] = nc.dram_tensor(nm, list(shp), F32,
                                      kind="ExternalInput").ap()
    return dram


def build_layer_nc(lyr):
    nc = bass.Bass("TRN2", target_bir_lowering=False)
    P = Prog(nc)
    sb = SbufAlloc(nc)
    ps = PsumAlloc(nc)
    dram = _declare_weights(nc, lyr)
    x_in = nc.dram_tensor("x", [D, T], F32, kind="ExternalInput").ap()
    out = nc.dram_tensor("out", [D, T], F32, kind="ExternalOutput").ap()
    xn_hbm = nc.dram_tensor("xn_hbm", [D, T], BF16, kind="Internal").ap()
    C = Ctx(nc, P, sb, ps, dram)
    prewarm_shared(C)
    P.dma(SP, out, x_in, chan="xcopy", tag="xcopy")
    build_layer(C, lyr, xn_hbm, out)
    stats = P.finalize()
    return nc, stats


def build_full_nc():
    nc = bass.Bass("TRN2", target_bir_lowering=False)
    P = Prog(nc)
    sb = SbufAlloc(nc)
    ps = PsumAlloc(nc)
    dram = _declare_weights(nc, None)
    x_in = nc.dram_tensor("x", [D, T], F32, kind="ExternalInput").ap()
    out = nc.dram_tensor("out", [D, T], F32, kind="ExternalOutput").ap()
    xn_hbm = nc.dram_tensor("xn_hbm", [D, T], BF16, kind="Internal").ap()
    C = Ctx(nc, P, sb, ps, dram)
    prewarm_shared(C)
    P.dma(SP, out, x_in, chan="xcopy", tag="xcopy")
    for lyr in range(4):
        build_layer(C, lyr, xn_hbm, out)
    stats = P.finalize()
    return nc, stats


_FUSED = True


def kernel(**inputs):
    x = np.asarray(inputs["L0_attn_x"], dtype=np.float32)
    xs = [np.ascontiguousarray(x[c].T) for c in range(NCORES)]
    w_all = {nm: np.ascontiguousarray(np.asarray(inputs[nm], np.float32))
             for nm, _ in _INPUT_SPECS if nm != "L0_attn_x"}
    if _FUSED:
        nc, stats = build_full_nc()
        in_maps = []
        for c in range(NCORES):
            m = dict(w_all)
            m["x"] = xs[c]
            in_maps.append(m)
        res = run_bass_kernel_spmd(nc, in_maps, core_ids=list(range(NCORES)))
        outs = [res.results[c]["out"].T for c in range(NCORES)]
        return np.ascontiguousarray(np.stack(outs, 0))
    cur = xs
    for lyr in range(4):
        nc, stats = build_layer_nc(lyr)
        pref = f"L{lyr}_"
        in_maps = []
        for c in range(NCORES):
            m = {nm: w for nm, w in w_all.items() if nm.startswith(pref)}
            m["x"] = cur[c]
            in_maps.append(m)
        res = run_bass_kernel_spmd(nc, in_maps, core_ids=list(range(NCORES)))
        cur = [res.results[c]["out"] for c in range(NCORES)]
    return np.ascontiguousarray(np.stack([c.T for c in cur], 0))
